# version 13
# baseline (speedup 1.0000x reference)
import contextlib
import numpy as np
import concourse.bass as bass
import concourse.mybir as mybir
from concourse.bass_utils import run_bass_kernel_spmd

F32 = mybir.dt.float32
BF16 = mybir.dt.bfloat16
AF = mybir.ActivationFunctionType
ALU = mybir.AluOpType
AX = mybir.AxisListType

D = 1024
C = 128
E, EH = 8, 2048
GATE_NORM = 16.0
EPS = 1e-5

T_FULL = 8192
TPC_FULL = 2048

class Ctr:
    def __init__(self, nc, name):
        self.h = nc.semaphore(name).__enter__()
        self.n = 0


def build(T=8192, TPC=2048, taps=(), phase2=True, detect_races=False):
    NCH = T // C
    NT = TPC // C
    TB = 256
    NTB = TPC // TB
    TTB = TB // C
    nc = bass.Bass("TRN2", target_bir_lowering=False, detect_race_conditions=detect_races)

    xb = nc.dram_tensor("xb", [T, D], F32, kind="ExternalInput").ap()
    xq = nc.dram_tensor("xq", [TPC, D], F32, kind="ExternalInput").ap()
    wqT_d = nc.dram_tensor("wqT", [D, 256], F32, kind="ExternalInput")
    wkT_d = nc.dram_tensor("wkT", [D, 256], F32, kind="ExternalInput")
    wvT_d = nc.dram_tensor("wvT", [D, 256], F32, kind="ExternalInput")
    wfT_d = nc.dram_tensor("wfT", [D, 256], F32, kind="ExternalInput")
    woT_d = nc.dram_tensor("woT", [D, D], F32, kind="ExternalInput")
    nwc = nc.dram_tensor("nwc", [128, 8], F32, kind="ExternalInput").ap()
    gwc = nc.dram_tensor("gwc", [128, 8], F32, kind="ExternalInput").ap()
    gateT_d = nc.dram_tensor("gateT", [D, 8], F32, kind="ExternalInput")
    bbt_in = nc.dram_tensor("bbt", [128, 8], F32, kind="ExternalInput").ap()
    w1T_d = nc.dram_tensor("w1T", [E, D, EH], F32, kind="ExternalInput")
    w2T_d = nc.dram_tensor("w2T", [E, EH, D], F32, kind="ExternalInput")
    w1b_d = nc.dram_tensor("w1bf", [E * D * EH], BF16, kind="Internal")
    w2b_d = nc.dram_tensor("w2bf", [E * EH * D], BF16, kind="Internal")
    triu_in = nc.dram_tensor("triu", [128, 128], F32, kind="ExternalInput").ap()
    Lg_in = nc.dram_tensor("Lg", [128, 128], F32, kind="ExternalInput").ap()

    out = nc.dram_tensor("out", [TPC, D], F32, kind="ExternalOutput").ap()
    cc_in = nc.dram_tensor("cc_in", [T, 256], BF16, kind="Internal").ap()
    cc_out_d = nc.dram_tensor("cc_out", [4 * T, 256], BF16, kind="Internal")
    cc_out = cc_out_d.ap()

    tap_aps = {}
    def tap(name, shape, dtype=F32):
        if name in taps:
            tap_aps[name] = nc.dram_tensor(name, shape, dtype, kind="ExternalOutput").ap()

    tap("tap_o", [T, 256], BF16)
    tap("tap_x1", [TPC, D])
    tap("tap_comb", [128, 8 * NT])
    tap("tap_sp", [128, 256])
    tap("tap_A", [128, 256])
    tap("tap_z", [128, 256])

    def sb(name, shape, dtype):
        return nc.sbuf_tensor(name, shape, dtype).__enter__()

    wo_s = sb("wo_s", [128, 8 * 1024], BF16)
    gate_s = sb("gate_s", [128, 64], BF16)
    nwc_s = sb("nwc_s", [128, 8], F32)
    gwc_s = sb("gwc_s", [128, 8], F32)
    bbt_s = sb("bbt_s", [128, 8], F32)
    negsix = sb("negsix", [128, 1], F32)
    consts = sb("consts", [128, 2], F32)
    nc.const_aps.register(F32, EPS, consts[:, 0:1])
    nc.const_aps.register(F32, 1.0, consts[:, 1:2])

    S = lambda name: Ctr(nc, name)
    sx = S("sx"); s_sq = S("s_sq"); s_rinv = S("s_rinv"); g_xh = S("g_xh")
    d_xh = S("d_xh"); t_proj = S("t_proj"); t_f = S("t_f"); t_v = S("t_v")
    t_qk = S("t_qk"); s_ef = S("s_ef"); s_spf = S("s_spf"); t_gam = S("t_gam")
    t_gc = S("t_gc"); s_A = S("s_A"); s_eg = S("s_eg"); v_k2 = S("v_k2")
    d_k2 = S("d_k2"); s_swqk = S("s_swqk"); v_qk = S("v_qk"); d_k = S("d_k")
    s_swv = S("s_swv"); v_v = S("v_v"); t_S = S("t_S"); v_Sm = S("v_Sm")
    t_z = S("t_z"); v_z = S("v_z"); s_ez = S("s_ez"); v_pA = S("v_pA")
    d_pA = S("d_pA"); t_P2 = S("t_P2"); v_P2m = S("v_P2m"); t_o_ = S("t_o")
    s_o = S("s_o"); v_o = S("v_o"); sod = S("sod"); t_W = S("t_W")
    t_U = S("t_U"); t_abc = S("t_abc"); s_ac = S("s_ac"); v_hk = S("v_hk")
    v_hv = S("v_hv"); prep = S("prep"); prepg = S("prepg"); cc_s = S("cc_s"); s_acr = S("s_acr"); s_ch = S("s_ch"); v_ch = S("v_ch")
    s_hkb = S("s_hkb"); s_hvb = S("s_hvb"); gA0 = S("gA0"); gA1 = S("gA1"); gAr = S("gAr"); gB0 = S("gB0"); gB1 = S("gB1"); gBr = S("gBr")

    with contextlib.ExitStack() as p1stack:
        def sb1(name, shape, dtype):
            return p1stack.enter_context(nc.sbuf_tensor(name, shape, dtype))

        triu_s = sb1("triu_s", [128, 128], F32)
        Lg_s = sb1("Lg_s", [128, 128], F32)
        wq_s = sb1("wq_s", [128, 8 * 256], BF16)
        wk_s = sb1("wk_s", [128, 8 * 256], BF16)
        wv_s = sb1("wv_s", [128, 8 * 256], BF16)
        wf_s = sb1("wf_s", [128, 8 * 256], BF16)
        x_t = sb1("x_t", [128, 4 * D], F32)
        sq_scr = sb1("sq_scr", [128, D], F32)
        rms_c = sb1("rms_c", [128, 8], F32)
        xh_tm = sb1("xh_tm", [128, 2 * D], BF16)
        xh_fm = sb1("xh_fm", [128, 2 * D], BF16)
        q_fm = sb1("q_fm", [128, 2 * 256], BF16)
        k_fm = sb1("k_fm", [128, 2 * 256], BF16)
        k_tm = sb1("k_tm", [128, 2 * 256], BF16)
        v_tm = sb1("v_tm", [128, 3 * 256], BF16)
        sp_tm = sb1("sp_tm", [128, 2 * 256], F32)
        A_tm = sb1("A_tm", [128, 3 * 256], F32)
        Ainv_tm = sb1("Ainv_tm", [128, 256], F32)
        eg_tm = sb1("eg_tm", [128, 256], F32)
        s_tm = sb1("s_tm", [128, 256], F32)
        K2_tm = sb1("K2_tm", [128, 2 * 256], BF16)
        K2_fm = sb1("K2_fm", [128, 3 * 256], BF16)
        Sm_s = sb1("Sm_s", [128, 128], BF16)
        P2m_s = sb1("P2m_s", [128, 128], BF16)
        z_s = sb1("z_s", [128, 2 * 256], F32)
        ez_s = sb1("ez_s", [128, 256], F32)
        pA_tm = sb1("pA_tm", [128, 256], BF16)
        pA_fm = sb1("pA_fm", [128, 256], BF16)
        sm_cols = sb1("sm_cols", [128, 8], F32)
        ac_col = sb1("ac_col", [128, 4], F32)
        swi_scr = sb1("swi_scr", [128, 256], F32)
        swo_scr = sb1("swo_scr", [128, 256], F32)
        o_sw = sb1("o_sw", [128, 2 * 256], BF16)
        Hk_f = sb1("Hk_f", [128, 512], F32)
        Hv_f = sb1("Hv_f", [128, 512], F32)
        Hk_b = sb1("Hk_b", [128, 512], BF16)
        Hv_b = sb1("Hv_b", [128, 512], BF16)
        acr_s = sb1("acr_s", [1, 256], F32)

        with nc.Block() as blk:
            @blk.sync
            def _(sync):
                for (dst, src) in ((nwc_s, nwc), (gwc_s, gwc), (bbt_s, bbt_in),
                                   (triu_s, triu_in), (Lg_s, Lg_in)):
                    sync.dma_start(out=dst[:], in_=src[:]).then_inc(prep.h, 16)
                    prep.n += 16

            @blk.gpsimd
            def _(g):
                for (dst, src) in ((wq_s, wqT_d), (wk_s, wkT_d), (wv_s, wvT_d),
                                   (wf_s, wfT_d)):
                    g.dma_start(
                        out=bass.AP(dst, 0, [[2048, 128], [256, 8], [1, 256]]),
                        in_=bass.AP(src, 0, [[256, 128], [128 * 256, 8], [1, 256]]),
                    ).then_inc(prepg.h, 16)
                    prepg.n += 16
                g.dma_start(
                    out=bass.AP(wo_s, 0, [[8192, 128], [1024, 8], [1, 1024]]),
                    in_=bass.AP(woT_d, 0, [[1024, 128], [128 * 1024, 8], [1, 1024]]),
                ).then_inc(prepg.h, 16)
                prepg.n += 16
                g.dma_start(
                    out=bass.AP(gate_s, 0, [[64, 128], [8, 8], [1, 8]]),
                    in_=bass.AP(gateT_d, 0, [[8, 128], [128 * 8, 8], [1, 8]]),
                ).then_inc(prepg.h, 16)
                prepg.n += 16

            @blk.vector
            def _(v):
                v.memset(negsix[:], -1.0 / GATE_NORM)
                v.memset(sm_cols[:], 0.0)
                v.memset(consts[:, 0:1], EPS)
                v.memset(consts[:, 1:2], 1.0)
                v.memset(Hk_f[:], 0.0)
                v.memset(Hv_f[:], 0.0)
                v.memset(Hk_b[:], 0.0)
                v.memset(Hv_b[:], 0.0)
                v.wait_ge(prep.h, prep.n)
                v.wait_ge(prepg.h, prepg.n)
                for w in (wq_s, wk_s, wv_s, wf_s):
                    for dc in range(8):
                        vv = v.tensor_scalar_mul(out=w[:, dc * 256:(dc + 1) * 256],
                                            in0=w[:, dc * 256:(dc + 1) * 256],
                                            scalar1=nwc_s[:, dc:dc + 1])
                for dc in range(8):
                    vv = v.tensor_scalar_mul(out=wo_s[:, dc * 1024:(dc + 1) * 1024],
                                        in0=wo_s[:, dc * 1024:(dc + 1) * 1024],
                                        scalar1=gwc_s[:, dc:dc + 1])

        with contextlib.ExitStack() as psstack:
            def ps(name, shape, dtype):
                return psstack.enter_context(nc.psum_tensor(name, shape, dtype))

            B0 = ps("B0", [128, 512], F32)
            B1 = ps("B1", [128, 512], F32)
            B2 = ps("B2", [128, 512], F32)
            B3 = ps("B3", [128, 512], F32)
            B4 = ps("B4", [128, 512], F32)
            B5 = ps("B5", [128, 512], F32)
            B6 = ps("B6", [128, 512], F32)
            B7 = ps("B7", [128, 512], F32)

            with nc.Block() as blk:
                @blk.sync
                def _(sync):
                    def xhtr(n):
                        if n >= NCH:
                            return
                        p = n % 2
                        sync.wait_ge(g_xh.h, n + 1)
                        if n >= 2:
                            sync.wait_ge(t_qk.h, n - 1)
                        sync.wait_ge(d_xh.h, d_xh.n)
                        sync.dma_start(
                            out=bass.AP(xh_fm, p * D, [[2 * D, 128], [128, 8], [1, 128]]),
                            in_=bass.AP(xh_tm, p * D, [[2 * D, 128], [128, 8], [1, 128]]),
                            transpose=True).then_inc(d_xh.h, 16)
                        d_xh.n += 16

                    xhtr(0); xhtr(1)
                    for n in range(NCH + 2):
                        p = n % 2
                        q0 = p * 256
                        if n < NCH:
                            sync.wait_ge(v_qk.h, 2 * n + 2)
                            sync.wait_ge(d_k.h, d_k.n)
                            sync.dma_start(
                                out=bass.AP(k_tm, q0, [[512, 128], [128, 2], [1, 128]]),
                                in_=bass.AP(k_fm, q0, [[512, 128], [128, 2], [1, 128]]),
                                transpose=True).then_inc(d_k.h, 16)
                            d_k.n += 16
                            sync.wait_ge(v_k2.h, n + 1)
                            sync.wait_ge(d_k2.h, d_k2.n)
                            sync.dma_start(
                                out=bass.AP(K2_fm, (n % 3) * 256,
                                            [[768, 128], [128, 2], [1, 128]]),
                                in_=bass.AP(K2_tm, q0, [[512, 128], [128, 2], [1, 128]]),
                                transpose=True).then_inc(d_k2.h, 16)
                            d_k2.n += 16
                        if n >= 2:
                            sync.wait_ge(v_pA.h, 2 * (n - 1))
                            sync.wait_ge(d_pA.h, d_pA.n)
                            sync.dma_start(
                                out=bass.AP(pA_fm, 0, [[256, 128], [128, 2], [1, 128]]),
                                in_=bass.AP(pA_tm, 0, [[256, 128], [128, 2], [1, 128]]),
                                transpose=True).then_inc(d_pA.h, 16)
                            d_pA.n += 16
                        if n < NCH:
                            xhtr(n + 2)
                        if n == 0:
                            for nm, srcap, sem, cnt in (
                                    ("tap_sp", sp_tm[:, 0:256], s_spf, 1),
                                    ("tap_A", A_tm[:, 0:256], s_A, 1)):
                                if nm in tap_aps:
                                    sync.wait_ge(sem.h, cnt)
                                    sync.dma_start(out=tap_aps[nm][:], in_=srcap
                                                   ).then_inc(d_pA.h, 16)
                                    d_pA.n += 16
                        if n == 1 and "tap_z" in tap_aps:
                            sync.wait_ge(v_z.h, 1)
                            sync.dma_start(out=tap_aps["tap_z"][:], in_=z_s[:]
                                           ).then_inc(d_pA.h, 16)
                            d_pA.n += 16
                    if "tap_o" in tap_aps:
                        sync.wait_ge(sod.h, 16 * NCH)
                        sync.dma_start(out=tap_aps["tap_o"][:], in_=cc_in[:]
                                       ).then_inc(d_pA.h, 16)
                        d_pA.n += 16

                @blk.gpsimd
                def _(g):
                    def gxload(n):
                        if n >= NCH:
                            return
                        if n >= 4:
                            g.wait_ge(s_sq.h, n - 3)
                            g.wait_ge(g_xh.h, n - 3)
                        g.wait_ge(sx.h, sx.n)
                        g.dma_start(out=x_t[:, (n % 4) * D:(n % 4 + 1) * D],
                                    in_=xb[n * C:(n + 1) * C, :]).then_inc(sx.h, 16)
                        sx.n += 16

                    def gostore(n):
                        if n < 0 or n >= NCH:
                            return
                        p = n % 2
                        g.wait_ge(v_o.h, n + 1)
                        g.wait_ge(sod.h, sod.n)
                        g.dma_start(out=cc_in[n * C:(n + 1) * C, :],
                                    in_=o_sw[:, p * 256:(p + 1) * 256]
                                    ).then_inc(sod.h, 16)
                        sod.n += 16

                    gxload(0); gxload(1); gxload(2)
                    PCE = (E * D * EH) // 64
                    cast_pieces = [(wt, k) for k in range(64)
                                   for wt in range(2)]
                    NQ = NCH // 4
                    for n in range(NCH):
                        gxload(n + 3)
                        gostore(n - 2)
                        lo = len(cast_pieces) * n // NCH
                        hi = len(cast_pieces) * (n + 1) // NCH
                        for wt, k in cast_pieces[lo:hi]:
                            wsrc = (w1T_d, w2T_d)[wt]
                            wdst = (w1b_d, w2b_d)[wt]
                            gc = ((gA0, gA1, gAr), (gB0, gB1, gBr))[wt][
                                0 if k < 8 else (1 if k < 16 else 2)]
                            g.dma_start(
                                out=bass.AP(wdst, k * PCE,
                                            [[PCE // 8, 8], [1, PCE // 8]]),
                                in_=bass.AP(wsrc, k * PCE,
                                            [[PCE // 8, 8], [1, PCE // 8]]),
                            ).then_inc(gc.h, 16)
                            gc.n += 16
                        for qa in range(3):
                            if n == NQ * (qa + 1) + 1:
                                g.wait_ge(sod.h, 16 * NQ * (qa + 1))
                                g.collective_compute(
                                    "AllGather", ALU.bypass,
                                    replica_groups=[[0, 1, 2, 3], [4, 5, 6, 7]],
                                    ins=[cc_in[qa * (T // 4):
                                               (qa + 1) * (T // 4), :].opt()],
                                    outs=[cc_out[qa * T:(qa + 1) * T, :].opt()],
                                ).then_inc(cc_s.h, 1)
                                cc_s.n += 1
                    gostore(NCH - 2)
                    gostore(NCH - 1)
                    g.wait_ge(sod.h, 16 * NCH)
                    g.collective_compute(
                        "AllGather", ALU.bypass,
                        replica_groups=[[0, 1, 2, 3], [4, 5, 6, 7]],
                        ins=[cc_in[3 * (T // 4):T, :].opt()],
                        outs=[cc_out[3 * T:4 * T, :].opt()],
                    ).then_inc(cc_s.h, 1)
                    cc_s.n += 1

                @blk.scalar
                def _(s):
                    def hop(inst):
                        inst.then_inc(s_ch.h, 1)
                        s_ch.n += 1
                        s.wait_ge(s_ch.h, s_ch.n)

                    def swish_sig(src_ap):
                        hop(s.activation(out=swi_scr[:], in_=src_ap, func=AF.Exp,
                                         scale=-1.0))
                        hop(s.activation(out=swi_scr[:], in_=swi_scr[:], func=AF.Ln,
                                         bias=1.0))
                        return s.activation(out=swi_scr[:], in_=swi_scr[:], func=AF.Exp,
                                            scale=-1.0)

                    def rms_xhat(k):
                        if k >= NCH:
                            return
                        pk = k % 2
                        s.wait_ge(sx.h, 16 * (k + 1))
                        aa = s.activation(out=sq_scr[:],
                                          in_=x_t[:, (k % 4) * D:(k % 4 + 1) * D],
                                          func=AF.Square,
                                          accum_out=rms_c[:, (pk * 4):(pk * 4 + 1)])
                        aa.then_inc(s_sq.h, 1); s_sq.n += 1
                        s.wait_ge(s_sq.h, s_sq.n)
                        hop(s.activation(out=rms_c[:, (pk * 4 + 1):(pk * 4 + 2)],
                                         in_=rms_c[:, (pk * 4):(pk * 4 + 1)],
                                         func=AF.Ln, scale=1.0 / D, bias=EPS))
                        aa = s.activation(out=rms_c[:, (pk * 4 + 2):(pk * 4 + 3)],
                                          in_=rms_c[:, (pk * 4 + 1):(pk * 4 + 2)],
                                          func=AF.Exp, scale=-0.5)
                        aa.then_inc(s_rinv.h, 1); s_rinv.n += 1
                        s.wait_ge(s_rinv.h, s_rinv.n)
                        if k >= 2:
                            s.wait_ge(d_xh.h, 16 * (k - 1))
                        aa = s.activation(out=xh_tm[:, pk * D:(pk + 1) * D],
                                          in_=x_t[:, (k % 4) * D:(k % 4 + 1) * D],
                                          func=AF.Identity,
                                          scale=rms_c[:, (pk * 4 + 2):(pk * 4 + 3)])
                        aa.then_inc(g_xh.h, 1); g_xh.n += 1

                    rms_xhat(0); rms_xhat(1)
                    for n in range(NCH + 2):
                        p = n % 2
                        q0 = p * 256
                        if n < NCH:
                            rms_xhat(n + 2)
                        if 1 <= n <= NCH:
                            s.wait_ge(v_hk.h, n)
                            s.wait_ge(t_z.h, n)
                            aa = s.activation(out=Hk_b[:], in_=Hk_f[:], func=AF.Identity)
                            aa.then_inc(s_hkb.h, 1); s_hkb.n += 1
                        if n < NCH:
                            s.wait_ge(t_f.h, n + 1)
                            if n >= 2:
                                s.wait_ge(t_S.h, n - 1)
                                s.wait_ge(s_eg.h, n - 1)
                                s.wait_ge(t_gam.h, n - 1)
                            aa = s.activation(out=sp_tm[:, q0:q0 + 256], in_=B7[:, 0:256],
                                              func=AF.Exp, scale=-1.0)
                            aa.then_inc(s_ef.h, 1); s_ef.n += 1
                            s.wait_ge(s_ef.h, s_ef.n)
                            aa = s.activation(out=sp_tm[:, q0:q0 + 256],
                                              in_=sp_tm[:, q0:q0 + 256],
                                              func=AF.Ln, bias=1.0)
                            aa.then_inc(s_spf.h, 1); s_spf.n += 1
                            s.wait_ge(s_spf.h, s_spf.n)
                            s.wait_ge(t_gam.h, n + 1)
                            if n >= 3:
                                s.wait_ge(v_pA.h, 2 * (n - 2))
                            if n >= 2:
                                s.wait_ge(t_abc.h, n - 1)
                            if n >= 1:
                                s.wait_ge(v_k2.h, n)
                            aa = s.activation(out=A_tm[:, (n % 3) * 256:(n % 3) * 256 + 256],
                                              in_=B2[:, 0:256], func=AF.Exp)
                            aa = s.activation(out=Ainv_tm[:], in_=B2[:, 0:256],
                                              func=AF.Exp, scale=-1.0)
                            aa.then_inc(s_A.h, 1); s_A.n += 1
                            aa = s.activation(out=eg_tm[:], in_=sp_tm[:, q0:q0 + 256],
                                              func=AF.Exp, scale=-1.0 / GATE_NORM)
                            aa.then_inc(s_eg.h, 1); s_eg.n += 1
                            s.wait_ge(t_qk.h, n + 1)
                            aa = swish_sig(B0[:, 0:256])
                            aa.then_inc(s_swqk.h, 1); s_swqk.n += 1
                            s.wait_ge(v_qk.h, 2 * n + 1)
                            aa = swish_sig(B0[:, 256:512])
                            aa.then_inc(s_swqk.h, 1); s_swqk.n += 1
                            s.wait_ge(t_v.h, n + 1)
                            s.wait_ge(v_qk.h, 2 * n + 2)
                            aa = swish_sig(B1[:, 0:256])
                            aa.then_inc(s_swv.h, 1); s_swv.n += 1
                            s.wait_ge(t_S.h, n + 1)
                            s.wait_ge(v_Sm.h, n + 1)
                            if n >= 1:
                                s.wait_ge(t_abc.h, n)
                            aa = s.activation(out=acr_s[0:1, :], in_=B4[0:1, 256:512],
                                              func=AF.Exp)
                            aa.then_inc(s_acr.h, 1); s_acr.n += 1
                            if n >= 2:
                                s.wait_ge(v_hv.h, n - 1)
                            pa2 = (n % 2) * 2
                            aa = s.activation(out=ac_col[:, pa2:pa2 + 1],
                                              in_=B4[:, 128:129], func=AF.Exp)
                            aa = s.activation(out=ac_col[:, pa2 + 1:pa2 + 2],
                                              in_=B4[:, 129:130], func=AF.Exp)
                            aa.then_inc(s_ac.h, 1); s_ac.n += 1
                        if n >= 2:
                            pz = (n - 2) % 2
                            s.wait_ge(v_z.h, n - 1)
                            aa = s.activation(out=ez_s[:], in_=z_s[:, pz * 256:(pz + 1) * 256],
                                              func=AF.Exp,
                                              bias=sm_cols[:, 4 + pz:5 + pz])
                            aa.then_inc(s_ez.h, 1); s_ez.n += 1
                            s.wait_ge(t_o_.h, n - 1)
                            if n >= 3:
                                s.wait_ge(v_o.h, n - 2)
                            hop(s.activation(out=swo_scr[:], in_=B3[:, 0:256],
                                             func=AF.Exp, scale=-1.0))
                            hop(s.activation(out=swo_scr[:], in_=swo_scr[:], func=AF.Ln,
                                             bias=1.0))
                            aa = s.activation(out=swo_scr[:], in_=swo_scr[:],
                                              func=AF.Exp, scale=-1.0)
                            aa.then_inc(s_o.h, 1); s_o.n += 1
                            s.wait_ge(v_hv.h, n - 1)
                            s.wait_ge(t_o_.h, n - 1)
                            aa = s.activation(out=Hv_b[:], in_=Hv_f[:], func=AF.Identity)
                            aa.then_inc(s_hvb.h, 1); s_hvb.n += 1

                @blk.tensor
                def _(t):
                    for n in range(NCH + 2):
                        p = n % 2
                        q0 = p * 256

                        def xh_c(dc):
                            return xh_fm[:, p * D + dc * 128: p * D + (dc + 1) * 128]

                        if n < NCH:
                            t.wait_ge(d_xh.h, 16 * (n + 1))
                            if n >= 1:
                                t.wait_ge(s_ef.h, n)
                                t.wait_ge(v_z.h, n)
                            for dc in range(8):
                                mm = t.matmul(B7[:, 0:256], xh_c(dc),
                                              wf_s[:, dc * 256:(dc + 1) * 256],
                                              start=(dc == 0), stop=(dc == 7))
                            mm.then_inc(t_f.h, 1); t_f.n += 1
                            if n >= 1:
                                t.wait_ge(v_v.h, n)
                            for dc in range(8):
                                mm = t.matmul(B1[:, 0:256], xh_c(dc),
                                              wv_s[:, dc * 256:(dc + 1) * 256],
                                              start=(dc == 0), stop=(dc == 7))
                            mm.then_inc(t_v.h, 1); t_v.n += 1
                            if n >= 1:
                                t.wait_ge(v_qk.h, 2 * n)
                            for j in range(2):
                                for dc in range(8):
                                    mm = t.matmul(
                                        B0[:, j * 128:(j + 1) * 128],
                                        wq_s[:, dc * 256 + j * 128: dc * 256 + (j + 1) * 128],
                                        xh_c(dc), start=(dc == 0), stop=(dc == 7))
                            for j in range(2):
                                for dc in range(8):
                                    mm = t.matmul(
                                        B0[:, 256 + j * 128: 256 + (j + 1) * 128],
                                        wk_s[:, dc * 256 + j * 128: dc * 256 + (j + 1) * 128],
                                        xh_c(dc), start=(dc == 0), stop=(dc == 7))
                            mm.then_inc(t_qk.h, 1); t_qk.n += 1
                            t.wait_ge(s_spf.h, n + 1)
                            if n >= 1:
                                t.wait_ge(s_A.h, n)
                                t.wait_ge(v_hk.h, n)
                            mm = t.matmul(B2[:, 0:256], Lg_s[:], sp_tm[:, q0:q0 + 256],
                                          start=True, stop=True)
                            mm.then_inc(t_gam.h, 1); t_gam.n += 1
                            t.wait_ge(v_qk.h, 2 * n + 2)
                            if n >= 1:
                                t.wait_ge(v_Sm.h, n)
                                t.wait_ge(s_acr.h, n)
                                t.wait_ge(s_ac.h, n)
                            for j in range(2):
                                mm = t.matmul(B4[:, 0:128],
                                              k_fm[:, q0 + j * 128: q0 + (j + 1) * 128],
                                              q_fm[:, q0 + j * 128: q0 + (j + 1) * 128],
                                              start=(j == 0), stop=(j == 1))
                            mm = t.matmul(B4[0:1, 256:512], negsix[:],
                                          sp_tm[:, q0:q0 + 256], start=True, stop=True)
                            mm = t.matmul(B4[:, 128:129], sp_tm[:, q0:q0 + 128],
                                          negsix[:], start=True, stop=True)
                            mm = t.matmul(B4[:, 129:130], sp_tm[:, q0 + 128:q0 + 256],
                                          negsix[:], start=True, stop=True)
                            mm.then_inc(t_S.h, 1); t_S.n += 1
                            t.wait_ge(s_A.h, n + 1)
                            t.wait_ge(s_acr.h, n + 1)
                            if n >= 1:
                                t.wait_ge(v_hk.h, n)
                            mm = t.matmul(B2[:, 256:512], triu_s[0:1, :], acr_s[0:1, :],
                                          start=True, stop=True)
                            mm.then_inc(t_abc.h, 1); t_abc.n += 1
                            t.wait_ge(d_k.h, 16 * (n + 1))
                            t.wait_ge(v_k2.h, n + 1)
                            if n >= 1:
                                t.wait_ge(v_hk.h, n)
                            for j in range(2):
                                mm = t.matmul(B5[:, j * 256:(j + 1) * 256],
                                              k_tm[:, q0 + j * 128: q0 + (j + 1) * 128],
                                              K2_tm[:, q0:q0 + 256], start=True, stop=True)
                            mm.then_inc(t_W.h, 1); t_W.n += 1
                            t.wait_ge(s_ef.h, n + 1)
                            t.wait_ge(s_hkb.h, n)
                            if n >= 1:
                                t.wait_ge(v_z.h, n)
                            t.wait_ge(v_Sm.h, n + 1)
                            for j in range(2):
                                mm = t.matmul(B7[:, 256:512],
                                              q_fm[:, q0 + j * 128: q0 + (j + 1) * 128],
                                              Hk_b[:, j * 256:(j + 1) * 256],
                                              start=(j == 0), stop=False)
                            mm = t.matmul(B7[:, 256:512], Sm_s[:], K2_tm[:, q0:q0 + 256],
                                          start=False, stop=True)
                            mm.then_inc(t_z.h, 1); t_z.n += 1
                        if 1 <= n <= NCH:
                            nu = n - 1
                            qu = (nu % 2) * 256
                            t.wait_ge(v_v.h, nu + 1)
                            t.wait_ge(v_k2.h, nu + 1)
                            if n >= 2:
                                t.wait_ge(v_hv.h, nu)
                            for j in range(2):
                                mm = t.matmul(B6[:, j * 256:(j + 1) * 256],
                                              K2_tm[:, qu + j * 128: qu + (j + 1) * 128],
                                              v_tm[:, (nu % 3) * 256:(nu % 3) * 256 + 256],
                                              start=True, stop=True)
                            mm.then_inc(t_U.h, 1); t_U.n += 1
                        if n >= 2:
                            nm = n - 2
                            qm3 = (nm % 3) * 256
                            t.wait_ge(d_k2.h, 16 * (nm + 1))
                            t.wait_ge(d_pA.h, 16 * (nm + 1))
                            if n >= 3:
                                t.wait_ge(v_o.h, nm)
                            for j in range(2):
                                mm = t.matmul(B3[:, 256:384],
                                              K2_fm[:, qm3 + j * 128: qm3 + (j + 1) * 128],
                                              pA_fm[:, j * 128:(j + 1) * 128],
                                              start=(j == 0), stop=(j == 1))
                            mm.then_inc(t_P2.h, 1); t_P2.n += 1
                            t.wait_ge(s_hvb.h, nm)
                            t.wait_ge(v_P2m.h, nm + 1)
                            t.wait_ge(v_v.h, nm + 1)
                            for j in range(2):
                                mm = t.matmul(B3[:, 0:256],
                                              pA_fm[:, j * 128:(j + 1) * 128],
                                              Hv_b[:, j * 256:(j + 1) * 256],
                                              start=(j == 0), stop=False)
                            mm = t.matmul(B3[:, 0:256], P2m_s[:], v_tm[:, qm3:qm3 + 256],
                                          start=False, stop=True)
                            mm.then_inc(t_o_.h, 1); t_o_.n += 1

                @blk.vector
                def _(v):
                    def vhop(inst):
                        inst.then_inc(v_ch.h, 1)
                        v_ch.n += 1
                        v.wait_ge(v_ch.h, v_ch.n)

                    for n in range(NCH + 2):
                        p = n % 2
                        q0 = p * 256
                        if n < NCH:
                            v.wait_ge(s_swqk.h, 2 * n + 1)
                            if n >= 2:
                                v.wait_ge(d_k.h, 16 * (n - 1))
                                v.wait_ge(t_S.h, n - 1)
                                v.wait_ge(t_z.h, n - 1)
                            vv = v.tensor_tensor(out=q_fm[:, q0:q0 + 256], in0=swi_scr[:],
                                                 in1=B0[:, 0:256], op=ALU.mult)
                            vv.then_inc(v_qk.h, 1); v_qk.n += 1
                            v.wait_ge(s_swqk.h, 2 * n + 2)
                            vv = v.tensor_tensor(out=k_fm[:, q0:q0 + 256], in0=swi_scr[:],
                                                 in1=B0[:, 256:512], op=ALU.mult)
                            vv.then_inc(v_qk.h, 1); v_qk.n += 1
                            v.wait_ge(s_swv.h, n + 1)
                            if n >= 3:
                                v.wait_ge(t_U.h, n - 2)
                                v.wait_ge(t_o_.h, n - 2)
                            vv = v.tensor_tensor(out=v_tm[:, (n % 3) * 256:(n % 3) * 256 + 256],
                                                 in0=swi_scr[:],
                                                 in1=B1[:, 0:256], op=ALU.mult)
                            vv.then_inc(v_v.h, 1); v_v.n += 1
                            v.wait_ge(s_eg.h, n + 1)
                            if n >= 2:
                                v.wait_ge(t_z.h, n - 1)
                                v.wait_ge(t_W.h, n - 1)
                                v.wait_ge(t_U.h, n - 1)
                                v.wait_ge(d_k2.h, 16 * (n - 1))
                            if n >= 3:
                                v.wait_ge(t_P2.h, n - 2)
                            vhop(v.tensor_scalar(out=s_tm[:], in0=eg_tm[:], scalar1=1.0,
                                                 scalar2=-1.0, op0=ALU.subtract,
                                                 op1=ALU.mult))
                            vv = v.tensor_tensor(out=K2_tm[:, q0:q0 + 256], in0=s_tm[:],
                                                 in1=Ainv_tm[:], op=ALU.mult)
                            vv.then_inc(v_k2.h, 1); v_k2.n += 1
                            v.wait_ge(t_S.h, n + 1)
                            if n >= 1:
                                v.wait_ge(t_z.h, n)
                            vv = v.tensor_tensor(out=Sm_s[:], in0=B4[:, 0:128],
                                                 in1=triu_s[:], op=ALU.mult)
                            vv.then_inc(v_Sm.h, 1); v_Sm.n += 1
                            v.wait_ge(t_W.h, n + 1)
                            v.wait_ge(t_abc.h, n + 1)
                            v.wait_ge(s_hkb.h, n)
                            for j in range(2):
                                vhop(v.tensor_tensor(out=Hk_f[:, j * 256:(j + 1) * 256],
                                                     in0=Hk_f[:, j * 256:(j + 1) * 256],
                                                     in1=B5[:, j * 256:(j + 1) * 256],
                                                     op=ALU.add))
                                vv = v.tensor_tensor(out=Hk_f[:, j * 256:(j + 1) * 256],
                                                     in0=Hk_f[:, j * 256:(j + 1) * 256],
                                                     in1=B2[:, 256:512], op=ALU.mult)
                                vhop(vv)
                            vv = v.tensor_copy(out=sm_cols[:, 3:4], in_=sm_cols[:, 3:4])
                            vv.then_inc(v_hk.h, 1); v_hk.n += 1
                            v.wait_ge(t_z.h, n + 1)
                            if n >= 2:
                                v.wait_ge(s_ez.h, n - 1)
                            pz = n % 2
                            vhop(v.tensor_tensor(out=z_s[:, pz * 256:(pz + 1) * 256],
                                                 in0=B7[:, 256:512],
                                                 in1=A_tm[:, (n % 3) * 256:(n % 3) * 256 + 256],
                                                 op=ALU.mult))
                            vv = v.reduce_max(sm_cols[:, 4 + pz:5 + pz],
                                              z_s[:, pz * 256:(pz + 1) * 256], axis=AX.X,
                                              negate=True)
                            vv.then_inc(v_z.h, 1); v_z.n += 1
                        if n >= 2:
                            nm = n - 2
                            qm3 = (nm % 3) * 256
                            v.wait_ge(s_ez.h, nm + 1)
                            vhop(v.reduce_sum(sm_cols[:, 1:2], ez_s[:], axis=AX.X))
                            vv = v.reciprocal(out=sm_cols[:, 2:3], in_=sm_cols[:, 1:2])
                            vv.then_inc(v_pA.h, 1); v_pA.n += 1
                            v.wait_ge(v_pA.h, v_pA.n)
                            if n >= 3:
                                v.wait_ge(d_pA.h, 16 * nm)
                            vv = v.scalar_tensor_tensor(out=pA_tm[:], in0=ez_s[:],
                                                        scalar=sm_cols[:, 2:3],
                                                        in1=A_tm[:, qm3:qm3 + 256],
                                                        op0=ALU.mult, op1=ALU.mult)
                            vv.then_inc(v_pA.h, 1); v_pA.n += 1
                            v.wait_ge(t_P2.h, nm + 1)
                            if n >= 3:
                                v.wait_ge(t_o_.h, nm)
                            vv = v.tensor_tensor(out=P2m_s[:], in0=B3[:, 256:384],
                                                 in1=triu_s[:], op=ALU.mult)
                            vv.then_inc(v_P2m.h, 1); v_P2m.n += 1
                            v.wait_ge(s_o.h, nm + 1)
                            if n >= 4:
                                v.wait_ge(sod.h, 16 * (nm - 1))
                            pm = nm % 2
                            vv = v.tensor_tensor(out=o_sw[:, pm * 256:(pm + 1) * 256],
                                                 in0=swo_scr[:], in1=B3[:, 0:256],
                                                 op=ALU.mult)
                            vv.then_inc(v_o.h, 1); v_o.n += 1
                        if 1 <= n <= NCH:
                            nu = n - 1
                            v.wait_ge(t_U.h, nu + 1)
                            v.wait_ge(s_ac.h, nu + 1)
                            v.wait_ge(s_hvb.h, nu)
                            for j in range(2):
                                vhop(v.tensor_tensor(out=Hv_f[:, j * 256:(j + 1) * 256],
                                                     in0=Hv_f[:, j * 256:(j + 1) * 256],
                                                     in1=B6[:, j * 256:(j + 1) * 256],
                                                     op=ALU.add))
                                vv = v.tensor_scalar_mul(
                                    out=Hv_f[:, j * 256:(j + 1) * 256],
                                    in0=Hv_f[:, j * 256:(j + 1) * 256],
                                    scalar1=ac_col[:, (nu % 2) * 2 + j:(nu % 2) * 2 + j + 1])
                                vhop(vv)
                            vv = v.tensor_copy(out=sm_cols[:, 3:4], in_=sm_cols[:, 3:4])
                            vv.then_inc(v_hv.h, 1); v_hv.n += 1

        if not phase2:
            with nc.Block() as blk:
                @blk.sync
                def _(sync):
                    sync.wait_ge(cc_s.h, 4)
            return nc, tap_aps

    so_r = S("so_r"); sxq = S("sxq"); s2_oh = S("s2_oh"); d2_oh = S("d2_oh")
    t2_at = S("t2_at"); v2_z1 = S("v2_z1"); s2_sq2 = S("s2_sq2"); v2_ln = S("v2_ln")
    s2_rs = S("s2_rs"); v2_nb = S("v2_nb"); s2_x1 = S("s2_x1"); v2_x1b = S("v2_x1b")
    d2_x1 = S("d2_x1"); t2_g = S("t2_g"); v2_sc = S("v2_sc")
    g2w1 = S("g2w1"); g2w2 = S("g2w2"); g2w2b = S("g2w2b"); t2_q = S("t2_q"); t2_m2e = S("t2_m2e")
    s2_h = S("s2_h"); t2_m1 = S("t2_m1"); t2_y = S("t2_y"); v2_y = S("v2_y")
    s2_ot = S("s2_ot"); v2_l2 = S("v2_l2"); sout = S("sout")

    with contextlib.ExitStack() as p2stack:
        def sb2(name, shape, dtype):
            return p2stack.enter_context(nc.sbuf_tensor(name, shape, dtype))

        ot_bf = sb2("ot_bf", [128, 3 * 1024], BF16)
        oh_fm = sb2("oh_fm", [128, 2 * 1024], BF16)
        xr_s = sb2("xr_s", [128, 1024], F32)
        z1_s = sb2("z1_s", [128, 1024], F32)
        ln_c = sb2("ln_c", [128, 16], F32)
        y_acc = sb2("y_acc", [128, NT * 1024], F32)
        x1bf = sb2("x1bf", [128, 1024], BF16)
        x1_fm = sb2("x1_fm", [128, 8 * TPC], BF16)
        scr2 = sb2("scr2", [128, 1024], BF16)
        scores = sb2("scores", [128, 8 * NT], F32)
        comb = sb2("comb", [128, 8 * NT], F32)
        tp_c = sb2("tp_c", [128, 8], F32)
        t2scr = sb2("t2scr", [128, 16], F32)
        w1s = sb2("w1s", [128, 8 * EH], BF16)
        w2s = sb2("w2s", [128, 16 * 1024], BF16)
        h_fm = sb2("h_fm", [128, 16 * TB], BF16)

        PA_cm = nc.psum_tensor("PA", [128, 1024], F32); PA = PA_cm.__enter__()
        PG_cm = nc.psum_tensor("PG", [128, 512], F32); PG = PG_cm.__enter__()

        with nc.Block() as blk:
            @blk.gpsimd
            def _(g):
                with g.register("rq") as rq, g.register("roff") as roff, \
                     g.register("rt") as rt:
                    g.reg_mod(rq, g.partition_id(), 4)
                    g.reg_mul(roff, rq, T * 256)
                    g.wait_ge(gB0.h, gB0.n)
                    g.wait_ge(s_hvb.h, s_hvb.n)
                    g.wait_ge(v_hv.h, v_hv.n)
                    g.dma_start(
                        out=bass.AP(w2s, 8 * 1024,
                                    [[16384, 128], [1024, 8], [1, 1024]]),
                        in_=bass.AP(w2b_d, 8 * 128 * 1024,
                                    [[1024, 128], [128 * 1024, 8], [1, 1024]]),
                    ).then_inc(g2w2b.h, 16)
                    g2w2b.n += 16
                    g.wait_ge(cc_s.h, 4)
                    for i in range(NT):
                        g.reg_mov(rt, i * C * 256)
                        g.reg_add(rt, rt, roff)
                        if i >= 3:
                            g.wait_ge(s2_oh.h, 4 * (i - 2))
                            g.wait_ge(d2_oh.h, 16 * (i - 2))
                        g.wait_ge(so_r.h, so_r.n)
                        g.dma_start(
                            out=bass.AP(ot_bf, (i % 3) * 1024,
                                        [[3072, 128], [256, 4], [1, 256]]),
                            in_=bass.AP(cc_out_d, rt,
                                        [[256, 128], [(T // 4) * 256, 4], [1, 256]]),
                        ).then_inc(so_r.h, 16)
                        so_r.n += 16

            @blk.sync
            def _(sync):
                sync.wait_ge(gA0.h, gA0.n)
                sync.wait_ge(s_hvb.h, s_hvb.n)
                sync.wait_ge(v_hv.h, v_hv.n)
                for qq in range(4):
                    sync.wait_ge(g2w1.h, g2w1.n)
                    sync.dma_start(
                        out=bass.AP(w1s, qq * 4096, [[16384, 128], [512, 8], [1, 512]]),
                        in_=bass.AP(w1b_d, qq * 512,
                                    [[EH, 128], [128 * EH, 8], [1, 512]]),
                    ).then_inc(g2w1.h, 16)
                    g2w1.n += 16
                for i in range(NT):
                    if i >= 1:
                        sync.wait_ge(v2_z1.h, i)
                    sync.wait_ge(sxq.h, sxq.n)
                    sync.dma_start(out=xr_s[:],
                                   in_=xq[i * C:(i + 1) * C, :]).then_inc(sxq.h, 16)
                    sxq.n += 16

            @blk.scalar
            def _(s):
                def ostage(k):
                    if k >= NT:
                        return
                    ko = (k % 3) * 1024
                    s.wait_ge(so_r.h, 16 * (k + 1))
                    aa = s.activation(out=scr2[:], in_=ot_bf[:, ko:ko + 1024],
                                 func=AF.Square,
                                 accum_out=ln_c[:, 0:1]).then_inc(s2_oh.h, 1)
                    s2_oh.n += 1
                    s.wait_ge(s2_oh.h, s2_oh.n)
                    aa = s.activation(out=ln_c[:, 1:2], in_=ln_c[:, 0:1], func=AF.Ln,
                                 scale=1.0 / D, bias=EPS).then_inc(s2_oh.h, 1)
                    s2_oh.n += 1
                    s.wait_ge(s2_oh.h, s2_oh.n)
                    aa = s.activation(out=ln_c[:, 2:3], in_=ln_c[:, 1:2], func=AF.Exp,
                                 scale=-0.5).then_inc(s2_oh.h, 1)
                    s2_oh.n += 1
                    s.wait_ge(s2_oh.h, s2_oh.n)
                    aa = s.activation(out=ot_bf[:, ko:ko + 1024],
                                 in_=ot_bf[:, ko:ko + 1024], func=AF.Identity,
                                 scale=ln_c[:, 2:3]).then_inc(s2_oh.h, 1)
                    s2_oh.n += 1
                    if k >= 2:
                        s.wait_ge(t2_at.h, k - 1)
                    s.wait_ge(s2_oh.h, s2_oh.n)
                    s.wait_ge(d2_oh.h, d2_oh.n)
                    s.dma_start(out=bass.AP(oh_fm, (k % 2) * 1024,
                                            [[2048, 128], [128, 8], [1, 128]]),
                                in_=bass.AP(ot_bf, ko,
                                            [[3072, 128], [128, 8], [1, 128]]),
                                transpose=True).then_inc(d2_oh.h, 16)
                    d2_oh.n += 16

                s.wait_ge(gB0.h, gB0.n)
                s.wait_ge(v_hv.h, v_hv.n)
                s.dma_start(
                    out=bass.AP(w2s, 0, [[16384, 128], [1024, 8], [1, 1024]]),
                    in_=bass.AP(w2b_d, 0, [[1024, 128], [128 * 1024, 8], [1, 1024]]),
                ).then_inc(g2w2.h, 16)
                g2w2.n += 16
                ostage(0)
                for i in range(NT):
                    ostage(i + 1)
                    s.wait_ge(v2_z1.h, i + 1)
                    aa = s.activation(out=scr2[:], in_=z1_s[:], func=AF.Square,
                                 accum_out=ln_c[:, 4:5]).then_inc(s2_sq2.h, 1)
                    s2_sq2.n += 1
                    s.wait_ge(v2_ln.h, i + 1)
                    aa = s.activation(out=ln_c[:, 8:9], in_=ln_c[:, 7:8], func=AF.Ln,
                                 bias=EPS).then_inc(s2_rs.h, 1)
                    s2_rs.n += 1
                    s.wait_ge(s2_rs.h, s2_rs.n)
                    aa = s.activation(out=ln_c[:, 9:10], in_=ln_c[:, 8:9], func=AF.Exp,
                                 scale=-0.5).then_inc(s2_rs.h, 1)
                    s2_rs.n += 1
                    s.wait_ge(s2_rs.h, s2_rs.n)
                    s.wait_ge(v2_nb.h, i + 1)
                    aa = s.activation(out=y_acc[:, i * 1024:(i + 1) * 1024], in_=z1_s[:],
                                 func=AF.Identity, scale=ln_c[:, 9:10],
                                 bias=ln_c[:, 11:12]).then_inc(s2_x1.h, 1)
                    s2_x1.n += 1
                    s.wait_ge(v2_x1b.h, i + 1)
                    s.wait_ge(d2_x1.h, d2_x1.n)
                    s.dma_start(out=bass.AP(x1_fm, i * 128,
                                            [[8 * TPC, 128], [TPC, 8], [1, 128]]),
                                in_=bass.AP(x1bf, 0, [[1024, 128], [128, 8], [1, 128]]),
                                transpose=True).then_inc(d2_x1.h, 16)
                    d2_x1.n += 16

            @blk.tensor
            def _(t):
                for i in range(NT):
                    t.wait_ge(d2_oh.h, 16 * (i + 1))
                    if i >= 1:
                        t.wait_ge(v2_z1.h, i)
                    ko = (i % 2) * 1024
                    for dc in range(8):
                        for half in range(2):
                            mm = t.matmul(PA[:, half * 512:(half + 1) * 512],
                                     oh_fm[:, ko + dc * 128: ko + (dc + 1) * 128],
                                     wo_s[:, dc * 1024 + half * 512:
                                          dc * 1024 + (half + 1) * 512],
                                     start=(dc == 0), stop=(dc == 7))
                    mm.then_inc(t2_at.h, 1); t2_at.n += 1

            @blk.vector
            def _(v):
                def vhop(inst):
                    inst.then_inc(v_ch.h, 1)
                    v_ch.n += 1
                    v.wait_ge(v_ch.h, v_ch.n)

                for i in range(NT):
                    v.wait_ge(t2_at.h, i + 1)
                    v.wait_ge(sxq.h, 16 * (i + 1))
                    if i >= 1:
                        v.wait_ge(s2_x1.h, i)
                    vhop(v.tensor_tensor(out=z1_s[:], in0=PA[:],
                                    in1=xr_s[:], op=ALU.add))
                    vv = v.reduce_sum(ln_c[:, 3:4], z1_s[:], axis=AX.X)
                    vv.then_inc(v2_z1.h, 1); v2_z1.n += 1
                    v.wait_ge(s2_sq2.h, i + 1)
                    vhop(v.tensor_scalar_mul(out=ln_c[:, 5:6], in0=ln_c[:, 3:4],
                                        scalar1=1.0 / D))
                    vv = v.tensor_tensor(out=ln_c[:, 6:7], in0=ln_c[:, 5:6],
                                    in1=ln_c[:, 5:6], op=ALU.mult)
                    vhop(v.tensor_scalar_mul(out=ln_c[:, 7:8], in0=ln_c[:, 4:5],
                                        scalar1=1.0 / D))
                    vv = v.tensor_tensor(out=ln_c[:, 7:8], in0=ln_c[:, 7:8],
                                    in1=ln_c[:, 6:7], op=ALU.subtract)
                    vv.then_inc(v2_ln.h, 1); v2_ln.n += 1
                    v.wait_ge(s2_rs.h, 2 * (i + 1))
                    vhop(v.tensor_tensor(out=ln_c[:, 10:11], in0=ln_c[:, 5:6],
                                    in1=ln_c[:, 9:10], op=ALU.mult))
                    vv = v.tensor_scalar_mul(out=ln_c[:, 11:12], in0=ln_c[:, 10:11],
                                        scalar1=-1.0)
                    vv.then_inc(v2_nb.h, 1); v2_nb.n += 1
                    v.wait_ge(s2_x1.h, i + 1)
                    if i >= 1:
                        v.wait_ge(d2_x1.h, 16 * i)
                    vv = v.tensor_copy(out=x1bf[:], in_=y_acc[:, i * 1024:(i + 1) * 1024])
                    vv.then_inc(v2_x1b.h, 1); v2_x1b.n += 1


        with nc.Block() as blk:
            @blk.tensor
            def _(t):
                for i in range(NT):
                    t.wait_ge(d2_x1.h, 16 * (i + 1))
                    if i >= 1:
                        t.wait_ge(v2_sc.h, i)
                    for dc in range(8):
                        mm = t.matmul(PG[:, 0:8],
                                 x1_fm[:, dc * TPC + i * 128: dc * TPC + (i + 1) * 128],
                                 gate_s[:, dc * 8:(dc + 1) * 8],
                                 start=(dc == 0), stop=(dc == 7))
                    mm.then_inc(t2_g.h, 1); t2_g.n += 1

            @blk.vector
            def _(v2x):
                for i in range(NT):
                    v2x.wait_ge(t2_g.h, i + 1)
                    vv = v2x.tensor_tensor(out=scores[:, i * 8:(i + 1) * 8],
                                           in0=PG[:, 0:8], in1=bbt_s[:], op=ALU.add)
                    vv.then_inc(v2_sc.h, 1); v2_sc.n += 1

            @blk.scalar
            def _(s):
                for i in range(NT):
                    s.wait_ge(v2_sc.h, i + 1)
                    aa = s.activation(out=scores[:, i * 8:(i + 1) * 8],
                                 in_=scores[:, i * 8:(i + 1) * 8],
                                 func=AF.Sigmoid)
                aa.then_inc(s2_ot.h, 1)
                s2_ot.n += 1

            @blk.vector
            def _(v):
                def vhop(inst):
                    inst.then_inc(v_ch.h, 1)
                    v_ch.n += 1
                    v.wait_ge(v_ch.h, v_ch.n)

                v.wait_ge(s2_ot.h, 1)
                for i in range(NT):
                    sc = scores[:, i * 8:(i + 1) * 8]
                    vv = v.reduce_max(tp_c[:, 0:1], sc, axis=AX.X)
                    vv.then_inc(v2_l2.h, 1); v2_l2.n += 1
                    v.wait_ge(v2_l2.h, v2_l2.n)
                    vhop(v.tensor_single_scalar(out=comb[:, i * 8:(i + 1) * 8],
                                           in_=sc, scalar=tp_c[:, 0:1], op=ALU.is_ge))
                    vhop(v.scalar_tensor_tensor(out=t2scr[:, 0:8], in0=comb[:, i * 8:(i + 1) * 8],
                                           scalar=-2.0, in1=sc, op0=ALU.mult, op1=ALU.add))
                    vv = v.reduce_max(tp_c[:, 1:2], t2scr[:, 0:8], axis=AX.X)
                    vv.then_inc(v2_l2.h, 1); v2_l2.n += 1
                    v.wait_ge(v2_l2.h, v2_l2.n)
                    vhop(v.tensor_single_scalar(out=t2scr[:, 8:16], in_=t2scr[:, 0:8],
                                           scalar=tp_c[:, 1:2], op=ALU.is_ge))
                    vhop(v.tensor_tensor(out=comb[:, i * 8:(i + 1) * 8],
                                    in0=comb[:, i * 8:(i + 1) * 8],
                                    in1=t2scr[:, 8:16], op=ALU.add))
                    vv = v.tensor_scalar(out=tp_c[:, 2:3], in0=tp_c[:, 0:1],
                                    scalar1=tp_c[:, 1:2], scalar2=1e-6,
                                    op0=ALU.add, op1=ALU.add)
                    vv.then_inc(v2_l2.h, 1); v2_l2.n += 1
                    v.wait_ge(v2_l2.h, v2_l2.n)
                    vv = v.reciprocal(out=tp_c[:, 3:4], in_=tp_c[:, 2:3])
                    vv.then_inc(v2_l2.h, 1); v2_l2.n += 1
                    v.wait_ge(v2_l2.h, v2_l2.n)
                    vv = v.scalar_tensor_tensor(out=comb[:, i * 8:(i + 1) * 8], in0=sc,
                                           scalar=tp_c[:, 3:4],
                                           in1=comb[:, i * 8:(i + 1) * 8],
                                           op0=ALU.mult, op1=ALU.mult)
                vv.then_inc(v2_l2.h, 1); v2_l2.n += 1

            @blk.sync
            def _(sync):
                if "tap_x1" in tap_aps:
                    for i in range(NT):
                        sync.dma_start(out=tap_aps["tap_x1"][i * 128:(i + 1) * 128, :],
                                       in_=y_acc[:, i * 1024:(i + 1) * 1024]
                                       ).then_inc(sout.h, 16)
                        sout.n += 16
                if "tap_comb" in tap_aps:
                    sync.wait_ge(v2_l2.h, v2_l2.n)
                    sync.dma_start(out=tap_aps["tap_comb"][:], in_=comb[:]
                                   ).then_inc(sout.h, 16)
                    sout.n += 16

        PG_cm.__exit__(None, None, None)
        PA_cm.__exit__(None, None, None)
        PH0_cm = nc.psum_tensor("PH0", [128, 512], F32); PH0 = PH0_cm.__enter__()
        PH1_cm = nc.psum_tensor("PH1", [128, 512], F32); PH1 = PH1_cm.__enter__()
        PY0_cm = nc.psum_tensor("PY0", [128, 1024], F32); PY0 = PY0_cm.__enter__()
        PY1_cm = nc.psum_tensor("PY1", [128, 1024], F32); PY1 = PY1_cm.__enter__()

        with nc.Block() as blk:
            @blk.sync
            def _(sync):
                for e in range(1, E):
                    ga = (gA1 if e == 1 else gAr)
                    sync.wait_ge(ga.h, ga.n)
                    for qq in range(4):
                        sync.wait_ge(t2_m1.h,
                                     16 * NTB * (e - 1) + 16 * (NTB - 1) + 4 * qq + 4)
                        sync.wait_ge(g2w1.h, g2w1.n)
                        sync.dma_start(
                            out=bass.AP(w1s, qq * 4096,
                                        [[16384, 128], [512, 8], [1, 512]]),
                            in_=bass.AP(w1b_d, e * D * EH + qq * 512,
                                        [[EH, 128], [128 * EH, 8], [1, 512]]),
                        ).then_inc(g2w1.h, 16)
                        g2w1.n += 16

            @blk.tensor
            def _(t):
                for e in range(E):
                    for tb in range(NTB):
                        for ehc in range(16):
                            qq = ehc // 4
                            er = ehc % 4
                            if er == 0:
                                t.wait_ge(g2w1.h, 16 * (4 * e + qq + 1))
                            gsil = 16 * (NTB * e + tb) + ehc
                            if gsil >= 2:
                                t.wait_ge(s2_h.h, gsil - 1)
                            ph = (PH0, PH1)[ehc % 2]
                            for dc in range(8):
                                col = qq * 4096 + dc * 512 + er * 128
                                mm = t.matmul(ph[:, 0:TB], w1s[:, col: col + 128],
                                         x1_fm[:, dc * TPC + tb * TB:
                                               dc * TPC + (tb + 1) * TB],
                                         start=(dc == 0), stop=(dc == 7))
                            mm.then_inc(t2_m1.h, 1); t2_m1.n += 1
                        t.wait_ge(g2w2.h, 16 * (e + 1))
                        t.wait_ge(g2w2b.h, 16 * (e + 1))
                        t.wait_ge(s2_h.h, 16 * (NTB * e + tb + 1))
                        for tt in range(TTB):
                            gt = (NTB * e + tb) * TTB + tt
                            if gt >= 2:
                                t.wait_ge(v2_y.h, gt - 1)
                            PY = (PY0, PY1)[gt % 2]
                            for ehc in range(16):
                                hh = h_fm[:, ehc * TB + tt * 128: ehc * TB + (tt + 1) * 128]
                                mm = t.matmul(PY[:, 0:512], hh,
                                         w2s[:, ehc * 1024: ehc * 1024 + 512],
                                         start=(ehc == 0), stop=(ehc == 15))
                                mm = t.matmul(PY[:, 512:1024], hh,
                                         w2s[:, ehc * 1024 + 512: (ehc + 1) * 1024],
                                         start=(ehc == 0), stop=(ehc == 15))
                            mm.then_inc(t2_y.h, 1); t2_y.n += 1


            @blk.scalar
            def _(s):
                for e in range(E):
                    if e >= 1:
                        gb = (gB1 if e == 1 else gBr)
                        s.wait_ge(gb.h, gb.n)
                        s.wait_ge(t2_y.h, NTB * TTB * e)
                        s.dma_start(
                            out=bass.AP(w2s, 0, [[16384, 128], [1024, 8], [1, 1024]]),
                            in_=bass.AP(w2b_d, e * EH * D,
                                        [[1024, 128], [128 * 1024, 8], [1, 1024]]),
                        ).then_inc(g2w2.h, 16)
                        g2w2.n += 16
                    for tb in range(NTB):
                        for ehc in range(16):
                            s.wait_ge(t2_m1.h, 16 * (NTB * e + tb) + ehc + 1)
                            ph = (PH0, PH1)[ehc % 2]
                            aa = s.activation(out=h_fm[:, ehc * TB:(ehc + 1) * TB],
                                         in_=ph[:, 0:TB], func=AF.Silu
                                         ).then_inc(s2_h.h, 1)
                            s2_h.n += 1

            @blk.gpsimd
            def _(g):
                for e in range(1, E):
                    gb = (gB1 if e == 1 else gBr)
                    g.wait_ge(gb.h, gb.n)
                    g.wait_ge(t2_y.h, NTB * TTB * e)
                    g.dma_start(
                        out=bass.AP(w2s, 8 * 1024,
                                    [[16384, 128], [1024, 8], [1, 1024]]),
                        in_=bass.AP(w2b_d, e * EH * D + 8 * 128 * 1024,
                                    [[1024, 128], [128 * 1024, 8], [1, 1024]]),
                    ).then_inc(g2w2b.h, 16)
                    g2w2b.n += 16

            @blk.vector
            def _(v):
                for e in range(E):
                    for tb in range(NTB):
                        for tt in range(TTB):
                            gt = (NTB * e + tb) * TTB + tt
                            tile = tb * TTB + tt
                            v.wait_ge(t2_y.h, gt + 1)
                            PY = (PY0, PY1)[gt % 2]
                            vv = v.scalar_tensor_tensor(
                                out=y_acc[:, tile * 1024:(tile + 1) * 1024],
                                in0=PY[:], scalar=comb[:, tile * 8 + e: tile * 8 + e + 1],
                                in1=y_acc[:, tile * 1024:(tile + 1) * 1024],
                                op0=ALU.mult, op1=ALU.add)
                            vv.then_inc(v2_y.h, 1); v2_y.n += 1

        with nc.Block() as blk:
            @blk.scalar
            def _(s):
                for i in range(NT):
                    yt = y_acc[:, i * 1024:(i + 1) * 1024]
                    aa = s.activation(out=scr2[:], in_=yt, func=AF.Square,
                                 accum_out=ln_c[:, 4:5]).then_inc(s2_sq2.h, 1)
                    s2_sq2.n += 1
                    s.wait_ge(v2_ln.h, v2_ln.n + i + 1)
                    aa = s.activation(out=ln_c[:, 8:9], in_=ln_c[:, 7:8], func=AF.Ln,
                                 bias=EPS).then_inc(s2_rs.h, 1)
                    s2_rs.n += 1
                    s.wait_ge(s2_rs.h, s2_rs.n)
                    aa = s.activation(out=ln_c[:, 9:10], in_=ln_c[:, 8:9], func=AF.Exp,
                                 scale=-0.5).then_inc(s2_rs.h, 1)
                    s2_rs.n += 1
                    s.wait_ge(s2_rs.h, s2_rs.n)
                    s.wait_ge(v2_nb.h, v2_nb.n + i + 1)
                    if i >= 1:
                        s.wait_ge(sout.h, sout.n + 16 * i)
                    aa = s.activation(out=z1_s[:],
                                 in_=yt, func=AF.Identity, scale=ln_c[:, 9:10],
                                 bias=ln_c[:, 11:12]).then_inc(s2_ot.h, 1)
                    s2_ot.n += 1

            @blk.vector
            def _(v):
                def vhop(inst):
                    inst.then_inc(v_ch.h, 1)
                    v_ch.n += 1
                    v.wait_ge(v_ch.h, v_ch.n)

                for i in range(NT):
                    yt = y_acc[:, i * 1024:(i + 1) * 1024]
                    vhop(v.reduce_sum(ln_c[:, 3:4], yt, axis=AX.X))
                    v.wait_ge(s2_sq2.h, s2_sq2.n - NT + i + 1)
                    vhop(v.tensor_scalar_mul(out=ln_c[:, 5:6], in0=ln_c[:, 3:4],
                                        scalar1=1.0 / D))
                    vv = v.tensor_tensor(out=ln_c[:, 6:7], in0=ln_c[:, 5:6],
                                    in1=ln_c[:, 5:6], op=ALU.mult)
                    vhop(v.tensor_scalar_mul(out=ln_c[:, 7:8], in0=ln_c[:, 4:5],
                                        scalar1=1.0 / D))
                    vv = v.tensor_tensor(out=ln_c[:, 7:8], in0=ln_c[:, 7:8],
                                    in1=ln_c[:, 6:7], op=ALU.subtract)
                    vv.then_inc(v2_ln.h, 1); v2_ln.n += 1
                    v.wait_ge(s2_rs.h, s2_rs.n - 2 * NT + 2 * (i + 1))
                    vhop(v.tensor_tensor(out=ln_c[:, 10:11], in0=ln_c[:, 5:6],
                                    in1=ln_c[:, 9:10], op=ALU.mult))
                    vv = v.tensor_scalar_mul(out=ln_c[:, 11:12], in0=ln_c[:, 10:11],
                                        scalar1=-1.0)
                    vv.then_inc(v2_nb.h, 1); v2_nb.n += 1

            @blk.sync
            def _(sync):
                base = s2_ot.n - NT
                for i in range(NT):
                    sync.wait_ge(s2_ot.h, base + i + 1)
                    sync.wait_ge(sout.h, sout.n)
                    sync.dma_start(out=out[i * 128:(i + 1) * 128, :],
                                   in_=z1_s[:]).then_inc(sout.h, 16)
                    sout.n += 16
                sync.wait_ge(sout.h, sout.n)

        PY1_cm.__exit__(None, None, None)
        PY0_cm.__exit__(None, None, None)
        PH1_cm.__exit__(None, None, None)
        PH0_cm.__exit__(None, None, None)

    return nc, tap_aps


_CACHE = {}


def _get_nc():
    if "nc" not in _CACHE:
        _CACHE["nc"] = build(T=T_FULL, TPC=TPC_FULL)[0]
    return _CACHE["nc"]


def kernel(L0_attn_hidden_states, L0_attn_norm_weight, L0_attn_w_q, L0_attn_w_k,
           L0_attn_w_v, L0_attn_w_f, L0_attn_gnorm_weight, L0_attn_w_o,
           L0_ffn_gate_w, L0_ffn_balance_bias, L0_ffn_w1, L0_ffn_w2):
    x = np.asarray(L0_attn_hidden_states)
    nw = np.asarray(L0_attn_norm_weight)
    gw = np.asarray(L0_attn_gnorm_weight)
    w_q, w_k = np.asarray(L0_attn_w_q), np.asarray(L0_attn_w_k)
    w_v, w_f = np.asarray(L0_attn_w_v), np.asarray(L0_attn_w_f)
    w_o = np.asarray(L0_attn_w_o)
    gate_w = np.asarray(L0_ffn_gate_w)
    bb = np.asarray(L0_ffn_balance_bias)
    w1, w2 = np.asarray(L0_ffn_w1), np.asarray(L0_ffn_w2)

    triu = np.triu(np.ones((128, 128), np.float32))
    Lg = np.ascontiguousarray(triu * np.float32(-1.0 / GATE_NORM))
    w1T = np.ascontiguousarray(w1.transpose(0, 2, 1))
    w2T = np.ascontiguousarray(w2.transpose(0, 2, 1))
    wqT_f = np.ascontiguousarray(w_q.T)
    wkT_f = np.ascontiguousarray(w_k.T)
    wvT_f = np.ascontiguousarray(w_v.T)
    wfT_f = np.ascontiguousarray(w_f.T)
    woT = np.ascontiguousarray(w_o.T)
    nwc = np.ascontiguousarray(nw.reshape(8, 128).T)
    gwc = np.ascontiguousarray(gw.reshape(8, 128).T)
    gateT = np.ascontiguousarray(gate_w.T)
    bbt = np.ascontiguousarray(np.broadcast_to(bb, (128, 8)))

    in_maps = []
    for c in range(8):
        b, h = c // 4, c % 4
        qq = c % 4
        in_maps.append(dict(
            xb=np.ascontiguousarray(x[b]),
            xq=np.ascontiguousarray(x[b, qq * TPC_FULL:(qq + 1) * TPC_FULL]),
            wqT=np.ascontiguousarray(wqT_f[:, 256 * h:256 * (h + 1)]),
            wkT=np.ascontiguousarray(wkT_f[:, 256 * h:256 * (h + 1)]),
            wvT=np.ascontiguousarray(wvT_f[:, 256 * h:256 * (h + 1)]),
            wfT=np.ascontiguousarray(wfT_f[:, 256 * h:256 * (h + 1)]),
            woT=woT, nwc=nwc, gwc=gwc, gateT=gateT, bbt=bbt,
            w1T=w1T, w2T=w2T, triu=triu, Lg=Lg,
        ))

    nc = _get_nc()
    res = run_bass_kernel_spmd(nc, in_maps, core_ids=list(range(8)))
    out = np.concatenate([np.asarray(res.results[c]["out"]) for c in range(8)],
                         axis=0).reshape(2, T_FULL, D)
    return out.astype(np.float32)


# version 14 (final)
# speedup vs baseline: 1.0061x; 1.0061x over previous
import contextlib
import numpy as np
import concourse.bass as bass
import concourse.mybir as mybir
from concourse.bass_utils import run_bass_kernel_spmd

F32 = mybir.dt.float32
BF16 = mybir.dt.bfloat16
AF = mybir.ActivationFunctionType
ALU = mybir.AluOpType
AX = mybir.AxisListType

D = 1024
C = 128
E, EH = 8, 2048
GATE_NORM = 16.0
EPS = 1e-5

T_FULL = 8192
TPC_FULL = 2048

class Ctr:
    def __init__(self, nc, name):
        self.h = nc.semaphore(name).__enter__()
        self.n = 0


def build(T=8192, TPC=2048, taps=(), phase2=True, detect_races=False):
    NCH = T // C
    NT = TPC // C
    TB = 256
    NTB = TPC // TB
    TTB = TB // C
    nc = bass.Bass("TRN2", target_bir_lowering=False, detect_race_conditions=detect_races)

    xb = nc.dram_tensor("xb", [T, D], F32, kind="ExternalInput").ap()
    xq = nc.dram_tensor("xq", [TPC, D], F32, kind="ExternalInput").ap()
    wqT_d = nc.dram_tensor("wqT", [D, 256], F32, kind="ExternalInput")
    wkT_d = nc.dram_tensor("wkT", [D, 256], F32, kind="ExternalInput")
    wvT_d = nc.dram_tensor("wvT", [D, 256], F32, kind="ExternalInput")
    wfT_d = nc.dram_tensor("wfT", [D, 256], F32, kind="ExternalInput")
    woT_d = nc.dram_tensor("woT", [D, D], F32, kind="ExternalInput")
    nwc = nc.dram_tensor("nwc", [128, 8], F32, kind="ExternalInput").ap()
    gwc = nc.dram_tensor("gwc", [128, 8], F32, kind="ExternalInput").ap()
    gateT_d = nc.dram_tensor("gateT", [D, 8], F32, kind="ExternalInput")
    bbt_in = nc.dram_tensor("bbt", [128, 8], F32, kind="ExternalInput").ap()
    w1T_d = nc.dram_tensor("w1T", [E, D, EH], F32, kind="ExternalInput")
    w2T_d = nc.dram_tensor("w2T", [E, EH, D], F32, kind="ExternalInput")
    w1b_d = nc.dram_tensor("w1bf", [E * D * EH], BF16, kind="Internal")
    w2b_d = nc.dram_tensor("w2bf", [E * EH * D], BF16, kind="Internal")
    triu_in = nc.dram_tensor("triu", [128, 128], F32, kind="ExternalInput").ap()
    Lg_in = nc.dram_tensor("Lg", [128, 128], F32, kind="ExternalInput").ap()

    out = nc.dram_tensor("out", [TPC, D], F32, kind="ExternalOutput").ap()
    cc_in = nc.dram_tensor("cc_in", [T, 256], BF16, kind="Internal").ap()
    cc_out_d = nc.dram_tensor("cc_out", [4 * T, 256], BF16, kind="Internal")
    cc_out = cc_out_d.ap()

    tap_aps = {}
    def tap(name, shape, dtype=F32):
        if name in taps:
            tap_aps[name] = nc.dram_tensor(name, shape, dtype, kind="ExternalOutput").ap()

    tap("tap_o", [T, 256], BF16)
    tap("tap_x1", [TPC, D])
    tap("tap_comb", [128, 8 * NT])
    tap("tap_sp", [128, 256])
    tap("tap_A", [128, 256])
    tap("tap_z", [128, 256])

    def sb(name, shape, dtype):
        return nc.sbuf_tensor(name, shape, dtype).__enter__()

    wo_s = sb("wo_s", [128, 8 * 1024], BF16)
    gate_s = sb("gate_s", [128, 64], BF16)
    nwc_s = sb("nwc_s", [128, 8], F32)
    gwc_s = sb("gwc_s", [128, 8], F32)
    bbt_s = sb("bbt_s", [128, 8], F32)
    negsix = sb("negsix", [128, 1], F32)
    consts = sb("consts", [128, 2], F32)
    nc.const_aps.register(F32, EPS, consts[:, 0:1])
    nc.const_aps.register(F32, 1.0, consts[:, 1:2])

    S = lambda name: Ctr(nc, name)
    sx = S("sx"); s_sq = S("s_sq"); s_rinv = S("s_rinv"); g_xh = S("g_xh")
    d_xh = S("d_xh"); t_proj = S("t_proj"); t_f = S("t_f"); t_v = S("t_v")
    t_qk = S("t_qk"); s_ef = S("s_ef"); s_spf = S("s_spf"); t_gam = S("t_gam")
    t_gc = S("t_gc"); s_A = S("s_A"); s_eg = S("s_eg"); v_k2 = S("v_k2")
    d_k2 = S("d_k2"); s_swqk = S("s_swqk"); v_qk = S("v_qk"); d_k = S("d_k")
    s_swv = S("s_swv"); v_v = S("v_v"); t_S = S("t_S"); v_Sm = S("v_Sm")
    t_z = S("t_z"); v_z = S("v_z"); s_ez = S("s_ez"); v_pA = S("v_pA")
    d_pA = S("d_pA"); t_P2 = S("t_P2"); v_P2m = S("v_P2m"); t_o_ = S("t_o")
    s_o = S("s_o"); v_o = S("v_o"); sod = S("sod"); t_W = S("t_W")
    t_U = S("t_U"); t_abc = S("t_abc"); s_ac = S("s_ac"); v_hk = S("v_hk")
    v_hv = S("v_hv"); prep = S("prep"); prepg = S("prepg"); cc_s = S("cc_s"); s_acr = S("s_acr"); s_ch = S("s_ch"); v_ch = S("v_ch")
    s_hkb = S("s_hkb"); s_hvb = S("s_hvb"); gA0 = S("gA0"); gA1 = S("gA1"); gAr = S("gAr"); gB0 = S("gB0"); gB1 = S("gB1"); gBr = S("gBr")

    with contextlib.ExitStack() as p1stack:
        def sb1(name, shape, dtype):
            return p1stack.enter_context(nc.sbuf_tensor(name, shape, dtype))

        triu_s = sb1("triu_s", [128, 128], F32)
        Lg_s = sb1("Lg_s", [128, 128], F32)
        wq_s = sb1("wq_s", [128, 8 * 256], BF16)
        wk_s = sb1("wk_s", [128, 8 * 256], BF16)
        wv_s = sb1("wv_s", [128, 8 * 256], BF16)
        wf_s = sb1("wf_s", [128, 8 * 256], BF16)
        x_t = sb1("x_t", [128, 4 * D], F32)
        sq_scr = sb1("sq_scr", [128, D], F32)
        rms_c = sb1("rms_c", [128, 8], F32)
        xh_tm = sb1("xh_tm", [128, 2 * D], BF16)
        xh_fm = sb1("xh_fm", [128, 2 * D], BF16)
        q_fm = sb1("q_fm", [128, 2 * 256], BF16)
        k_fm = sb1("k_fm", [128, 2 * 256], BF16)
        k_tm = sb1("k_tm", [128, 2 * 256], BF16)
        v_tm = sb1("v_tm", [128, 3 * 256], BF16)
        sp_tm = sb1("sp_tm", [128, 2 * 256], F32)
        A_tm = sb1("A_tm", [128, 3 * 256], F32)
        Ainv_tm = sb1("Ainv_tm", [128, 256], F32)
        eg_tm = sb1("eg_tm", [128, 256], F32)
        s_tm = sb1("s_tm", [128, 256], F32)
        K2_tm = sb1("K2_tm", [128, 2 * 256], BF16)
        K2_fm = sb1("K2_fm", [128, 3 * 256], BF16)
        Sm_s = sb1("Sm_s", [128, 128], BF16)
        P2m_s = sb1("P2m_s", [128, 128], BF16)
        z_s = sb1("z_s", [128, 2 * 256], F32)
        ez_s = sb1("ez_s", [128, 256], F32)
        pA_tm = sb1("pA_tm", [128, 256], BF16)
        pA_fm = sb1("pA_fm", [128, 256], BF16)
        sm_cols = sb1("sm_cols", [128, 8], F32)
        ac_col = sb1("ac_col", [128, 4], F32)
        swi_scr = sb1("swi_scr", [128, 256], F32)
        swo_scr = sb1("swo_scr", [128, 256], F32)
        o_sw = sb1("o_sw", [128, 2 * 256], BF16)
        Hk_f = sb1("Hk_f", [128, 512], F32)
        Hv_f = sb1("Hv_f", [128, 512], F32)
        Hk_b = sb1("Hk_b", [128, 512], BF16)
        Hv_b = sb1("Hv_b", [128, 512], BF16)
        acr_s = sb1("acr_s", [1, 256], F32)

        with nc.Block() as blk:
            @blk.sync
            def _(sync):
                for (dst, src) in ((nwc_s, nwc), (gwc_s, gwc), (bbt_s, bbt_in),
                                   (triu_s, triu_in), (Lg_s, Lg_in)):
                    sync.dma_start(out=dst[:], in_=src[:]).then_inc(prep.h, 16)
                    prep.n += 16

            @blk.gpsimd
            def _(g):
                for (dst, src) in ((wq_s, wqT_d), (wk_s, wkT_d), (wv_s, wvT_d),
                                   (wf_s, wfT_d)):
                    g.dma_start(
                        out=bass.AP(dst, 0, [[2048, 128], [256, 8], [1, 256]]),
                        in_=bass.AP(src, 0, [[256, 128], [128 * 256, 8], [1, 256]]),
                    ).then_inc(prepg.h, 16)
                    prepg.n += 16
                g.dma_start(
                    out=bass.AP(wo_s, 0, [[8192, 128], [1024, 8], [1, 1024]]),
                    in_=bass.AP(woT_d, 0, [[1024, 128], [128 * 1024, 8], [1, 1024]]),
                ).then_inc(prepg.h, 16)
                prepg.n += 16
                g.dma_start(
                    out=bass.AP(gate_s, 0, [[64, 128], [8, 8], [1, 8]]),
                    in_=bass.AP(gateT_d, 0, [[8, 128], [128 * 8, 8], [1, 8]]),
                ).then_inc(prepg.h, 16)
                prepg.n += 16

            @blk.vector
            def _(v):
                v.memset(negsix[:], -1.0 / GATE_NORM)
                v.memset(sm_cols[:], 0.0)
                v.memset(consts[:, 0:1], EPS)
                v.memset(consts[:, 1:2], 1.0)
                v.memset(Hk_f[:], 0.0)
                v.memset(Hv_f[:], 0.0)
                v.memset(Hk_b[:], 0.0)
                v.memset(Hv_b[:], 0.0)
                v.wait_ge(prep.h, prep.n)
                v.wait_ge(prepg.h, prepg.n)
                for w in (wq_s, wk_s, wv_s, wf_s):
                    for dc in range(8):
                        vv = v.tensor_scalar_mul(out=w[:, dc * 256:(dc + 1) * 256],
                                            in0=w[:, dc * 256:(dc + 1) * 256],
                                            scalar1=nwc_s[:, dc:dc + 1])
                for dc in range(8):
                    vv = v.tensor_scalar_mul(out=wo_s[:, dc * 1024:(dc + 1) * 1024],
                                        in0=wo_s[:, dc * 1024:(dc + 1) * 1024],
                                        scalar1=gwc_s[:, dc:dc + 1])

        with contextlib.ExitStack() as psstack:
            def ps(name, shape, dtype):
                return psstack.enter_context(nc.psum_tensor(name, shape, dtype))

            B0 = ps("B0", [128, 512], F32)
            B1 = ps("B1", [128, 512], F32)
            B2 = ps("B2", [128, 512], F32)
            B3 = ps("B3", [128, 512], F32)
            B4 = ps("B4", [128, 512], F32)
            B5 = ps("B5", [128, 512], F32)
            B6 = ps("B6", [128, 512], F32)
            B7 = ps("B7", [128, 512], F32)

            with nc.Block() as blk:
                @blk.sync
                def _(sync):
                    def xhtr(n):
                        if n >= NCH:
                            return
                        p = n % 2
                        sync.wait_ge(g_xh.h, n + 1)
                        if n >= 2:
                            sync.wait_ge(t_qk.h, n - 1)
                        sync.wait_ge(d_xh.h, d_xh.n)
                        sync.dma_start(
                            out=bass.AP(xh_fm, p * D, [[2 * D, 128], [128, 8], [1, 128]]),
                            in_=bass.AP(xh_tm, p * D, [[2 * D, 128], [128, 8], [1, 128]]),
                            transpose=True).then_inc(d_xh.h, 16)
                        d_xh.n += 16

                    xhtr(0); xhtr(1)
                    for n in range(NCH + 2):
                        p = n % 2
                        q0 = p * 256
                        if n < NCH:
                            sync.wait_ge(v_qk.h, 2 * n + 2)
                            sync.wait_ge(d_k.h, d_k.n)
                            sync.dma_start(
                                out=bass.AP(k_tm, q0, [[512, 128], [128, 2], [1, 128]]),
                                in_=bass.AP(k_fm, q0, [[512, 128], [128, 2], [1, 128]]),
                                transpose=True).then_inc(d_k.h, 16)
                            d_k.n += 16
                            sync.wait_ge(v_k2.h, n + 1)
                            sync.wait_ge(d_k2.h, d_k2.n)
                            sync.dma_start(
                                out=bass.AP(K2_fm, (n % 3) * 256,
                                            [[768, 128], [128, 2], [1, 128]]),
                                in_=bass.AP(K2_tm, q0, [[512, 128], [128, 2], [1, 128]]),
                                transpose=True).then_inc(d_k2.h, 16)
                            d_k2.n += 16
                        if n >= 2:
                            sync.wait_ge(v_pA.h, 2 * (n - 1))
                            sync.wait_ge(d_pA.h, d_pA.n)
                            sync.dma_start(
                                out=bass.AP(pA_fm, 0, [[256, 128], [128, 2], [1, 128]]),
                                in_=bass.AP(pA_tm, 0, [[256, 128], [128, 2], [1, 128]]),
                                transpose=True).then_inc(d_pA.h, 16)
                            d_pA.n += 16
                        if n < NCH:
                            xhtr(n + 2)
                        if n == 0:
                            for nm, srcap, sem, cnt in (
                                    ("tap_sp", sp_tm[:, 0:256], s_spf, 1),
                                    ("tap_A", A_tm[:, 0:256], s_A, 1)):
                                if nm in tap_aps:
                                    sync.wait_ge(sem.h, cnt)
                                    sync.dma_start(out=tap_aps[nm][:], in_=srcap
                                                   ).then_inc(d_pA.h, 16)
                                    d_pA.n += 16
                        if n == 1 and "tap_z" in tap_aps:
                            sync.wait_ge(v_z.h, 1)
                            sync.dma_start(out=tap_aps["tap_z"][:], in_=z_s[:]
                                           ).then_inc(d_pA.h, 16)
                            d_pA.n += 16
                    if "tap_o" in tap_aps:
                        sync.wait_ge(sod.h, 16 * NCH)
                        sync.dma_start(out=tap_aps["tap_o"][:], in_=cc_in[:]
                                       ).then_inc(d_pA.h, 16)
                        d_pA.n += 16

                @blk.gpsimd
                def _(g):
                    def gxload(n):
                        if n >= NCH:
                            return
                        if n >= 4:
                            g.wait_ge(s_sq.h, n - 3)
                            g.wait_ge(g_xh.h, n - 3)
                        g.wait_ge(sx.h, sx.n)
                        g.dma_start(out=x_t[:, (n % 4) * D:(n % 4 + 1) * D],
                                    in_=xb[n * C:(n + 1) * C, :]).then_inc(sx.h, 16)
                        sx.n += 16

                    def gostore(n):
                        if n < 0 or n >= NCH:
                            return
                        p = n % 2
                        g.wait_ge(v_o.h, n + 1)
                        g.wait_ge(sod.h, sod.n)
                        g.dma_start(out=cc_in[n * C:(n + 1) * C, :],
                                    in_=o_sw[:, p * 256:(p + 1) * 256]
                                    ).then_inc(sod.h, 16)
                        sod.n += 16

                    gxload(0); gxload(1); gxload(2)
                    PCE = (E * D * EH) // 64
                    cast_pieces = [(wt, k) for k in range(64)
                                   for wt in range(2)]
                    NQ = NCH // 4
                    for n in range(NCH):
                        gxload(n + 3)
                        gostore(n - 2)
                        lo = len(cast_pieces) * n // NCH
                        hi = len(cast_pieces) * (n + 1) // NCH
                        for wt, k in cast_pieces[lo:hi]:
                            wsrc = (w1T_d, w2T_d)[wt]
                            wdst = (w1b_d, w2b_d)[wt]
                            gc = ((gA0, gA1, gAr), (gB0, gB1, gBr))[wt][
                                0 if k < 8 else (1 if k < 16 else 2)]
                            g.dma_start(
                                out=bass.AP(wdst, k * PCE,
                                            [[PCE // 8, 8], [1, PCE // 8]]),
                                in_=bass.AP(wsrc, k * PCE,
                                            [[PCE // 8, 8], [1, PCE // 8]]),
                            ).then_inc(gc.h, 16)
                            gc.n += 16
                        for qa in range(3):
                            if n == NQ * (qa + 1) + 1:
                                g.wait_ge(sod.h, 16 * NQ * (qa + 1))
                                g.collective_compute(
                                    "AllGather", ALU.bypass,
                                    replica_groups=[[0, 1, 2, 3], [4, 5, 6, 7]],
                                    ins=[cc_in[qa * (T // 4):
                                               (qa + 1) * (T // 4), :].opt()],
                                    outs=[cc_out[qa * T:(qa + 1) * T, :].opt()],
                                ).then_inc(cc_s.h, 1)
                                cc_s.n += 1
                    gostore(NCH - 2)
                    gostore(NCH - 1)
                    g.wait_ge(sod.h, 16 * NCH)
                    g.collective_compute(
                        "AllGather", ALU.bypass,
                        replica_groups=[[0, 1, 2, 3], [4, 5, 6, 7]],
                        ins=[cc_in[3 * (T // 4):T, :].opt()],
                        outs=[cc_out[3 * T:4 * T, :].opt()],
                    ).then_inc(cc_s.h, 1)
                    cc_s.n += 1

                @blk.scalar
                def _(s):
                    def hop(inst):
                        inst.then_inc(s_ch.h, 1)
                        s_ch.n += 1
                        s.wait_ge(s_ch.h, s_ch.n)

                    def swish_sig(src_ap):
                        hop(s.activation(out=swi_scr[:], in_=src_ap, func=AF.Exp,
                                         scale=-1.0))
                        hop(s.activation(out=swi_scr[:], in_=swi_scr[:], func=AF.Ln,
                                         bias=1.0))
                        return s.activation(out=swi_scr[:], in_=swi_scr[:], func=AF.Exp,
                                            scale=-1.0)

                    def rms_xhat(k):
                        if k >= NCH:
                            return
                        pk = k % 2
                        s.wait_ge(sx.h, 16 * (k + 1))
                        aa = s.activation(out=sq_scr[:],
                                          in_=x_t[:, (k % 4) * D:(k % 4 + 1) * D],
                                          func=AF.Square,
                                          accum_out=rms_c[:, (pk * 4):(pk * 4 + 1)])
                        aa.then_inc(s_sq.h, 1); s_sq.n += 1
                        s.wait_ge(s_sq.h, s_sq.n)
                        hop(s.activation(out=rms_c[:, (pk * 4 + 1):(pk * 4 + 2)],
                                         in_=rms_c[:, (pk * 4):(pk * 4 + 1)],
                                         func=AF.Ln, scale=1.0 / D, bias=EPS))
                        aa = s.activation(out=rms_c[:, (pk * 4 + 2):(pk * 4 + 3)],
                                          in_=rms_c[:, (pk * 4 + 1):(pk * 4 + 2)],
                                          func=AF.Exp, scale=-0.5)
                        aa.then_inc(s_rinv.h, 1); s_rinv.n += 1
                        s.wait_ge(s_rinv.h, s_rinv.n)
                        if k >= 2:
                            s.wait_ge(d_xh.h, 16 * (k - 1))
                        aa = s.activation(out=xh_tm[:, pk * D:(pk + 1) * D],
                                          in_=x_t[:, (k % 4) * D:(k % 4 + 1) * D],
                                          func=AF.Identity,
                                          scale=rms_c[:, (pk * 4 + 2):(pk * 4 + 3)])
                        aa.then_inc(g_xh.h, 1); g_xh.n += 1

                    rms_xhat(0); rms_xhat(1)
                    for n in range(NCH + 2):
                        p = n % 2
                        q0 = p * 256
                        if n < NCH:
                            rms_xhat(n + 2)
                        if 1 <= n <= NCH:
                            s.wait_ge(v_hk.h, n)
                            s.wait_ge(t_z.h, n)
                            aa = s.activation(out=Hk_b[:], in_=Hk_f[:], func=AF.Identity)
                            aa.then_inc(s_hkb.h, 1); s_hkb.n += 1
                        if n < NCH:
                            s.wait_ge(t_f.h, n + 1)
                            if n >= 2:
                                s.wait_ge(t_S.h, n - 1)
                                s.wait_ge(s_eg.h, n - 1)
                                s.wait_ge(t_gam.h, n - 1)
                            aa = s.activation(out=sp_tm[:, q0:q0 + 256], in_=B7[:, 0:256],
                                              func=AF.Exp, scale=-1.0)
                            aa.then_inc(s_ef.h, 1); s_ef.n += 1
                            s.wait_ge(s_ef.h, s_ef.n)
                            aa = s.activation(out=sp_tm[:, q0:q0 + 256],
                                              in_=sp_tm[:, q0:q0 + 256],
                                              func=AF.Ln, bias=1.0)
                            aa.then_inc(s_spf.h, 1); s_spf.n += 1
                            s.wait_ge(s_spf.h, s_spf.n)
                            s.wait_ge(t_gam.h, n + 1)
                            if n >= 3:
                                s.wait_ge(v_pA.h, 2 * (n - 2))
                            if n >= 2:
                                s.wait_ge(t_abc.h, n - 1)
                            if n >= 1:
                                s.wait_ge(v_k2.h, n)
                            aa = s.activation(out=A_tm[:, (n % 3) * 256:(n % 3) * 256 + 256],
                                              in_=B2[:, 0:256], func=AF.Exp)
                            aa = s.activation(out=Ainv_tm[:], in_=B2[:, 0:256],
                                              func=AF.Exp, scale=-1.0)
                            aa.then_inc(s_A.h, 1); s_A.n += 1
                            aa = s.activation(out=eg_tm[:], in_=sp_tm[:, q0:q0 + 256],
                                              func=AF.Exp, scale=-1.0 / GATE_NORM)
                            aa.then_inc(s_eg.h, 1); s_eg.n += 1
                            s.wait_ge(t_qk.h, n + 1)
                            aa = swish_sig(B0[:, 0:256])
                            aa.then_inc(s_swqk.h, 1); s_swqk.n += 1
                            s.wait_ge(v_qk.h, 2 * n + 1)
                            aa = swish_sig(B0[:, 256:512])
                            aa.then_inc(s_swqk.h, 1); s_swqk.n += 1
                            s.wait_ge(t_v.h, n + 1)
                            s.wait_ge(v_qk.h, 2 * n + 2)
                            aa = swish_sig(B1[:, 0:256])
                            aa.then_inc(s_swv.h, 1); s_swv.n += 1
                            s.wait_ge(t_S.h, n + 1)
                            s.wait_ge(v_Sm.h, n + 1)
                            if n >= 1:
                                s.wait_ge(t_abc.h, n)
                            aa = s.activation(out=acr_s[0:1, :], in_=B4[0:1, 256:512],
                                              func=AF.Exp)
                            aa.then_inc(s_acr.h, 1); s_acr.n += 1
                            if n >= 2:
                                s.wait_ge(v_hv.h, n - 1)
                            pa2 = (n % 2) * 2
                            aa = s.activation(out=ac_col[:, pa2:pa2 + 1],
                                              in_=B4[:, 128:129], func=AF.Exp)
                            aa = s.activation(out=ac_col[:, pa2 + 1:pa2 + 2],
                                              in_=B4[:, 129:130], func=AF.Exp)
                            aa.then_inc(s_ac.h, 1); s_ac.n += 1
                        if n >= 2:
                            pz = (n - 2) % 2
                            s.wait_ge(v_z.h, n - 1)
                            aa = s.activation(out=ez_s[:], in_=z_s[:, pz * 256:(pz + 1) * 256],
                                              func=AF.Exp,
                                              bias=sm_cols[:, 4 + pz:5 + pz])
                            aa.then_inc(s_ez.h, 1); s_ez.n += 1
                            s.wait_ge(t_o_.h, n - 1)
                            if n >= 3:
                                s.wait_ge(v_o.h, n - 2)
                            hop(s.activation(out=swo_scr[:], in_=B3[:, 0:256],
                                             func=AF.Exp, scale=-1.0))
                            hop(s.activation(out=swo_scr[:], in_=swo_scr[:], func=AF.Ln,
                                             bias=1.0))
                            aa = s.activation(out=swo_scr[:], in_=swo_scr[:],
                                              func=AF.Exp, scale=-1.0)
                            aa.then_inc(s_o.h, 1); s_o.n += 1
                            s.wait_ge(v_hv.h, n - 1)
                            s.wait_ge(t_o_.h, n - 1)
                            aa = s.activation(out=Hv_b[:], in_=Hv_f[:], func=AF.Identity)
                            aa.then_inc(s_hvb.h, 1); s_hvb.n += 1

                @blk.tensor
                def _(t):
                    for n in range(NCH + 2):
                        p = n % 2
                        q0 = p * 256

                        def xh_c(dc):
                            return xh_fm[:, p * D + dc * 128: p * D + (dc + 1) * 128]

                        if n < NCH:
                            t.wait_ge(d_xh.h, 16 * (n + 1))
                            if n >= 1:
                                t.wait_ge(s_ef.h, n)
                                t.wait_ge(v_z.h, n)
                            for dc in range(8):
                                mm = t.matmul(B7[:, 0:256], xh_c(dc),
                                              wf_s[:, dc * 256:(dc + 1) * 256],
                                              start=(dc == 0), stop=(dc == 7))
                            mm.then_inc(t_f.h, 1); t_f.n += 1
                            if n >= 1:
                                t.wait_ge(v_v.h, n)
                            for dc in range(8):
                                mm = t.matmul(B1[:, 0:256], xh_c(dc),
                                              wv_s[:, dc * 256:(dc + 1) * 256],
                                              start=(dc == 0), stop=(dc == 7))
                            mm.then_inc(t_v.h, 1); t_v.n += 1
                            if n >= 1:
                                t.wait_ge(v_qk.h, 2 * n)
                            for j in range(2):
                                for dc in range(8):
                                    mm = t.matmul(
                                        B0[:, j * 128:(j + 1) * 128],
                                        wq_s[:, dc * 256 + j * 128: dc * 256 + (j + 1) * 128],
                                        xh_c(dc), start=(dc == 0), stop=(dc == 7))
                            for j in range(2):
                                for dc in range(8):
                                    mm = t.matmul(
                                        B0[:, 256 + j * 128: 256 + (j + 1) * 128],
                                        wk_s[:, dc * 256 + j * 128: dc * 256 + (j + 1) * 128],
                                        xh_c(dc), start=(dc == 0), stop=(dc == 7))
                            mm.then_inc(t_qk.h, 1); t_qk.n += 1
                            t.wait_ge(s_spf.h, n + 1)
                            if n >= 1:
                                t.wait_ge(s_A.h, n)
                                t.wait_ge(v_hk.h, n)
                            mm = t.matmul(B2[:, 0:256], Lg_s[:], sp_tm[:, q0:q0 + 256],
                                          start=True, stop=True)
                            mm.then_inc(t_gam.h, 1); t_gam.n += 1
                            t.wait_ge(v_qk.h, 2 * n + 2)
                            if n >= 1:
                                t.wait_ge(v_Sm.h, n)
                                t.wait_ge(s_acr.h, n)
                                t.wait_ge(s_ac.h, n)
                            for j in range(2):
                                mm = t.matmul(B4[:, 0:128],
                                              k_fm[:, q0 + j * 128: q0 + (j + 1) * 128],
                                              q_fm[:, q0 + j * 128: q0 + (j + 1) * 128],
                                              start=(j == 0), stop=(j == 1))
                            mm = t.matmul(B4[0:1, 256:512], negsix[:],
                                          sp_tm[:, q0:q0 + 256], start=True, stop=True)
                            mm = t.matmul(B4[:, 128:129], sp_tm[:, q0:q0 + 128],
                                          negsix[:], start=True, stop=True)
                            mm = t.matmul(B4[:, 129:130], sp_tm[:, q0 + 128:q0 + 256],
                                          negsix[:], start=True, stop=True)
                            mm.then_inc(t_S.h, 1); t_S.n += 1
                            t.wait_ge(s_A.h, n + 1)
                            t.wait_ge(s_acr.h, n + 1)
                            if n >= 1:
                                t.wait_ge(v_hk.h, n)
                            mm = t.matmul(B2[:, 256:512], triu_s[0:1, :], acr_s[0:1, :],
                                          start=True, stop=True)
                            mm.then_inc(t_abc.h, 1); t_abc.n += 1
                            t.wait_ge(d_k.h, 16 * (n + 1))
                            t.wait_ge(v_k2.h, n + 1)
                            if n >= 1:
                                t.wait_ge(v_hk.h, n)
                            for j in range(2):
                                mm = t.matmul(B5[:, j * 256:(j + 1) * 256],
                                              k_tm[:, q0 + j * 128: q0 + (j + 1) * 128],
                                              K2_tm[:, q0:q0 + 256], start=True, stop=True)
                            mm.then_inc(t_W.h, 1); t_W.n += 1
                            t.wait_ge(s_ef.h, n + 1)
                            t.wait_ge(s_hkb.h, n)
                            if n >= 1:
                                t.wait_ge(v_z.h, n)
                            t.wait_ge(v_Sm.h, n + 1)
                            for j in range(2):
                                mm = t.matmul(B7[:, 256:512],
                                              q_fm[:, q0 + j * 128: q0 + (j + 1) * 128],
                                              Hk_b[:, j * 256:(j + 1) * 256],
                                              start=(j == 0), stop=False)
                            mm = t.matmul(B7[:, 256:512], Sm_s[:], K2_tm[:, q0:q0 + 256],
                                          start=False, stop=True)
                            mm.then_inc(t_z.h, 1); t_z.n += 1
                        if 1 <= n <= NCH:
                            nu = n - 1
                            qu = (nu % 2) * 256
                            t.wait_ge(v_v.h, nu + 1)
                            t.wait_ge(v_k2.h, nu + 1)
                            if n >= 2:
                                t.wait_ge(v_hv.h, nu)
                            for j in range(2):
                                mm = t.matmul(B6[:, j * 256:(j + 1) * 256],
                                              K2_tm[:, qu + j * 128: qu + (j + 1) * 128],
                                              v_tm[:, (nu % 3) * 256:(nu % 3) * 256 + 256],
                                              start=True, stop=True)
                            mm.then_inc(t_U.h, 1); t_U.n += 1
                        if n >= 2:
                            nm = n - 2
                            qm3 = (nm % 3) * 256
                            t.wait_ge(d_k2.h, 16 * (nm + 1))
                            t.wait_ge(d_pA.h, 16 * (nm + 1))
                            if n >= 3:
                                t.wait_ge(v_o.h, nm)
                            for j in range(2):
                                mm = t.matmul(B3[:, 256:384],
                                              K2_fm[:, qm3 + j * 128: qm3 + (j + 1) * 128],
                                              pA_fm[:, j * 128:(j + 1) * 128],
                                              start=(j == 0), stop=(j == 1))
                            mm.then_inc(t_P2.h, 1); t_P2.n += 1
                            t.wait_ge(s_hvb.h, nm)
                            t.wait_ge(v_P2m.h, nm + 1)
                            t.wait_ge(v_v.h, nm + 1)
                            for j in range(2):
                                mm = t.matmul(B3[:, 0:256],
                                              pA_fm[:, j * 128:(j + 1) * 128],
                                              Hv_b[:, j * 256:(j + 1) * 256],
                                              start=(j == 0), stop=False)
                            mm = t.matmul(B3[:, 0:256], P2m_s[:], v_tm[:, qm3:qm3 + 256],
                                          start=False, stop=True)
                            mm.then_inc(t_o_.h, 1); t_o_.n += 1

                @blk.vector
                def _(v):
                    def vhop(inst):
                        inst.then_inc(v_ch.h, 1)
                        v_ch.n += 1
                        v.wait_ge(v_ch.h, v_ch.n)

                    for n in range(NCH + 2):
                        p = n % 2
                        q0 = p * 256
                        if n < NCH:
                            v.wait_ge(s_swqk.h, 2 * n + 1)
                            if n >= 2:
                                v.wait_ge(d_k.h, 16 * (n - 1))
                                v.wait_ge(t_S.h, n - 1)
                                v.wait_ge(t_z.h, n - 1)
                            vv = v.tensor_tensor(out=q_fm[:, q0:q0 + 256], in0=swi_scr[:],
                                                 in1=B0[:, 0:256], op=ALU.mult)
                            vv.then_inc(v_qk.h, 1); v_qk.n += 1
                            v.wait_ge(s_swqk.h, 2 * n + 2)
                            vv = v.tensor_tensor(out=k_fm[:, q0:q0 + 256], in0=swi_scr[:],
                                                 in1=B0[:, 256:512], op=ALU.mult)
                            vv.then_inc(v_qk.h, 1); v_qk.n += 1
                            v.wait_ge(s_swv.h, n + 1)
                            if n >= 3:
                                v.wait_ge(t_U.h, n - 2)
                                v.wait_ge(t_o_.h, n - 2)
                            vv = v.tensor_tensor(out=v_tm[:, (n % 3) * 256:(n % 3) * 256 + 256],
                                                 in0=swi_scr[:],
                                                 in1=B1[:, 0:256], op=ALU.mult)
                            vv.then_inc(v_v.h, 1); v_v.n += 1
                            v.wait_ge(s_eg.h, n + 1)
                            if n >= 2:
                                v.wait_ge(t_z.h, n - 1)
                                v.wait_ge(t_W.h, n - 1)
                                v.wait_ge(t_U.h, n - 1)
                                v.wait_ge(d_k2.h, 16 * (n - 1))
                            if n >= 3:
                                v.wait_ge(t_P2.h, n - 2)
                            vhop(v.tensor_scalar(out=s_tm[:], in0=eg_tm[:], scalar1=1.0,
                                                 scalar2=-1.0, op0=ALU.subtract,
                                                 op1=ALU.mult))
                            vv = v.tensor_tensor(out=K2_tm[:, q0:q0 + 256], in0=s_tm[:],
                                                 in1=Ainv_tm[:], op=ALU.mult)
                            vv.then_inc(v_k2.h, 1); v_k2.n += 1
                            v.wait_ge(t_S.h, n + 1)
                            if n >= 1:
                                v.wait_ge(t_z.h, n)
                            vv = v.tensor_tensor(out=Sm_s[:], in0=B4[:, 0:128],
                                                 in1=triu_s[:], op=ALU.mult)
                            vv.then_inc(v_Sm.h, 1); v_Sm.n += 1
                            v.wait_ge(t_W.h, n + 1)
                            v.wait_ge(t_abc.h, n + 1)
                            v.wait_ge(s_hkb.h, n)
                            for j in range(2):
                                vhop(v.tensor_tensor(out=Hk_f[:, j * 256:(j + 1) * 256],
                                                     in0=Hk_f[:, j * 256:(j + 1) * 256],
                                                     in1=B5[:, j * 256:(j + 1) * 256],
                                                     op=ALU.add))
                                vv = v.tensor_tensor(out=Hk_f[:, j * 256:(j + 1) * 256],
                                                     in0=Hk_f[:, j * 256:(j + 1) * 256],
                                                     in1=B2[:, 256:512], op=ALU.mult)
                                vhop(vv)
                            vv = v.tensor_copy(out=sm_cols[:, 3:4], in_=sm_cols[:, 3:4])
                            vv.then_inc(v_hk.h, 1); v_hk.n += 1
                            v.wait_ge(t_z.h, n + 1)
                            if n >= 2:
                                v.wait_ge(s_ez.h, n - 1)
                            pz = n % 2
                            vhop(v.tensor_tensor(out=z_s[:, pz * 256:(pz + 1) * 256],
                                                 in0=B7[:, 256:512],
                                                 in1=A_tm[:, (n % 3) * 256:(n % 3) * 256 + 256],
                                                 op=ALU.mult))
                            vv = v.reduce_max(sm_cols[:, 4 + pz:5 + pz],
                                              z_s[:, pz * 256:(pz + 1) * 256], axis=AX.X,
                                              negate=True)
                            vv.then_inc(v_z.h, 1); v_z.n += 1
                        if n >= 2:
                            nm = n - 2
                            qm3 = (nm % 3) * 256
                            v.wait_ge(s_ez.h, nm + 1)
                            vhop(v.reduce_sum(sm_cols[:, 1:2], ez_s[:], axis=AX.X))
                            vv = v.reciprocal(out=sm_cols[:, 2:3], in_=sm_cols[:, 1:2])
                            vv.then_inc(v_pA.h, 1); v_pA.n += 1
                            v.wait_ge(v_pA.h, v_pA.n)
                            if n >= 3:
                                v.wait_ge(d_pA.h, 16 * nm)
                            vv = v.scalar_tensor_tensor(out=pA_tm[:], in0=ez_s[:],
                                                        scalar=sm_cols[:, 2:3],
                                                        in1=A_tm[:, qm3:qm3 + 256],
                                                        op0=ALU.mult, op1=ALU.mult)
                            vv.then_inc(v_pA.h, 1); v_pA.n += 1
                            v.wait_ge(t_P2.h, nm + 1)
                            if n >= 3:
                                v.wait_ge(t_o_.h, nm)
                            vv = v.tensor_tensor(out=P2m_s[:], in0=B3[:, 256:384],
                                                 in1=triu_s[:], op=ALU.mult)
                            vv.then_inc(v_P2m.h, 1); v_P2m.n += 1
                            v.wait_ge(s_o.h, nm + 1)
                            if n >= 4:
                                v.wait_ge(sod.h, 16 * (nm - 1))
                            pm = nm % 2
                            vv = v.tensor_tensor(out=o_sw[:, pm * 256:(pm + 1) * 256],
                                                 in0=swo_scr[:], in1=B3[:, 0:256],
                                                 op=ALU.mult)
                            vv.then_inc(v_o.h, 1); v_o.n += 1
                        if 1 <= n <= NCH:
                            nu = n - 1
                            v.wait_ge(t_U.h, nu + 1)
                            v.wait_ge(s_ac.h, nu + 1)
                            v.wait_ge(s_hvb.h, nu)
                            for j in range(2):
                                vhop(v.tensor_tensor(out=Hv_f[:, j * 256:(j + 1) * 256],
                                                     in0=Hv_f[:, j * 256:(j + 1) * 256],
                                                     in1=B6[:, j * 256:(j + 1) * 256],
                                                     op=ALU.add))
                                vv = v.tensor_scalar_mul(
                                    out=Hv_f[:, j * 256:(j + 1) * 256],
                                    in0=Hv_f[:, j * 256:(j + 1) * 256],
                                    scalar1=ac_col[:, (nu % 2) * 2 + j:(nu % 2) * 2 + j + 1])
                                vhop(vv)
                            vv = v.tensor_copy(out=sm_cols[:, 3:4], in_=sm_cols[:, 3:4])
                            vv.then_inc(v_hv.h, 1); v_hv.n += 1

        if not phase2:
            with nc.Block() as blk:
                @blk.sync
                def _(sync):
                    sync.wait_ge(cc_s.h, 4)
            return nc, tap_aps

    so_r = S("so_r"); sxq = S("sxq"); s2_oh = S("s2_oh"); d2_oh = S("d2_oh")
    t2_at = S("t2_at"); v2_z1 = S("v2_z1"); s2_sq2 = S("s2_sq2"); v2_ln = S("v2_ln")
    s2_rs = S("s2_rs"); v2_nb = S("v2_nb"); s2_x1 = S("s2_x1"); v2_x1b = S("v2_x1b")
    d2_x1 = S("d2_x1"); t2_g = S("t2_g"); v2_sc = S("v2_sc")
    g2w1 = S("g2w1"); g2w2 = S("g2w2"); g2w2b = S("g2w2b"); t2_q = S("t2_q"); t2_m2e = S("t2_m2e")
    s2_h = S("s2_h"); t2_m1 = S("t2_m1"); t2_y = S("t2_y"); v2_y = S("v2_y")
    s2_ot = S("s2_ot"); v2_l2 = S("v2_l2"); sout = S("sout")

    with contextlib.ExitStack() as p2stack:
        def sb2(name, shape, dtype):
            return p2stack.enter_context(nc.sbuf_tensor(name, shape, dtype))

        ot_bf = sb2("ot_bf", [128, 3 * 1024], BF16)
        oh_fm = sb2("oh_fm", [128, 2 * 1024], BF16)
        xr_s = sb2("xr_s", [128, 1024], F32)
        z1_s = sb2("z1_s", [128, 1024], F32)
        ln_c = sb2("ln_c", [128, 16], F32)
        y_acc = sb2("y_acc", [128, NT * 1024], F32)
        x1bf = sb2("x1bf", [128, 1024], BF16)
        x1_fm = sb2("x1_fm", [128, 8 * TPC], BF16)
        scr2 = sb2("scr2", [128, 1024], BF16)
        scores = sb2("scores", [128, 8 * NT], F32)
        comb = sb2("comb", [128, 8 * NT], F32)
        tp_c = sb2("tp_c", [128, 8], F32)
        t2scr = sb2("t2scr", [128, 16], F32)
        w1s = sb2("w1s", [128, 8 * EH], BF16)
        w2s = sb2("w2s", [128, 16 * 1024], BF16)
        h_fm = sb2("h_fm", [128, 16 * TB], BF16)

        PG_cm = nc.psum_tensor("PG", [128, 512], F32); PG = PG_cm.__enter__()
        PA_cm = nc.psum_tensor("PA", [128, 1024], F32); PA = PA_cm.__enter__()

        with nc.Block() as blk:
            @blk.gpsimd
            def _(g):
                with g.register("rq") as rq, g.register("roff") as roff, \
                     g.register("rt") as rt:
                    g.reg_mod(rq, g.partition_id(), 4)
                    g.reg_mul(roff, rq, T * 256)
                    g.wait_ge(gB0.h, gB0.n)
                    g.wait_ge(s_hvb.h, s_hvb.n)
                    g.wait_ge(v_hv.h, v_hv.n)
                    g.dma_start(
                        out=bass.AP(w2s, 8 * 1024,
                                    [[16384, 128], [1024, 8], [1, 1024]]),
                        in_=bass.AP(w2b_d, 8 * 128 * 1024,
                                    [[1024, 128], [128 * 1024, 8], [1, 1024]]),
                    ).then_inc(g2w2b.h, 16)
                    g2w2b.n += 16
                    g.wait_ge(cc_s.h, 4)
                    for i in range(NT):
                        g.reg_mov(rt, i * C * 256)
                        g.reg_add(rt, rt, roff)
                        if i >= 3:
                            g.wait_ge(s2_oh.h, 4 * (i - 2))
                            g.wait_ge(d2_oh.h, 16 * (i - 2))
                        g.wait_ge(so_r.h, so_r.n)
                        g.dma_start(
                            out=bass.AP(ot_bf, (i % 3) * 1024,
                                        [[3072, 128], [256, 4], [1, 256]]),
                            in_=bass.AP(cc_out_d, rt,
                                        [[256, 128], [(T // 4) * 256, 4], [1, 256]]),
                        ).then_inc(so_r.h, 16)
                        so_r.n += 16

            @blk.sync
            def _(sync):
                sync.wait_ge(gA0.h, gA0.n)
                sync.wait_ge(s_hvb.h, s_hvb.n)
                sync.wait_ge(v_hv.h, v_hv.n)
                for qq in range(4):
                    sync.wait_ge(g2w1.h, g2w1.n)
                    sync.dma_start(
                        out=bass.AP(w1s, qq * 4096, [[16384, 128], [512, 8], [1, 512]]),
                        in_=bass.AP(w1b_d, qq * 512,
                                    [[EH, 128], [128 * EH, 8], [1, 512]]),
                    ).then_inc(g2w1.h, 16)
                    g2w1.n += 16
                for i in range(NT):
                    if i >= 1:
                        sync.wait_ge(v2_z1.h, i)
                    sync.wait_ge(sxq.h, sxq.n)
                    sync.dma_start(out=xr_s[:],
                                   in_=xq[i * C:(i + 1) * C, :]).then_inc(sxq.h, 16)
                    sxq.n += 16

            @blk.scalar
            def _(s):
                def ostage(k):
                    if k >= NT:
                        return
                    ko = (k % 3) * 1024
                    s.wait_ge(so_r.h, 16 * (k + 1))
                    aa = s.activation(out=scr2[:], in_=ot_bf[:, ko:ko + 1024],
                                 func=AF.Square,
                                 accum_out=ln_c[:, 0:1]).then_inc(s2_oh.h, 1)
                    s2_oh.n += 1
                    s.wait_ge(s2_oh.h, s2_oh.n)
                    aa = s.activation(out=ln_c[:, 1:2], in_=ln_c[:, 0:1], func=AF.Ln,
                                 scale=1.0 / D, bias=EPS).then_inc(s2_oh.h, 1)
                    s2_oh.n += 1
                    s.wait_ge(s2_oh.h, s2_oh.n)
                    aa = s.activation(out=ln_c[:, 2:3], in_=ln_c[:, 1:2], func=AF.Exp,
                                 scale=-0.5).then_inc(s2_oh.h, 1)
                    s2_oh.n += 1
                    s.wait_ge(s2_oh.h, s2_oh.n)
                    aa = s.activation(out=ot_bf[:, ko:ko + 1024],
                                 in_=ot_bf[:, ko:ko + 1024], func=AF.Identity,
                                 scale=ln_c[:, 2:3]).then_inc(s2_oh.h, 1)
                    s2_oh.n += 1
                    if k >= 2:
                        s.wait_ge(t2_at.h, k - 1)
                    s.wait_ge(s2_oh.h, s2_oh.n)
                    s.wait_ge(d2_oh.h, d2_oh.n)
                    s.dma_start(out=bass.AP(oh_fm, (k % 2) * 1024,
                                            [[2048, 128], [128, 8], [1, 128]]),
                                in_=bass.AP(ot_bf, ko,
                                            [[3072, 128], [128, 8], [1, 128]]),
                                transpose=True).then_inc(d2_oh.h, 16)
                    d2_oh.n += 16

                s.wait_ge(gB0.h, gB0.n)
                s.wait_ge(v_hv.h, v_hv.n)
                s.dma_start(
                    out=bass.AP(w2s, 0, [[16384, 128], [1024, 8], [1, 1024]]),
                    in_=bass.AP(w2b_d, 0, [[1024, 128], [128 * 1024, 8], [1, 1024]]),
                ).then_inc(g2w2.h, 16)
                g2w2.n += 16
                ostage(0)
                for i in range(NT):
                    ostage(i + 1)
                    s.wait_ge(v2_z1.h, i + 1)
                    aa = s.activation(out=scr2[:], in_=z1_s[:], func=AF.Square,
                                 accum_out=ln_c[:, 4:5]).then_inc(s2_sq2.h, 1)
                    s2_sq2.n += 1
                    s.wait_ge(v2_ln.h, i + 1)
                    aa = s.activation(out=ln_c[:, 8:9], in_=ln_c[:, 7:8], func=AF.Ln,
                                 bias=EPS).then_inc(s2_rs.h, 1)
                    s2_rs.n += 1
                    s.wait_ge(s2_rs.h, s2_rs.n)
                    aa = s.activation(out=ln_c[:, 9:10], in_=ln_c[:, 8:9], func=AF.Exp,
                                 scale=-0.5).then_inc(s2_rs.h, 1)
                    s2_rs.n += 1
                    s.wait_ge(s2_rs.h, s2_rs.n)
                    s.wait_ge(v2_nb.h, i + 1)
                    aa = s.activation(out=y_acc[:, i * 1024:(i + 1) * 1024], in_=z1_s[:],
                                 func=AF.Identity, scale=ln_c[:, 9:10],
                                 bias=ln_c[:, 11:12]).then_inc(s2_x1.h, 1)
                    s2_x1.n += 1
                    s.wait_ge(v2_x1b.h, i + 1)
                    s.wait_ge(d2_x1.h, d2_x1.n)
                    s.dma_start(out=bass.AP(x1_fm, i * 128,
                                            [[8 * TPC, 128], [TPC, 8], [1, 128]]),
                                in_=bass.AP(x1bf, 0, [[1024, 128], [128, 8], [1, 128]]),
                                transpose=True).then_inc(d2_x1.h, 16)
                    d2_x1.n += 16

            @blk.tensor
            def _(t):
                for i in range(NT):
                    t.wait_ge(d2_oh.h, 16 * (i + 1))
                    if i >= 1:
                        t.wait_ge(v2_z1.h, i)
                    ko = (i % 2) * 1024
                    for dc in range(8):
                        for half in range(2):
                            mm = t.matmul(PA[:, half * 512:(half + 1) * 512],
                                     oh_fm[:, ko + dc * 128: ko + (dc + 1) * 128],
                                     wo_s[:, dc * 1024 + half * 512:
                                          dc * 1024 + (half + 1) * 512],
                                     start=(dc == 0), stop=(dc == 7))
                    mm.then_inc(t2_at.h, 1); t2_at.n += 1

            @blk.vector
            def _(v):
                def vhop(inst):
                    inst.then_inc(v_ch.h, 1)
                    v_ch.n += 1
                    v.wait_ge(v_ch.h, v_ch.n)

                for i in range(NT):
                    v.wait_ge(t2_at.h, i + 1)
                    v.wait_ge(sxq.h, 16 * (i + 1))
                    if i >= 1:
                        v.wait_ge(s2_x1.h, i)
                    vhop(v.tensor_tensor(out=z1_s[:], in0=PA[:],
                                    in1=xr_s[:], op=ALU.add))
                    vv = v.reduce_sum(ln_c[:, 3:4], z1_s[:], axis=AX.X)
                    vv.then_inc(v2_z1.h, 1); v2_z1.n += 1
                    v.wait_ge(s2_sq2.h, i + 1)
                    vhop(v.tensor_scalar_mul(out=ln_c[:, 5:6], in0=ln_c[:, 3:4],
                                        scalar1=1.0 / D))
                    vv = v.tensor_tensor(out=ln_c[:, 6:7], in0=ln_c[:, 5:6],
                                    in1=ln_c[:, 5:6], op=ALU.mult)
                    vhop(v.tensor_scalar_mul(out=ln_c[:, 7:8], in0=ln_c[:, 4:5],
                                        scalar1=1.0 / D))
                    vv = v.tensor_tensor(out=ln_c[:, 7:8], in0=ln_c[:, 7:8],
                                    in1=ln_c[:, 6:7], op=ALU.subtract)
                    vv.then_inc(v2_ln.h, 1); v2_ln.n += 1
                    v.wait_ge(s2_rs.h, 2 * (i + 1))
                    vhop(v.tensor_tensor(out=ln_c[:, 10:11], in0=ln_c[:, 5:6],
                                    in1=ln_c[:, 9:10], op=ALU.mult))
                    vv = v.tensor_scalar_mul(out=ln_c[:, 11:12], in0=ln_c[:, 10:11],
                                        scalar1=-1.0)
                    vv.then_inc(v2_nb.h, 1); v2_nb.n += 1
                    v.wait_ge(s2_x1.h, i + 1)
                    if i >= 1:
                        v.wait_ge(d2_x1.h, 16 * i)
                    vv = v.tensor_copy(out=x1bf[:], in_=y_acc[:, i * 1024:(i + 1) * 1024])
                    vv.then_inc(v2_x1b.h, 1); v2_x1b.n += 1


        PA_cm.__exit__(None, None, None)
        PH0_cm = nc.psum_tensor("PH0", [128, 512], F32); PH0 = PH0_cm.__enter__()
        PH1_cm = nc.psum_tensor("PH1", [128, 512], F32); PH1 = PH1_cm.__enter__()
        PY0_cm = nc.psum_tensor("PY0", [128, 1024], F32); PY0 = PY0_cm.__enter__()
        PY1_cm = nc.psum_tensor("PY1", [128, 1024], F32); PY1 = PY1_cm.__enter__()

        with nc.Block() as blk:
            @blk.tensor
            def _(t):
                for i in range(NT):
                    t.wait_ge(d2_x1.h, 16 * (i + 1))
                    if i >= 1:
                        t.wait_ge(v2_sc.h, i)
                    for dc in range(8):
                        mm = t.matmul(PG[:, 0:8],
                                 x1_fm[:, dc * TPC + i * 128: dc * TPC + (i + 1) * 128],
                                 gate_s[:, dc * 8:(dc + 1) * 8],
                                 start=(dc == 0), stop=(dc == 7))
                    mm.then_inc(t2_g.h, 1); t2_g.n += 1

            @blk.vector
            def _(v2x):
                for i in range(NT):
                    v2x.wait_ge(t2_g.h, i + 1)
                    vv = v2x.tensor_tensor(out=scores[:, i * 8:(i + 1) * 8],
                                           in0=PG[:, 0:8], in1=bbt_s[:], op=ALU.add)
                    vv.then_inc(v2_sc.h, 1); v2_sc.n += 1

            @blk.scalar
            def _(s):
                for i in range(NT):
                    s.wait_ge(v2_sc.h, i + 1)
                    aa = s.activation(out=scores[:, i * 8:(i + 1) * 8],
                                 in_=scores[:, i * 8:(i + 1) * 8],
                                 func=AF.Sigmoid)
                aa.then_inc(s2_ot.h, 1)
                s2_ot.n += 1

            @blk.vector
            def _(v):
                def vhop(inst):
                    inst.then_inc(v_ch.h, 1)
                    v_ch.n += 1
                    v.wait_ge(v_ch.h, v_ch.n)

                v.wait_ge(s2_ot.h, 1)
                for i in range(NT):
                    sc = scores[:, i * 8:(i + 1) * 8]
                    vv = v.reduce_max(tp_c[:, 0:1], sc, axis=AX.X)
                    vv.then_inc(v2_l2.h, 1); v2_l2.n += 1
                    v.wait_ge(v2_l2.h, v2_l2.n)
                    vhop(v.tensor_single_scalar(out=comb[:, i * 8:(i + 1) * 8],
                                           in_=sc, scalar=tp_c[:, 0:1], op=ALU.is_ge))
                    vhop(v.scalar_tensor_tensor(out=t2scr[:, 0:8], in0=comb[:, i * 8:(i + 1) * 8],
                                           scalar=-2.0, in1=sc, op0=ALU.mult, op1=ALU.add))
                    vv = v.reduce_max(tp_c[:, 1:2], t2scr[:, 0:8], axis=AX.X)
                    vv.then_inc(v2_l2.h, 1); v2_l2.n += 1
                    v.wait_ge(v2_l2.h, v2_l2.n)
                    vhop(v.tensor_single_scalar(out=t2scr[:, 8:16], in_=t2scr[:, 0:8],
                                           scalar=tp_c[:, 1:2], op=ALU.is_ge))
                    vhop(v.tensor_tensor(out=comb[:, i * 8:(i + 1) * 8],
                                    in0=comb[:, i * 8:(i + 1) * 8],
                                    in1=t2scr[:, 8:16], op=ALU.add))
                    vv = v.tensor_scalar(out=tp_c[:, 2:3], in0=tp_c[:, 0:1],
                                    scalar1=tp_c[:, 1:2], scalar2=1e-6,
                                    op0=ALU.add, op1=ALU.add)
                    vv.then_inc(v2_l2.h, 1); v2_l2.n += 1
                    v.wait_ge(v2_l2.h, v2_l2.n)
                    vv = v.reciprocal(out=tp_c[:, 3:4], in_=tp_c[:, 2:3])
                    vv.then_inc(v2_l2.h, 1); v2_l2.n += 1
                    v.wait_ge(v2_l2.h, v2_l2.n)
                    vv = v.scalar_tensor_tensor(out=comb[:, i * 8:(i + 1) * 8], in0=sc,
                                           scalar=tp_c[:, 3:4],
                                           in1=comb[:, i * 8:(i + 1) * 8],
                                           op0=ALU.mult, op1=ALU.mult)
                vv.then_inc(v2_l2.h, 1); v2_l2.n += 1

            @blk.sync
            def _(sync):
                if "tap_x1" in tap_aps:
                    for i in range(NT):
                        sync.dma_start(out=tap_aps["tap_x1"][i * 128:(i + 1) * 128, :],
                                       in_=y_acc[:, i * 1024:(i + 1) * 1024]
                                       ).then_inc(sout.h, 16)
                        sout.n += 16
                if "tap_comb" in tap_aps:
                    sync.wait_ge(v2_l2.h, v2_l2.n)
                    sync.dma_start(out=tap_aps["tap_comb"][:], in_=comb[:]
                                   ).then_inc(sout.h, 16)
                    sout.n += 16

            @blk.sync
            def _(sync):
                for e in range(1, E):
                    ga = (gA1 if e == 1 else gAr)
                    sync.wait_ge(ga.h, ga.n)
                    for qq in range(4):
                        sync.wait_ge(t2_m1.h,
                                     16 * NTB * (e - 1) + 16 * (NTB - 1) + 4 * qq + 4)
                        sync.wait_ge(g2w1.h, g2w1.n)
                        sync.dma_start(
                            out=bass.AP(w1s, qq * 4096,
                                        [[16384, 128], [512, 8], [1, 512]]),
                            in_=bass.AP(w1b_d, e * D * EH + qq * 512,
                                        [[EH, 128], [128 * EH, 8], [1, 512]]),
                        ).then_inc(g2w1.h, 16)
                        g2w1.n += 16

            @blk.tensor
            def _(t):
                for e in range(E):
                    for tb in range(NTB):
                        for ehc in range(16):
                            qq = ehc // 4
                            er = ehc % 4
                            if er == 0:
                                t.wait_ge(g2w1.h, 16 * (4 * e + qq + 1))
                            gsil = 16 * (NTB * e + tb) + ehc
                            if gsil >= 2:
                                t.wait_ge(s2_h.h, gsil - 1)
                            ph = (PH0, PH1)[ehc % 2]
                            for dc in range(8):
                                col = qq * 4096 + dc * 512 + er * 128
                                mm = t.matmul(ph[:, 0:TB], w1s[:, col: col + 128],
                                         x1_fm[:, dc * TPC + tb * TB:
                                               dc * TPC + (tb + 1) * TB],
                                         start=(dc == 0), stop=(dc == 7))
                            mm.then_inc(t2_m1.h, 1); t2_m1.n += 1
                        t.wait_ge(g2w2.h, 16 * (e + 1))
                        t.wait_ge(g2w2b.h, 16 * (e + 1))
                        t.wait_ge(s2_h.h, 16 * (NTB * e + tb + 1))
                        for tt in range(TTB):
                            gt = (NTB * e + tb) * TTB + tt
                            if gt >= 2:
                                t.wait_ge(v2_y.h, gt - 1)
                            PY = (PY0, PY1)[gt % 2]
                            for ehc in range(16):
                                hh = h_fm[:, ehc * TB + tt * 128: ehc * TB + (tt + 1) * 128]
                                mm = t.matmul(PY[:, 0:512], hh,
                                         w2s[:, ehc * 1024: ehc * 1024 + 512],
                                         start=(ehc == 0), stop=(ehc == 15))
                                mm = t.matmul(PY[:, 512:1024], hh,
                                         w2s[:, ehc * 1024 + 512: (ehc + 1) * 1024],
                                         start=(ehc == 0), stop=(ehc == 15))
                            mm.then_inc(t2_y.h, 1); t2_y.n += 1


            @blk.scalar
            def _(s):
                for e in range(E):
                    if e >= 1:
                        gb = (gB1 if e == 1 else gBr)
                        s.wait_ge(gb.h, gb.n)
                        s.wait_ge(t2_y.h, NTB * TTB * e)
                        s.dma_start(
                            out=bass.AP(w2s, 0, [[16384, 128], [1024, 8], [1, 1024]]),
                            in_=bass.AP(w2b_d, e * EH * D,
                                        [[1024, 128], [128 * 1024, 8], [1, 1024]]),
                        ).then_inc(g2w2.h, 16)
                        g2w2.n += 16
                    for tb in range(NTB):
                        for ehc in range(16):
                            s.wait_ge(t2_m1.h, 16 * (NTB * e + tb) + ehc + 1)
                            ph = (PH0, PH1)[ehc % 2]
                            aa = s.activation(out=h_fm[:, ehc * TB:(ehc + 1) * TB],
                                         in_=ph[:, 0:TB], func=AF.Silu
                                         ).then_inc(s2_h.h, 1)
                            s2_h.n += 1

            @blk.gpsimd
            def _(g):
                for e in range(1, E):
                    gb = (gB1 if e == 1 else gBr)
                    g.wait_ge(gb.h, gb.n)
                    g.wait_ge(t2_y.h, NTB * TTB * e)
                    g.dma_start(
                        out=bass.AP(w2s, 8 * 1024,
                                    [[16384, 128], [1024, 8], [1, 1024]]),
                        in_=bass.AP(w2b_d, e * EH * D + 8 * 128 * 1024,
                                    [[1024, 128], [128 * 1024, 8], [1, 1024]]),
                    ).then_inc(g2w2b.h, 16)
                    g2w2b.n += 16

            @blk.vector
            def _(v):
                v.wait_ge(v2_l2.h, v2_l2.n)
                for e in range(E):
                    for tb in range(NTB):
                        for tt in range(TTB):
                            gt = (NTB * e + tb) * TTB + tt
                            tile = tb * TTB + tt
                            v.wait_ge(t2_y.h, gt + 1)
                            PY = (PY0, PY1)[gt % 2]
                            vv = v.scalar_tensor_tensor(
                                out=y_acc[:, tile * 1024:(tile + 1) * 1024],
                                in0=PY[:], scalar=comb[:, tile * 8 + e: tile * 8 + e + 1],
                                in1=y_acc[:, tile * 1024:(tile + 1) * 1024],
                                op0=ALU.mult, op1=ALU.add)
                            vv.then_inc(v2_y.h, 1); v2_y.n += 1

        with nc.Block() as blk:
            @blk.scalar
            def _(s):
                for i in range(NT):
                    yt = y_acc[:, i * 1024:(i + 1) * 1024]
                    aa = s.activation(out=scr2[:], in_=yt, func=AF.Square,
                                 accum_out=ln_c[:, 4:5]).then_inc(s2_sq2.h, 1)
                    s2_sq2.n += 1
                    s.wait_ge(v2_ln.h, v2_ln.n + i + 1)
                    aa = s.activation(out=ln_c[:, 8:9], in_=ln_c[:, 7:8], func=AF.Ln,
                                 bias=EPS).then_inc(s2_rs.h, 1)
                    s2_rs.n += 1
                    s.wait_ge(s2_rs.h, s2_rs.n)
                    aa = s.activation(out=ln_c[:, 9:10], in_=ln_c[:, 8:9], func=AF.Exp,
                                 scale=-0.5).then_inc(s2_rs.h, 1)
                    s2_rs.n += 1
                    s.wait_ge(s2_rs.h, s2_rs.n)
                    s.wait_ge(v2_nb.h, v2_nb.n + i + 1)
                    if i >= 1:
                        s.wait_ge(sout.h, sout.n + 16 * i)
                    aa = s.activation(out=z1_s[:],
                                 in_=yt, func=AF.Identity, scale=ln_c[:, 9:10],
                                 bias=ln_c[:, 11:12]).then_inc(s2_ot.h, 1)
                    s2_ot.n += 1

            @blk.vector
            def _(v):
                def vhop(inst):
                    inst.then_inc(v_ch.h, 1)
                    v_ch.n += 1
                    v.wait_ge(v_ch.h, v_ch.n)

                for i in range(NT):
                    yt = y_acc[:, i * 1024:(i + 1) * 1024]
                    vhop(v.reduce_sum(ln_c[:, 3:4], yt, axis=AX.X))
                    v.wait_ge(s2_sq2.h, s2_sq2.n - NT + i + 1)
                    vhop(v.tensor_scalar_mul(out=ln_c[:, 5:6], in0=ln_c[:, 3:4],
                                        scalar1=1.0 / D))
                    vv = v.tensor_tensor(out=ln_c[:, 6:7], in0=ln_c[:, 5:6],
                                    in1=ln_c[:, 5:6], op=ALU.mult)
                    vhop(v.tensor_scalar_mul(out=ln_c[:, 7:8], in0=ln_c[:, 4:5],
                                        scalar1=1.0 / D))
                    vv = v.tensor_tensor(out=ln_c[:, 7:8], in0=ln_c[:, 7:8],
                                    in1=ln_c[:, 6:7], op=ALU.subtract)
                    vv.then_inc(v2_ln.h, 1); v2_ln.n += 1
                    v.wait_ge(s2_rs.h, s2_rs.n - 2 * NT + 2 * (i + 1))
                    vhop(v.tensor_tensor(out=ln_c[:, 10:11], in0=ln_c[:, 5:6],
                                    in1=ln_c[:, 9:10], op=ALU.mult))
                    vv = v.tensor_scalar_mul(out=ln_c[:, 11:12], in0=ln_c[:, 10:11],
                                        scalar1=-1.0)
                    vv.then_inc(v2_nb.h, 1); v2_nb.n += 1

            @blk.sync
            def _(sync):
                base = s2_ot.n - NT
                for i in range(NT):
                    sync.wait_ge(s2_ot.h, base + i + 1)
                    sync.wait_ge(sout.h, sout.n)
                    sync.dma_start(out=out[i * 128:(i + 1) * 128, :],
                                   in_=z1_s[:]).then_inc(sout.h, 16)
                    sout.n += 16
                sync.wait_ge(sout.h, sout.n)

        PY1_cm.__exit__(None, None, None)
        PY0_cm.__exit__(None, None, None)
        PH1_cm.__exit__(None, None, None)
        PH0_cm.__exit__(None, None, None)
        PG_cm.__exit__(None, None, None)

    return nc, tap_aps


_CACHE = {}


def _get_nc():
    if "nc" not in _CACHE:
        _CACHE["nc"] = build(T=T_FULL, TPC=TPC_FULL)[0]
    return _CACHE["nc"]


def kernel(L0_attn_hidden_states, L0_attn_norm_weight, L0_attn_w_q, L0_attn_w_k,
           L0_attn_w_v, L0_attn_w_f, L0_attn_gnorm_weight, L0_attn_w_o,
           L0_ffn_gate_w, L0_ffn_balance_bias, L0_ffn_w1, L0_ffn_w2):
    x = np.asarray(L0_attn_hidden_states)
    nw = np.asarray(L0_attn_norm_weight)
    gw = np.asarray(L0_attn_gnorm_weight)
    w_q, w_k = np.asarray(L0_attn_w_q), np.asarray(L0_attn_w_k)
    w_v, w_f = np.asarray(L0_attn_w_v), np.asarray(L0_attn_w_f)
    w_o = np.asarray(L0_attn_w_o)
    gate_w = np.asarray(L0_ffn_gate_w)
    bb = np.asarray(L0_ffn_balance_bias)
    w1, w2 = np.asarray(L0_ffn_w1), np.asarray(L0_ffn_w2)

    triu = np.triu(np.ones((128, 128), np.float32))
    Lg = np.ascontiguousarray(triu * np.float32(-1.0 / GATE_NORM))
    w1T = np.ascontiguousarray(w1.transpose(0, 2, 1))
    w2T = np.ascontiguousarray(w2.transpose(0, 2, 1))
    wqT_f = np.ascontiguousarray(w_q.T)
    wkT_f = np.ascontiguousarray(w_k.T)
    wvT_f = np.ascontiguousarray(w_v.T)
    wfT_f = np.ascontiguousarray(w_f.T)
    woT = np.ascontiguousarray(w_o.T)
    nwc = np.ascontiguousarray(nw.reshape(8, 128).T)
    gwc = np.ascontiguousarray(gw.reshape(8, 128).T)
    gateT = np.ascontiguousarray(gate_w.T)
    bbt = np.ascontiguousarray(np.broadcast_to(bb, (128, 8)))

    in_maps = []
    for c in range(8):
        b, h = c // 4, c % 4
        qq = c % 4
        in_maps.append(dict(
            xb=np.ascontiguousarray(x[b]),
            xq=np.ascontiguousarray(x[b, qq * TPC_FULL:(qq + 1) * TPC_FULL]),
            wqT=np.ascontiguousarray(wqT_f[:, 256 * h:256 * (h + 1)]),
            wkT=np.ascontiguousarray(wkT_f[:, 256 * h:256 * (h + 1)]),
            wvT=np.ascontiguousarray(wvT_f[:, 256 * h:256 * (h + 1)]),
            wfT=np.ascontiguousarray(wfT_f[:, 256 * h:256 * (h + 1)]),
            woT=woT, nwc=nwc, gwc=gwc, gateT=gateT, bbt=bbt,
            w1T=w1T, w2T=w2T, triu=triu, Lg=Lg,
        ))

    nc = _get_nc()
    res = run_bass_kernel_spmd(nc, in_maps, core_ids=list(range(8)))
    out = np.concatenate([np.asarray(res.results[c]["out"]) for c in range(8)],
                         axis=0).reshape(2, T_FULL, D)
    return out.astype(np.float32)


# version 15
# speedup vs baseline: 1.0137x; 1.0076x over previous
import contextlib
import numpy as np
import concourse.bass as bass
import concourse.mybir as mybir
from concourse.bass_utils import run_bass_kernel_spmd

F32 = mybir.dt.float32
BF16 = mybir.dt.bfloat16
AF = mybir.ActivationFunctionType
ALU = mybir.AluOpType
AX = mybir.AxisListType

D = 1024
C = 128
E, EH = 8, 2048
GATE_NORM = 16.0
EPS = 1e-5

T_FULL = 8192
TPC_FULL = 2048

class Ctr:
    def __init__(self, nc, name):
        self.h = nc.semaphore(name).__enter__()
        self.n = 0


def build(T=8192, TPC=2048, taps=(), phase2=True, detect_races=False):
    NCH = T // C
    NT = TPC // C
    TB = 256
    NTB = TPC // TB
    TTB = TB // C
    nc = bass.Bass("TRN2", target_bir_lowering=False, detect_race_conditions=detect_races)

    xb = nc.dram_tensor("xb", [T, D], F32, kind="ExternalInput").ap()
    xq = nc.dram_tensor("xq", [TPC, D], F32, kind="ExternalInput").ap()
    wqT_d = nc.dram_tensor("wqT", [D, 256], F32, kind="ExternalInput")
    wkT_d = nc.dram_tensor("wkT", [D, 256], F32, kind="ExternalInput")
    wvT_d = nc.dram_tensor("wvT", [D, 256], F32, kind="ExternalInput")
    wfT_d = nc.dram_tensor("wfT", [D, 256], F32, kind="ExternalInput")
    woT_d = nc.dram_tensor("woT", [D, D], F32, kind="ExternalInput")
    nwc = nc.dram_tensor("nwc", [128, 8], F32, kind="ExternalInput").ap()
    gwc = nc.dram_tensor("gwc", [128, 8], F32, kind="ExternalInput").ap()
    gateT_d = nc.dram_tensor("gateT", [D, 8], F32, kind="ExternalInput")
    bbt_in = nc.dram_tensor("bbt", [128, 8], F32, kind="ExternalInput").ap()
    w1T_d = nc.dram_tensor("w1T", [E, D, EH], F32, kind="ExternalInput")
    w2T_d = nc.dram_tensor("w2T", [E, EH, D], F32, kind="ExternalInput")
    w1b_d = nc.dram_tensor("w1bf", [E * D * EH], BF16, kind="Internal")
    w2b_d = nc.dram_tensor("w2bf", [E * EH * D], BF16, kind="Internal")
    triu_in = nc.dram_tensor("triu", [128, 128], F32, kind="ExternalInput").ap()
    Lg_in = nc.dram_tensor("Lg", [128, 128], F32, kind="ExternalInput").ap()

    out = nc.dram_tensor("out", [TPC, D], F32, kind="ExternalOutput").ap()
    cc_in = nc.dram_tensor("cc_in", [T, 256], BF16, kind="Internal").ap()
    cc_out_d = nc.dram_tensor("cc_out", [4 * T, 256], BF16, kind="Internal")
    cc_out = cc_out_d.ap()

    tap_aps = {}
    def tap(name, shape, dtype=F32):
        if name in taps:
            tap_aps[name] = nc.dram_tensor(name, shape, dtype, kind="ExternalOutput").ap()

    tap("tap_o", [T, 256], BF16)
    tap("tap_x1", [TPC, D])
    tap("tap_comb", [128, 8 * NT])
    tap("tap_sp", [128, 256])
    tap("tap_A", [128, 256])
    tap("tap_z", [128, 256])

    def sb(name, shape, dtype):
        return nc.sbuf_tensor(name, shape, dtype).__enter__()

    wo_s = sb("wo_s", [128, 8 * 1024], BF16)
    gate_s = sb("gate_s", [128, 64], BF16)
    nwc_s = sb("nwc_s", [128, 8], F32)
    gwc_s = sb("gwc_s", [128, 8], F32)
    bbt_s = sb("bbt_s", [128, 8], F32)
    negsix = sb("negsix", [128, 1], F32)
    consts = sb("consts", [128, 2], F32)
    nc.const_aps.register(F32, EPS, consts[:, 0:1])
    nc.const_aps.register(F32, 1.0, consts[:, 1:2])

    S = lambda name: Ctr(nc, name)
    sx = S("sx"); s_sq = S("s_sq"); s_rinv = S("s_rinv"); g_xh = S("g_xh")
    d_xh = S("d_xh"); t_proj = S("t_proj"); t_f = S("t_f"); t_v = S("t_v")
    t_qk = S("t_qk"); s_ef = S("s_ef"); s_spf = S("s_spf"); t_gam = S("t_gam")
    t_gc = S("t_gc"); s_A = S("s_A"); s_eg = S("s_eg"); v_k2 = S("v_k2")
    d_k2 = S("d_k2"); s_swqk = S("s_swqk"); v_qk = S("v_qk"); d_k = S("d_k")
    s_swv = S("s_swv"); v_v = S("v_v"); t_S = S("t_S"); v_Sm = S("v_Sm")
    t_z = S("t_z"); v_z = S("v_z"); s_ez = S("s_ez"); v_pA = S("v_pA")
    d_pA = S("d_pA"); t_P2 = S("t_P2"); v_P2m = S("v_P2m"); t_o_ = S("t_o")
    s_o = S("s_o"); v_o = S("v_o"); sod = S("sod"); t_W = S("t_W")
    t_U = S("t_U"); t_abc = S("t_abc"); s_ac = S("s_ac"); v_hk = S("v_hk")
    v_hv = S("v_hv"); prep = S("prep"); prepg = S("prepg"); cc_s = S("cc_s"); s_acr = S("s_acr"); s_ch = S("s_ch"); v_ch = S("v_ch")
    s_hkb = S("s_hkb"); s_hvb = S("s_hvb"); gA0 = S("gA0"); gA1 = S("gA1"); gAr = S("gAr"); gB0 = S("gB0"); gB1 = S("gB1"); gBr = S("gBr")

    with contextlib.ExitStack() as p1stack:
        def sb1(name, shape, dtype):
            return p1stack.enter_context(nc.sbuf_tensor(name, shape, dtype))

        triu_s = sb1("triu_s", [128, 128], F32)
        Lg_s = sb1("Lg_s", [128, 128], F32)
        wq_s = sb1("wq_s", [128, 8 * 256], BF16)
        wk_s = sb1("wk_s", [128, 8 * 256], BF16)
        wv_s = sb1("wv_s", [128, 8 * 256], BF16)
        wf_s = sb1("wf_s", [128, 8 * 256], BF16)
        x_t = sb1("x_t", [128, 4 * D], F32)
        sq_scr = sb1("sq_scr", [128, D], F32)
        rms_c = sb1("rms_c", [128, 8], F32)
        xh_tm = sb1("xh_tm", [128, 2 * D], BF16)
        xh_fm = sb1("xh_fm", [128, 2 * D], BF16)
        q_fm = sb1("q_fm", [128, 2 * 256], BF16)
        k_fm = sb1("k_fm", [128, 2 * 256], BF16)
        k_tm = sb1("k_tm", [128, 2 * 256], BF16)
        v_tm = sb1("v_tm", [128, 3 * 256], BF16)
        sp_tm = sb1("sp_tm", [128, 2 * 256], F32)
        A_tm = sb1("A_tm", [128, 3 * 256], F32)
        Ainv_tm = sb1("Ainv_tm", [128, 256], F32)
        eg_tm = sb1("eg_tm", [128, 256], F32)
        s_tm = sb1("s_tm", [128, 256], F32)
        K2_tm = sb1("K2_tm", [128, 2 * 256], BF16)
        K2_fm = sb1("K2_fm", [128, 3 * 256], BF16)
        Sm_s = sb1("Sm_s", [128, 128], BF16)
        P2m_s = sb1("P2m_s", [128, 128], BF16)
        z_s = sb1("z_s", [128, 2 * 256], F32)
        ez_s = sb1("ez_s", [128, 256], F32)
        pA_tm = sb1("pA_tm", [128, 256], BF16)
        pA_fm = sb1("pA_fm", [128, 256], BF16)
        sm_cols = sb1("sm_cols", [128, 8], F32)
        ac_col = sb1("ac_col", [128, 4], F32)
        swi_scr = sb1("swi_scr", [128, 256], F32)
        swo_scr = sb1("swo_scr", [128, 256], F32)
        o_sw = sb1("o_sw", [128, 2 * 256], BF16)
        Hk_f = sb1("Hk_f", [128, 512], F32)
        Hv_f = sb1("Hv_f", [128, 512], F32)
        Hk_b = sb1("Hk_b", [128, 512], BF16)
        Hv_b = sb1("Hv_b", [128, 512], BF16)
        acr_s = sb1("acr_s", [1, 256], F32)

        with nc.Block() as blk:
            @blk.sync
            def _(sync):
                for (dst, src) in ((nwc_s, nwc), (gwc_s, gwc), (bbt_s, bbt_in),
                                   (triu_s, triu_in), (Lg_s, Lg_in)):
                    sync.dma_start(out=dst[:], in_=src[:]).then_inc(prep.h, 16)
                    prep.n += 16

            @blk.gpsimd
            def _(g):
                for (dst, src) in ((wq_s, wqT_d), (wk_s, wkT_d), (wv_s, wvT_d),
                                   (wf_s, wfT_d)):
                    g.dma_start(
                        out=bass.AP(dst, 0, [[2048, 128], [256, 8], [1, 256]]),
                        in_=bass.AP(src, 0, [[256, 128], [128 * 256, 8], [1, 256]]),
                    ).then_inc(prepg.h, 16)
                    prepg.n += 16
                g.dma_start(
                    out=bass.AP(wo_s, 0, [[8192, 128], [1024, 8], [1, 1024]]),
                    in_=bass.AP(woT_d, 0, [[1024, 128], [128 * 1024, 8], [1, 1024]]),
                ).then_inc(prepg.h, 16)
                prepg.n += 16
                g.dma_start(
                    out=bass.AP(gate_s, 0, [[64, 128], [8, 8], [1, 8]]),
                    in_=bass.AP(gateT_d, 0, [[8, 128], [128 * 8, 8], [1, 8]]),
                ).then_inc(prepg.h, 16)
                prepg.n += 16

            @blk.vector
            def _(v):
                v.memset(negsix[:], -1.0 / GATE_NORM)
                v.memset(sm_cols[:], 0.0)
                v.memset(consts[:, 0:1], EPS)
                v.memset(consts[:, 1:2], 1.0)
                v.memset(Hk_f[:], 0.0)
                v.memset(Hv_f[:], 0.0)
                v.memset(Hk_b[:], 0.0)
                v.memset(Hv_b[:], 0.0)
                v.wait_ge(prep.h, prep.n)
                v.wait_ge(prepg.h, prepg.n)
                for w in (wq_s, wk_s, wv_s, wf_s):
                    for dc in range(8):
                        vv = v.tensor_scalar_mul(out=w[:, dc * 256:(dc + 1) * 256],
                                            in0=w[:, dc * 256:(dc + 1) * 256],
                                            scalar1=nwc_s[:, dc:dc + 1])
                for dc in range(8):
                    vv = v.tensor_scalar_mul(out=wo_s[:, dc * 1024:(dc + 1) * 1024],
                                        in0=wo_s[:, dc * 1024:(dc + 1) * 1024],
                                        scalar1=gwc_s[:, dc:dc + 1])

        with contextlib.ExitStack() as psstack:
            def ps(name, shape, dtype):
                return psstack.enter_context(nc.psum_tensor(name, shape, dtype))

            B0 = ps("B0", [128, 512], F32)
            B1 = ps("B1", [128, 512], F32)
            B2 = ps("B2", [128, 512], F32)
            B3 = ps("B3", [128, 512], F32)
            B4 = ps("B4", [128, 512], F32)
            B5 = ps("B5", [128, 512], F32)
            B6 = ps("B6", [128, 512], F32)
            B7 = ps("B7", [128, 512], F32)

            with nc.Block() as blk:
                @blk.sync
                def _(sync):
                    def xhtr(n):
                        if n >= NCH:
                            return
                        p = n % 2
                        sync.wait_ge(g_xh.h, n + 1)
                        if n >= 2:
                            sync.wait_ge(t_qk.h, n - 1)
                        sync.wait_ge(d_xh.h, d_xh.n)
                        sync.dma_start(
                            out=bass.AP(xh_fm, p * D, [[2 * D, 128], [128, 8], [1, 128]]),
                            in_=bass.AP(xh_tm, p * D, [[2 * D, 128], [128, 8], [1, 128]]),
                            transpose=True).then_inc(d_xh.h, 16)
                        d_xh.n += 16

                    xhtr(0); xhtr(1)
                    for n in range(NCH + 2):
                        p = n % 2
                        q0 = p * 256
                        if n < NCH:
                            sync.wait_ge(v_qk.h, 2 * n + 2)
                            sync.wait_ge(d_k.h, d_k.n)
                            sync.dma_start(
                                out=bass.AP(k_tm, q0, [[512, 128], [128, 2], [1, 128]]),
                                in_=bass.AP(k_fm, q0, [[512, 128], [128, 2], [1, 128]]),
                                transpose=True).then_inc(d_k.h, 16)
                            d_k.n += 16
                            sync.wait_ge(v_k2.h, n + 1)
                            sync.wait_ge(d_k2.h, d_k2.n)
                            sync.dma_start(
                                out=bass.AP(K2_fm, (n % 3) * 256,
                                            [[768, 128], [128, 2], [1, 128]]),
                                in_=bass.AP(K2_tm, q0, [[512, 128], [128, 2], [1, 128]]),
                                transpose=True).then_inc(d_k2.h, 16)
                            d_k2.n += 16
                        if n >= 2:
                            sync.wait_ge(v_pA.h, 2 * (n - 1))
                            sync.wait_ge(d_pA.h, d_pA.n)
                            sync.dma_start(
                                out=bass.AP(pA_fm, 0, [[256, 128], [128, 2], [1, 128]]),
                                in_=bass.AP(pA_tm, 0, [[256, 128], [128, 2], [1, 128]]),
                                transpose=True).then_inc(d_pA.h, 16)
                            d_pA.n += 16
                        if n < NCH:
                            xhtr(n + 2)
                        if n == 0:
                            for nm, srcap, sem, cnt in (
                                    ("tap_sp", sp_tm[:, 0:256], s_spf, 1),
                                    ("tap_A", A_tm[:, 0:256], s_A, 1)):
                                if nm in tap_aps:
                                    sync.wait_ge(sem.h, cnt)
                                    sync.dma_start(out=tap_aps[nm][:], in_=srcap
                                                   ).then_inc(d_pA.h, 16)
                                    d_pA.n += 16
                        if n == 1 and "tap_z" in tap_aps:
                            sync.wait_ge(v_z.h, 1)
                            sync.dma_start(out=tap_aps["tap_z"][:], in_=z_s[:]
                                           ).then_inc(d_pA.h, 16)
                            d_pA.n += 16
                    if "tap_o" in tap_aps:
                        sync.wait_ge(sod.h, 16 * NCH)
                        sync.dma_start(out=tap_aps["tap_o"][:], in_=cc_in[:]
                                       ).then_inc(d_pA.h, 16)
                        d_pA.n += 16

                @blk.gpsimd
                def _(g):
                    def gxload(n):
                        if n >= NCH:
                            return
                        if n >= 4:
                            g.wait_ge(s_sq.h, n - 3)
                            g.wait_ge(g_xh.h, n - 3)
                        g.wait_ge(sx.h, sx.n)
                        g.dma_start(out=x_t[:, (n % 4) * D:(n % 4 + 1) * D],
                                    in_=xb[n * C:(n + 1) * C, :]).then_inc(sx.h, 16)
                        sx.n += 16

                    def gostore(n):
                        if n < 0 or n >= NCH:
                            return
                        p = n % 2
                        g.wait_ge(v_o.h, n + 1)
                        g.wait_ge(sod.h, sod.n)
                        g.dma_start(out=cc_in[n * C:(n + 1) * C, :],
                                    in_=o_sw[:, p * 256:(p + 1) * 256]
                                    ).then_inc(sod.h, 16)
                        sod.n += 16

                    gxload(0); gxload(1); gxload(2)
                    PCE = (E * D * EH) // 64
                    cast_pieces = [(wt, k) for k in range(64)
                                   for wt in range(2)]
                    NQ = NCH // 4
                    for n in range(NCH):
                        gxload(n + 3)
                        gostore(n - 2)
                        lo = len(cast_pieces) * n // NCH
                        hi = len(cast_pieces) * (n + 1) // NCH
                        for wt, k in cast_pieces[lo:hi]:
                            wsrc = (w1T_d, w2T_d)[wt]
                            wdst = (w1b_d, w2b_d)[wt]
                            gc = ((gA0, gA1, gAr), (gB0, gB1, gBr))[wt][
                                0 if k < 8 else (1 if k < 16 else 2)]
                            g.dma_start(
                                out=bass.AP(wdst, k * PCE,
                                            [[PCE // 8, 8], [1, PCE // 8]]),
                                in_=bass.AP(wsrc, k * PCE,
                                            [[PCE // 8, 8], [1, PCE // 8]]),
                            ).then_inc(gc.h, 16)
                            gc.n += 16
                        for qa in range(3):
                            if n == NQ * (qa + 1) + 1:
                                g.wait_ge(sod.h, 16 * NQ * (qa + 1))
                                g.collective_compute(
                                    "AllGather", ALU.bypass,
                                    replica_groups=[[0, 1, 2, 3], [4, 5, 6, 7]],
                                    ins=[cc_in[qa * (T // 4):
                                               (qa + 1) * (T // 4), :].opt()],
                                    outs=[cc_out[qa * T:(qa + 1) * T, :].opt()],
                                ).then_inc(cc_s.h, 1)
                                cc_s.n += 1
                    gostore(NCH - 2)
                    gostore(NCH - 1)
                    g.wait_ge(sod.h, 16 * NCH)
                    g.collective_compute(
                        "AllGather", ALU.bypass,
                        replica_groups=[[0, 1, 2, 3], [4, 5, 6, 7]],
                        ins=[cc_in[3 * (T // 4):T, :].opt()],
                        outs=[cc_out[3 * T:4 * T, :].opt()],
                    ).then_inc(cc_s.h, 1)
                    cc_s.n += 1

                @blk.scalar
                def _(s):
                    def hop(inst):
                        inst.then_inc(s_ch.h, 1)
                        s_ch.n += 1
                        s.wait_ge(s_ch.h, s_ch.n)

                    def swish_sig(src_ap):
                        hop(s.activation(out=swi_scr[:], in_=src_ap, func=AF.Exp,
                                         scale=-1.0))
                        hop(s.activation(out=swi_scr[:], in_=swi_scr[:], func=AF.Ln,
                                         bias=1.0))
                        return s.activation(out=swi_scr[:], in_=swi_scr[:], func=AF.Exp,
                                            scale=-1.0)

                    def rms_xhat(k):
                        if k >= NCH:
                            return
                        pk = k % 2
                        s.wait_ge(sx.h, 16 * (k + 1))
                        aa = s.activation(out=sq_scr[:],
                                          in_=x_t[:, (k % 4) * D:(k % 4 + 1) * D],
                                          func=AF.Square,
                                          accum_out=rms_c[:, (pk * 4):(pk * 4 + 1)])
                        aa.then_inc(s_sq.h, 1); s_sq.n += 1
                        s.wait_ge(s_sq.h, s_sq.n)
                        hop(s.activation(out=rms_c[:, (pk * 4 + 1):(pk * 4 + 2)],
                                         in_=rms_c[:, (pk * 4):(pk * 4 + 1)],
                                         func=AF.Ln, scale=1.0 / D, bias=EPS))
                        aa = s.activation(out=rms_c[:, (pk * 4 + 2):(pk * 4 + 3)],
                                          in_=rms_c[:, (pk * 4 + 1):(pk * 4 + 2)],
                                          func=AF.Exp, scale=-0.5)
                        aa.then_inc(s_rinv.h, 1); s_rinv.n += 1
                        s.wait_ge(s_rinv.h, s_rinv.n)
                        if k >= 2:
                            s.wait_ge(d_xh.h, 16 * (k - 1))
                        aa = s.activation(out=xh_tm[:, pk * D:(pk + 1) * D],
                                          in_=x_t[:, (k % 4) * D:(k % 4 + 1) * D],
                                          func=AF.Identity,
                                          scale=rms_c[:, (pk * 4 + 2):(pk * 4 + 3)])
                        aa.then_inc(g_xh.h, 1); g_xh.n += 1

                    rms_xhat(0); rms_xhat(1)
                    for n in range(NCH + 2):
                        p = n % 2
                        q0 = p * 256
                        if n < NCH:
                            rms_xhat(n + 2)
                        if 1 <= n <= NCH:
                            s.wait_ge(v_hk.h, n)
                            s.wait_ge(t_z.h, n)
                            aa = s.activation(out=Hk_b[:], in_=Hk_f[:], func=AF.Identity)
                            aa.then_inc(s_hkb.h, 1); s_hkb.n += 1
                        if n < NCH:
                            s.wait_ge(t_f.h, n + 1)
                            if n >= 2:
                                s.wait_ge(t_S.h, n - 1)
                                s.wait_ge(s_eg.h, n - 1)
                                s.wait_ge(t_gam.h, n - 1)
                            aa = s.activation(out=sp_tm[:, q0:q0 + 256], in_=B7[:, 0:256],
                                              func=AF.Exp, scale=-1.0)
                            aa.then_inc(s_ef.h, 1); s_ef.n += 1
                            s.wait_ge(s_ef.h, s_ef.n)
                            aa = s.activation(out=sp_tm[:, q0:q0 + 256],
                                              in_=sp_tm[:, q0:q0 + 256],
                                              func=AF.Ln, bias=1.0)
                            aa.then_inc(s_spf.h, 1); s_spf.n += 1
                            s.wait_ge(s_spf.h, s_spf.n)
                            s.wait_ge(t_gam.h, n + 1)
                            if n >= 3:
                                s.wait_ge(v_pA.h, 2 * (n - 2))
                            if n >= 2:
                                s.wait_ge(t_abc.h, n - 1)
                            if n >= 1:
                                s.wait_ge(v_k2.h, n)
                            aa = s.activation(out=A_tm[:, (n % 3) * 256:(n % 3) * 256 + 256],
                                              in_=B2[:, 0:256], func=AF.Exp)
                            aa = s.activation(out=Ainv_tm[:], in_=B2[:, 0:256],
                                              func=AF.Exp, scale=-1.0)
                            aa.then_inc(s_A.h, 1); s_A.n += 1
                            aa = s.activation(out=eg_tm[:], in_=sp_tm[:, q0:q0 + 256],
                                              func=AF.Exp, scale=-1.0 / GATE_NORM)
                            aa.then_inc(s_eg.h, 1); s_eg.n += 1
                            s.wait_ge(t_qk.h, n + 1)
                            aa = swish_sig(B0[:, 0:256])
                            aa.then_inc(s_swqk.h, 1); s_swqk.n += 1
                            s.wait_ge(v_qk.h, 2 * n + 1)
                            aa = swish_sig(B0[:, 256:512])
                            aa.then_inc(s_swqk.h, 1); s_swqk.n += 1
                            s.wait_ge(t_v.h, n + 1)
                            s.wait_ge(v_qk.h, 2 * n + 2)
                            aa = swish_sig(B1[:, 0:256])
                            aa.then_inc(s_swv.h, 1); s_swv.n += 1
                            s.wait_ge(t_S.h, n + 1)
                            s.wait_ge(v_Sm.h, n + 1)
                            if n >= 1:
                                s.wait_ge(t_abc.h, n)
                            aa = s.activation(out=acr_s[0:1, :], in_=B4[0:1, 256:512],
                                              func=AF.Exp)
                            aa.then_inc(s_acr.h, 1); s_acr.n += 1
                            if n >= 2:
                                s.wait_ge(v_hv.h, n - 1)
                            pa2 = (n % 2) * 2
                            aa = s.activation(out=ac_col[:, pa2:pa2 + 1],
                                              in_=B4[:, 128:129], func=AF.Exp)
                            aa = s.activation(out=ac_col[:, pa2 + 1:pa2 + 2],
                                              in_=B4[:, 129:130], func=AF.Exp)
                            aa.then_inc(s_ac.h, 1); s_ac.n += 1
                        if n >= 2:
                            pz = (n - 2) % 2
                            s.wait_ge(v_z.h, n - 1)
                            aa = s.activation(out=ez_s[:], in_=z_s[:, pz * 256:(pz + 1) * 256],
                                              func=AF.Exp,
                                              bias=sm_cols[:, 4 + pz:5 + pz])
                            aa.then_inc(s_ez.h, 1); s_ez.n += 1
                            s.wait_ge(t_o_.h, n - 1)
                            if n >= 3:
                                s.wait_ge(v_o.h, n - 2)
                            hop(s.activation(out=swo_scr[:], in_=B3[:, 0:256],
                                             func=AF.Exp, scale=-1.0))
                            hop(s.activation(out=swo_scr[:], in_=swo_scr[:], func=AF.Ln,
                                             bias=1.0))
                            aa = s.activation(out=swo_scr[:], in_=swo_scr[:],
                                              func=AF.Exp, scale=-1.0)
                            aa.then_inc(s_o.h, 1); s_o.n += 1
                            s.wait_ge(v_hv.h, n - 1)
                            s.wait_ge(t_o_.h, n - 1)
                            aa = s.activation(out=Hv_b[:], in_=Hv_f[:], func=AF.Identity)
                            aa.then_inc(s_hvb.h, 1); s_hvb.n += 1

                @blk.tensor
                def _(t):
                    for n in range(NCH + 2):
                        p = n % 2
                        q0 = p * 256

                        def xh_c(dc):
                            return xh_fm[:, p * D + dc * 128: p * D + (dc + 1) * 128]

                        if n < NCH:
                            t.wait_ge(d_xh.h, 16 * (n + 1))
                            if n >= 1:
                                t.wait_ge(s_ef.h, n)
                                t.wait_ge(v_z.h, n)
                            for dc in range(8):
                                mm = t.matmul(B7[:, 0:256], xh_c(dc),
                                              wf_s[:, dc * 256:(dc + 1) * 256],
                                              start=(dc == 0), stop=(dc == 7))
                            mm.then_inc(t_f.h, 1); t_f.n += 1
                            if n >= 1:
                                t.wait_ge(v_v.h, n)
                            for dc in range(8):
                                mm = t.matmul(B1[:, 0:256], xh_c(dc),
                                              wv_s[:, dc * 256:(dc + 1) * 256],
                                              start=(dc == 0), stop=(dc == 7))
                            mm.then_inc(t_v.h, 1); t_v.n += 1
                            if n >= 1:
                                t.wait_ge(v_qk.h, 2 * n)
                            for j in range(2):
                                for dc in range(8):
                                    mm = t.matmul(
                                        B0[:, j * 128:(j + 1) * 128],
                                        wq_s[:, dc * 256 + j * 128: dc * 256 + (j + 1) * 128],
                                        xh_c(dc), start=(dc == 0), stop=(dc == 7))
                            for j in range(2):
                                for dc in range(8):
                                    mm = t.matmul(
                                        B0[:, 256 + j * 128: 256 + (j + 1) * 128],
                                        wk_s[:, dc * 256 + j * 128: dc * 256 + (j + 1) * 128],
                                        xh_c(dc), start=(dc == 0), stop=(dc == 7))
                            mm.then_inc(t_qk.h, 1); t_qk.n += 1
                            t.wait_ge(s_spf.h, n + 1)
                            if n >= 1:
                                t.wait_ge(s_A.h, n)
                                t.wait_ge(v_hk.h, n)
                            mm = t.matmul(B2[:, 0:256], Lg_s[:], sp_tm[:, q0:q0 + 256],
                                          start=True, stop=True)
                            mm.then_inc(t_gam.h, 1); t_gam.n += 1
                            t.wait_ge(v_qk.h, 2 * n + 2)
                            if n >= 1:
                                t.wait_ge(v_Sm.h, n)
                                t.wait_ge(s_acr.h, n)
                                t.wait_ge(s_ac.h, n)
                            for j in range(2):
                                mm = t.matmul(B4[:, 0:128],
                                              k_fm[:, q0 + j * 128: q0 + (j + 1) * 128],
                                              q_fm[:, q0 + j * 128: q0 + (j + 1) * 128],
                                              start=(j == 0), stop=(j == 1))
                            mm = t.matmul(B4[0:1, 256:512], negsix[:],
                                          sp_tm[:, q0:q0 + 256], start=True, stop=True)
                            mm = t.matmul(B4[:, 128:129], sp_tm[:, q0:q0 + 128],
                                          negsix[:], start=True, stop=True)
                            mm = t.matmul(B4[:, 129:130], sp_tm[:, q0 + 128:q0 + 256],
                                          negsix[:], start=True, stop=True)
                            mm.then_inc(t_S.h, 1); t_S.n += 1
                            t.wait_ge(s_A.h, n + 1)
                            t.wait_ge(s_acr.h, n + 1)
                            if n >= 1:
                                t.wait_ge(v_hk.h, n)
                            mm = t.matmul(B2[:, 256:512], triu_s[0:1, :], acr_s[0:1, :],
                                          start=True, stop=True)
                            mm.then_inc(t_abc.h, 1); t_abc.n += 1
                            t.wait_ge(d_k.h, 16 * (n + 1))
                            t.wait_ge(v_k2.h, n + 1)
                            if n >= 1:
                                t.wait_ge(v_hk.h, n)
                            for j in range(2):
                                mm = t.matmul(B5[:, j * 256:(j + 1) * 256],
                                              k_tm[:, q0 + j * 128: q0 + (j + 1) * 128],
                                              K2_tm[:, q0:q0 + 256], start=True, stop=True)
                            mm.then_inc(t_W.h, 1); t_W.n += 1
                            t.wait_ge(s_ef.h, n + 1)
                            t.wait_ge(s_hkb.h, n)
                            if n >= 1:
                                t.wait_ge(v_z.h, n)
                            t.wait_ge(v_Sm.h, n + 1)
                            for j in range(2):
                                mm = t.matmul(B7[:, 256:512],
                                              q_fm[:, q0 + j * 128: q0 + (j + 1) * 128],
                                              Hk_b[:, j * 256:(j + 1) * 256],
                                              start=(j == 0), stop=False)
                            mm = t.matmul(B7[:, 256:512], Sm_s[:], K2_tm[:, q0:q0 + 256],
                                          start=False, stop=True)
                            mm.then_inc(t_z.h, 1); t_z.n += 1
                        if 1 <= n <= NCH:
                            nu = n - 1
                            qu = (nu % 2) * 256
                            t.wait_ge(v_v.h, nu + 1)
                            t.wait_ge(v_k2.h, nu + 1)
                            if n >= 2:
                                t.wait_ge(v_hv.h, nu)
                            for j in range(2):
                                mm = t.matmul(B6[:, j * 256:(j + 1) * 256],
                                              K2_tm[:, qu + j * 128: qu + (j + 1) * 128],
                                              v_tm[:, (nu % 3) * 256:(nu % 3) * 256 + 256],
                                              start=True, stop=True)
                            mm.then_inc(t_U.h, 1); t_U.n += 1
                        if n >= 2:
                            nm = n - 2
                            qm3 = (nm % 3) * 256
                            t.wait_ge(d_k2.h, 16 * (nm + 1))
                            t.wait_ge(d_pA.h, 16 * (nm + 1))
                            if n >= 3:
                                t.wait_ge(v_o.h, nm)
                            for j in range(2):
                                mm = t.matmul(B3[:, 256:384],
                                              K2_fm[:, qm3 + j * 128: qm3 + (j + 1) * 128],
                                              pA_fm[:, j * 128:(j + 1) * 128],
                                              start=(j == 0), stop=(j == 1))
                            mm.then_inc(t_P2.h, 1); t_P2.n += 1
                            t.wait_ge(s_hvb.h, nm)
                            t.wait_ge(v_P2m.h, nm + 1)
                            t.wait_ge(v_v.h, nm + 1)
                            for j in range(2):
                                mm = t.matmul(B3[:, 0:256],
                                              pA_fm[:, j * 128:(j + 1) * 128],
                                              Hv_b[:, j * 256:(j + 1) * 256],
                                              start=(j == 0), stop=False)
                            mm = t.matmul(B3[:, 0:256], P2m_s[:], v_tm[:, qm3:qm3 + 256],
                                          start=False, stop=True)
                            mm.then_inc(t_o_.h, 1); t_o_.n += 1

                @blk.vector
                def _(v):
                    def vhop(inst):
                        inst.then_inc(v_ch.h, 1)
                        v_ch.n += 1
                        v.wait_ge(v_ch.h, v_ch.n)

                    for n in range(NCH + 2):
                        p = n % 2
                        q0 = p * 256
                        if n < NCH:
                            v.wait_ge(s_swqk.h, 2 * n + 1)
                            if n >= 2:
                                v.wait_ge(d_k.h, 16 * (n - 1))
                                v.wait_ge(t_S.h, n - 1)
                                v.wait_ge(t_z.h, n - 1)
                            vv = v.tensor_tensor(out=q_fm[:, q0:q0 + 256], in0=swi_scr[:],
                                                 in1=B0[:, 0:256], op=ALU.mult)
                            vv.then_inc(v_qk.h, 1); v_qk.n += 1
                            v.wait_ge(s_swqk.h, 2 * n + 2)
                            vv = v.tensor_tensor(out=k_fm[:, q0:q0 + 256], in0=swi_scr[:],
                                                 in1=B0[:, 256:512], op=ALU.mult)
                            vv.then_inc(v_qk.h, 1); v_qk.n += 1
                            v.wait_ge(s_swv.h, n + 1)
                            if n >= 3:
                                v.wait_ge(t_U.h, n - 2)
                                v.wait_ge(t_o_.h, n - 2)
                            vv = v.tensor_tensor(out=v_tm[:, (n % 3) * 256:(n % 3) * 256 + 256],
                                                 in0=swi_scr[:],
                                                 in1=B1[:, 0:256], op=ALU.mult)
                            vv.then_inc(v_v.h, 1); v_v.n += 1
                            v.wait_ge(s_eg.h, n + 1)
                            if n >= 2:
                                v.wait_ge(t_z.h, n - 1)
                                v.wait_ge(t_W.h, n - 1)
                                v.wait_ge(t_U.h, n - 1)
                                v.wait_ge(d_k2.h, 16 * (n - 1))
                            if n >= 3:
                                v.wait_ge(t_P2.h, n - 2)
                            vhop(v.tensor_scalar(out=s_tm[:], in0=eg_tm[:], scalar1=1.0,
                                                 scalar2=-1.0, op0=ALU.subtract,
                                                 op1=ALU.mult))
                            vv = v.tensor_tensor(out=K2_tm[:, q0:q0 + 256], in0=s_tm[:],
                                                 in1=Ainv_tm[:], op=ALU.mult)
                            vv.then_inc(v_k2.h, 1); v_k2.n += 1
                            v.wait_ge(t_S.h, n + 1)
                            if n >= 1:
                                v.wait_ge(t_z.h, n)
                            vv = v.tensor_tensor(out=Sm_s[:], in0=B4[:, 0:128],
                                                 in1=triu_s[:], op=ALU.mult)
                            vv.then_inc(v_Sm.h, 1); v_Sm.n += 1
                            v.wait_ge(t_W.h, n + 1)
                            v.wait_ge(t_abc.h, n + 1)
                            v.wait_ge(s_hkb.h, n)
                            for j in range(2):
                                vhop(v.tensor_tensor(out=Hk_f[:, j * 256:(j + 1) * 256],
                                                     in0=Hk_f[:, j * 256:(j + 1) * 256],
                                                     in1=B5[:, j * 256:(j + 1) * 256],
                                                     op=ALU.add))
                                vv = v.tensor_tensor(out=Hk_f[:, j * 256:(j + 1) * 256],
                                                     in0=Hk_f[:, j * 256:(j + 1) * 256],
                                                     in1=B2[:, 256:512], op=ALU.mult)
                                vhop(vv)
                            vv = v.tensor_copy(out=sm_cols[:, 3:4], in_=sm_cols[:, 3:4])
                            vv.then_inc(v_hk.h, 1); v_hk.n += 1
                            v.wait_ge(t_z.h, n + 1)
                            if n >= 2:
                                v.wait_ge(s_ez.h, n - 1)
                            pz = n % 2
                            vhop(v.tensor_tensor(out=z_s[:, pz * 256:(pz + 1) * 256],
                                                 in0=B7[:, 256:512],
                                                 in1=A_tm[:, (n % 3) * 256:(n % 3) * 256 + 256],
                                                 op=ALU.mult))
                            vv = v.reduce_max(sm_cols[:, 4 + pz:5 + pz],
                                              z_s[:, pz * 256:(pz + 1) * 256], axis=AX.X,
                                              negate=True)
                            vv.then_inc(v_z.h, 1); v_z.n += 1
                        if n >= 2:
                            nm = n - 2
                            qm3 = (nm % 3) * 256
                            v.wait_ge(s_ez.h, nm + 1)
                            vhop(v.reduce_sum(sm_cols[:, 1:2], ez_s[:], axis=AX.X))
                            vv = v.reciprocal(out=sm_cols[:, 2:3], in_=sm_cols[:, 1:2])
                            vv.then_inc(v_pA.h, 1); v_pA.n += 1
                            v.wait_ge(v_pA.h, v_pA.n)
                            if n >= 3:
                                v.wait_ge(d_pA.h, 16 * nm)
                            vv = v.scalar_tensor_tensor(out=pA_tm[:], in0=ez_s[:],
                                                        scalar=sm_cols[:, 2:3],
                                                        in1=A_tm[:, qm3:qm3 + 256],
                                                        op0=ALU.mult, op1=ALU.mult)
                            vv.then_inc(v_pA.h, 1); v_pA.n += 1
                            v.wait_ge(t_P2.h, nm + 1)
                            if n >= 3:
                                v.wait_ge(t_o_.h, nm)
                            vv = v.tensor_tensor(out=P2m_s[:], in0=B3[:, 256:384],
                                                 in1=triu_s[:], op=ALU.mult)
                            vv.then_inc(v_P2m.h, 1); v_P2m.n += 1
                            v.wait_ge(s_o.h, nm + 1)
                            if n >= 4:
                                v.wait_ge(sod.h, 16 * (nm - 1))
                            pm = nm % 2
                            vv = v.tensor_tensor(out=o_sw[:, pm * 256:(pm + 1) * 256],
                                                 in0=swo_scr[:], in1=B3[:, 0:256],
                                                 op=ALU.mult)
                            vv.then_inc(v_o.h, 1); v_o.n += 1
                        if 1 <= n <= NCH:
                            nu = n - 1
                            v.wait_ge(t_U.h, nu + 1)
                            v.wait_ge(s_ac.h, nu + 1)
                            v.wait_ge(s_hvb.h, nu)
                            for j in range(2):
                                vhop(v.tensor_tensor(out=Hv_f[:, j * 256:(j + 1) * 256],
                                                     in0=Hv_f[:, j * 256:(j + 1) * 256],
                                                     in1=B6[:, j * 256:(j + 1) * 256],
                                                     op=ALU.add))
                                vv = v.tensor_scalar_mul(
                                    out=Hv_f[:, j * 256:(j + 1) * 256],
                                    in0=Hv_f[:, j * 256:(j + 1) * 256],
                                    scalar1=ac_col[:, (nu % 2) * 2 + j:(nu % 2) * 2 + j + 1])
                                vhop(vv)
                            vv = v.tensor_copy(out=sm_cols[:, 3:4], in_=sm_cols[:, 3:4])
                            vv.then_inc(v_hv.h, 1); v_hv.n += 1

        if not phase2:
            with nc.Block() as blk:
                @blk.sync
                def _(sync):
                    sync.wait_ge(cc_s.h, 4)
            return nc, tap_aps

    so_r = S("so_r"); sxq = S("sxq"); s2_oh = S("s2_oh"); d2_oh = S("d2_oh")
    t2_at = S("t2_at"); v2_z1 = S("v2_z1"); s2_sq2 = S("s2_sq2"); v2_ln = S("v2_ln")
    s2_rs = S("s2_rs"); v2_nb = S("v2_nb"); s2_x1 = S("s2_x1"); v2_x1b = S("v2_x1b")
    d2_x1 = S("d2_x1"); t2_g = S("t2_g"); v2_sc = S("v2_sc")
    g2w1 = S("g2w1"); g2w2 = S("g2w2"); g2w2b = S("g2w2b"); t2_q = S("t2_q"); t2_m2e = S("t2_m2e")
    s2_h = S("s2_h"); t2_m1 = S("t2_m1"); t2_y = S("t2_y"); v2_y = S("v2_y")
    s2_ot = S("s2_ot"); v2_l2 = S("v2_l2"); sout = S("sout")

    with contextlib.ExitStack() as p2stack:
        def sb2(name, shape, dtype):
            return p2stack.enter_context(nc.sbuf_tensor(name, shape, dtype))

        ot_bf = sb2("ot_bf", [128, 3 * 1024], BF16)
        oh_fm = sb2("oh_fm", [128, 2 * 1024], BF16)
        xr_s = sb2("xr_s", [128, 1024], F32)
        z1_s = sb2("z1_s", [128, 1024], F32)
        ln_c = sb2("ln_c", [128, 16], F32)
        y_acc = sb2("y_acc", [128, NT * 1024], F32)
        x1bf = sb2("x1bf", [128, 1024], BF16)
        x1_fm = sb2("x1_fm", [128, 8 * TPC], BF16)
        scr2 = sb2("scr2", [128, 1024], BF16)
        scores = sb2("scores", [128, 8 * NT], F32)
        comb = sb2("comb", [128, 8 * NT], F32)
        tp_c = sb2("tp_c", [128, 8], F32)
        t2scr = sb2("t2scr", [128, 16], F32)
        w1s = sb2("w1s", [128, 8 * EH], BF16)
        w2s = sb2("w2s", [128, 16 * 1024], BF16)
        h_fm = sb2("h_fm", [128, 16 * TB], BF16)

        PG_cm = nc.psum_tensor("PG", [128, 512], F32); PG = PG_cm.__enter__()
        PA_cm = nc.psum_tensor("PA", [128, 1024], F32); PA = PA_cm.__enter__()

        with nc.Block() as blk:
            @blk.gpsimd
            def _(g):
                with g.register("rq") as rq, g.register("roff") as roff, \
                     g.register("rt") as rt:
                    g.reg_mod(rq, g.partition_id(), 4)
                    g.reg_mul(roff, rq, T * 256)
                    g.wait_ge(gB0.h, gB0.n)
                    g.wait_ge(s_hvb.h, s_hvb.n)
                    g.wait_ge(v_hv.h, v_hv.n)
                    g.dma_start(
                        out=bass.AP(w2s, 8 * 1024,
                                    [[16384, 128], [1024, 8], [1, 1024]]),
                        in_=bass.AP(w2b_d, 8 * 128 * 1024,
                                    [[1024, 128], [128 * 1024, 8], [1, 1024]]),
                    ).then_inc(g2w2b.h, 16)
                    g2w2b.n += 16
                    g.wait_ge(cc_s.h, 4)
                    for i in range(NT):
                        g.reg_mov(rt, i * C * 256)
                        g.reg_add(rt, rt, roff)
                        if i >= 3:
                            g.wait_ge(s2_oh.h, 4 * (i - 2))
                            g.wait_ge(d2_oh.h, 16 * (i - 2))
                        g.wait_ge(so_r.h, so_r.n)
                        g.dma_start(
                            out=bass.AP(ot_bf, (i % 3) * 1024,
                                        [[3072, 128], [256, 4], [1, 256]]),
                            in_=bass.AP(cc_out_d, rt,
                                        [[256, 128], [(T // 4) * 256, 4], [1, 256]]),
                        ).then_inc(so_r.h, 16)
                        so_r.n += 16

            @blk.sync
            def _(sync):
                sync.wait_ge(gA0.h, gA0.n)
                sync.wait_ge(s_hvb.h, s_hvb.n)
                sync.wait_ge(v_hv.h, v_hv.n)
                for qq in range(4):
                    sync.wait_ge(g2w1.h, g2w1.n)
                    sync.dma_start(
                        out=bass.AP(w1s, qq * 4096, [[16384, 128], [512, 8], [1, 512]]),
                        in_=bass.AP(w1b_d, qq * 512,
                                    [[EH, 128], [128 * EH, 8], [1, 512]]),
                    ).then_inc(g2w1.h, 16)
                    g2w1.n += 16
                for i in range(NT):
                    if i >= 1:
                        sync.wait_ge(v2_z1.h, i)
                    sync.wait_ge(sxq.h, sxq.n)
                    sync.dma_start(out=xr_s[:],
                                   in_=xq[i * C:(i + 1) * C, :]).then_inc(sxq.h, 16)
                    sxq.n += 16

            @blk.scalar
            def _(s):
                def ostage(k):
                    if k >= NT:
                        return
                    ko = (k % 3) * 1024
                    s.wait_ge(so_r.h, 16 * (k + 1))
                    aa = s.activation(out=scr2[:], in_=ot_bf[:, ko:ko + 1024],
                                 func=AF.Square,
                                 accum_out=ln_c[:, 0:1]).then_inc(s2_oh.h, 1)
                    s2_oh.n += 1
                    s.wait_ge(s2_oh.h, s2_oh.n)
                    aa = s.activation(out=ln_c[:, 1:2], in_=ln_c[:, 0:1], func=AF.Ln,
                                 scale=1.0 / D, bias=EPS).then_inc(s2_oh.h, 1)
                    s2_oh.n += 1
                    s.wait_ge(s2_oh.h, s2_oh.n)
                    aa = s.activation(out=ln_c[:, 2:3], in_=ln_c[:, 1:2], func=AF.Exp,
                                 scale=-0.5).then_inc(s2_oh.h, 1)
                    s2_oh.n += 1
                    s.wait_ge(s2_oh.h, s2_oh.n)
                    aa = s.activation(out=ot_bf[:, ko:ko + 1024],
                                 in_=ot_bf[:, ko:ko + 1024], func=AF.Identity,
                                 scale=ln_c[:, 2:3]).then_inc(s2_oh.h, 1)
                    s2_oh.n += 1
                    if k >= 2:
                        s.wait_ge(t2_at.h, k - 1)
                    s.wait_ge(s2_oh.h, s2_oh.n)
                    s.wait_ge(d2_oh.h, d2_oh.n)
                    s.dma_start(out=bass.AP(oh_fm, (k % 2) * 1024,
                                            [[2048, 128], [128, 8], [1, 128]]),
                                in_=bass.AP(ot_bf, ko,
                                            [[3072, 128], [128, 8], [1, 128]]),
                                transpose=True).then_inc(d2_oh.h, 16)
                    d2_oh.n += 16

                s.wait_ge(gB0.h, gB0.n)
                s.wait_ge(v_hv.h, v_hv.n)
                s.dma_start(
                    out=bass.AP(w2s, 0, [[16384, 128], [1024, 8], [1, 1024]]),
                    in_=bass.AP(w2b_d, 0, [[1024, 128], [128 * 1024, 8], [1, 1024]]),
                ).then_inc(g2w2.h, 16)
                g2w2.n += 16
                ostage(0)
                for i in range(NT):
                    ostage(i + 1)
                    s.wait_ge(v2_z1.h, i + 1)
                    aa = s.activation(out=scr2[:], in_=z1_s[:], func=AF.Square,
                                 accum_out=ln_c[:, 4:5]).then_inc(s2_sq2.h, 1)
                    s2_sq2.n += 1
                    s.wait_ge(v2_ln.h, i + 1)
                    aa = s.activation(out=ln_c[:, 8:9], in_=ln_c[:, 7:8], func=AF.Ln,
                                 bias=EPS).then_inc(s2_rs.h, 1)
                    s2_rs.n += 1
                    s.wait_ge(s2_rs.h, s2_rs.n)
                    aa = s.activation(out=ln_c[:, 9:10], in_=ln_c[:, 8:9], func=AF.Exp,
                                 scale=-0.5).then_inc(s2_rs.h, 1)
                    s2_rs.n += 1
                    s.wait_ge(s2_rs.h, s2_rs.n)
                    s.wait_ge(v2_nb.h, i + 1)
                    aa = s.activation(out=y_acc[:, i * 1024:(i + 1) * 1024], in_=z1_s[:],
                                 func=AF.Identity, scale=ln_c[:, 9:10],
                                 bias=ln_c[:, 11:12]).then_inc(s2_x1.h, 1)
                    s2_x1.n += 1
                    s.wait_ge(v2_x1b.h, i + 1)
                    s.wait_ge(d2_x1.h, d2_x1.n)
                    s.dma_start(out=bass.AP(x1_fm, i * 128,
                                            [[8 * TPC, 128], [TPC, 8], [1, 128]]),
                                in_=bass.AP(x1bf, 0, [[1024, 128], [128, 8], [1, 128]]),
                                transpose=True).then_inc(d2_x1.h, 16)
                    d2_x1.n += 16

            @blk.tensor
            def _(t):
                for i in range(NT):
                    t.wait_ge(d2_oh.h, 16 * (i + 1))
                    if i >= 1:
                        t.wait_ge(v2_z1.h, i)
                    ko = (i % 2) * 1024
                    for dc in range(8):
                        for half in range(2):
                            mm = t.matmul(PA[:, half * 512:(half + 1) * 512],
                                     oh_fm[:, ko + dc * 128: ko + (dc + 1) * 128],
                                     wo_s[:, dc * 1024 + half * 512:
                                          dc * 1024 + (half + 1) * 512],
                                     start=(dc == 0), stop=(dc == 7))
                    mm.then_inc(t2_at.h, 1); t2_at.n += 1

            @blk.vector
            def _(v):
                def vhop(inst):
                    inst.then_inc(v_ch.h, 1)
                    v_ch.n += 1
                    v.wait_ge(v_ch.h, v_ch.n)

                for i in range(NT):
                    v.wait_ge(t2_at.h, i + 1)
                    v.wait_ge(sxq.h, 16 * (i + 1))
                    if i >= 1:
                        v.wait_ge(s2_x1.h, i)
                    vhop(v.tensor_tensor(out=z1_s[:], in0=PA[:],
                                    in1=xr_s[:], op=ALU.add))
                    vv = v.reduce_sum(ln_c[:, 3:4], z1_s[:], axis=AX.X)
                    vv.then_inc(v2_z1.h, 1); v2_z1.n += 1
                    v.wait_ge(s2_sq2.h, i + 1)
                    vhop(v.tensor_scalar_mul(out=ln_c[:, 5:6], in0=ln_c[:, 3:4],
                                        scalar1=1.0 / D))
                    vv = v.tensor_tensor(out=ln_c[:, 6:7], in0=ln_c[:, 5:6],
                                    in1=ln_c[:, 5:6], op=ALU.mult)
                    vhop(v.tensor_scalar_mul(out=ln_c[:, 7:8], in0=ln_c[:, 4:5],
                                        scalar1=1.0 / D))
                    vv = v.tensor_tensor(out=ln_c[:, 7:8], in0=ln_c[:, 7:8],
                                    in1=ln_c[:, 6:7], op=ALU.subtract)
                    vv.then_inc(v2_ln.h, 1); v2_ln.n += 1
                    v.wait_ge(s2_rs.h, 2 * (i + 1))
                    vhop(v.tensor_tensor(out=ln_c[:, 10:11], in0=ln_c[:, 5:6],
                                    in1=ln_c[:, 9:10], op=ALU.mult))
                    vv = v.tensor_scalar_mul(out=ln_c[:, 11:12], in0=ln_c[:, 10:11],
                                        scalar1=-1.0)
                    vv.then_inc(v2_nb.h, 1); v2_nb.n += 1
                    v.wait_ge(s2_x1.h, i + 1)
                    if i >= 1:
                        v.wait_ge(d2_x1.h, 16 * i)
                    vv = v.tensor_copy(out=x1bf[:], in_=y_acc[:, i * 1024:(i + 1) * 1024])
                    vv.then_inc(v2_x1b.h, 1); v2_x1b.n += 1


        PA_cm.__exit__(None, None, None)
        PH0_cm = nc.psum_tensor("PH0", [128, 512], F32); PH0 = PH0_cm.__enter__()
        PH1_cm = nc.psum_tensor("PH1", [128, 512], F32); PH1 = PH1_cm.__enter__()
        PY0_cm = nc.psum_tensor("PY0", [128, 1024], F32); PY0 = PY0_cm.__enter__()
        PY1_cm = nc.psum_tensor("PY1", [128, 1024], F32); PY1 = PY1_cm.__enter__()

        with nc.Block() as blk:
            @blk.tensor
            def _(t):
                for i in range(NT):
                    t.wait_ge(d2_x1.h, 16 * (i + 1))
                    if i >= 1:
                        t.wait_ge(v2_sc.h, i)
                    for dc in range(8):
                        mm = t.matmul(PG[:, 0:8],
                                 x1_fm[:, dc * TPC + i * 128: dc * TPC + (i + 1) * 128],
                                 gate_s[:, dc * 8:(dc + 1) * 8],
                                 start=(dc == 0), stop=(dc == 7))
                    mm.then_inc(t2_g.h, 1); t2_g.n += 1

            @blk.vector
            def _(v2x):
                for i in range(NT):
                    v2x.wait_ge(t2_g.h, i + 1)
                    vv = v2x.tensor_tensor(out=scores[:, i * 8:(i + 1) * 8],
                                           in0=PG[:, 0:8], in1=bbt_s[:], op=ALU.add)
                    vv.then_inc(v2_sc.h, 1); v2_sc.n += 1

            @blk.scalar
            def _(s):
                for i in range(NT):
                    s.wait_ge(v2_sc.h, i + 1)
                    aa = s.activation(out=scores[:, i * 8:(i + 1) * 8],
                                 in_=scores[:, i * 8:(i + 1) * 8],
                                 func=AF.Sigmoid)
                aa.then_inc(s2_ot.h, 1)
                s2_ot.n += 1

            @blk.vector
            def _(v):
                def vhop(inst):
                    inst.then_inc(v_ch.h, 1)
                    v_ch.n += 1
                    v.wait_ge(v_ch.h, v_ch.n)

                v.wait_ge(s2_ot.h, 1)
                for i in range(NT):
                    sc = scores[:, i * 8:(i + 1) * 8]
                    vv = v.reduce_max(tp_c[:, 0:1], sc, axis=AX.X)
                    vv.then_inc(v2_l2.h, 1); v2_l2.n += 1
                    v.wait_ge(v2_l2.h, v2_l2.n)
                    vhop(v.tensor_single_scalar(out=comb[:, i * 8:(i + 1) * 8],
                                           in_=sc, scalar=tp_c[:, 0:1], op=ALU.is_ge))
                    vhop(v.scalar_tensor_tensor(out=t2scr[:, 0:8], in0=comb[:, i * 8:(i + 1) * 8],
                                           scalar=-2.0, in1=sc, op0=ALU.mult, op1=ALU.add))
                    vv = v.reduce_max(tp_c[:, 1:2], t2scr[:, 0:8], axis=AX.X)
                    vv.then_inc(v2_l2.h, 1); v2_l2.n += 1
                    v.wait_ge(v2_l2.h, v2_l2.n)
                    vhop(v.tensor_single_scalar(out=t2scr[:, 8:16], in_=t2scr[:, 0:8],
                                           scalar=tp_c[:, 1:2], op=ALU.is_ge))
                    vhop(v.tensor_tensor(out=comb[:, i * 8:(i + 1) * 8],
                                    in0=comb[:, i * 8:(i + 1) * 8],
                                    in1=t2scr[:, 8:16], op=ALU.add))
                    vv = v.tensor_scalar(out=tp_c[:, 2:3], in0=tp_c[:, 0:1],
                                    scalar1=tp_c[:, 1:2], scalar2=1e-6,
                                    op0=ALU.add, op1=ALU.add)
                    vv.then_inc(v2_l2.h, 1); v2_l2.n += 1
                    v.wait_ge(v2_l2.h, v2_l2.n)
                    vv = v.reciprocal(out=tp_c[:, 3:4], in_=tp_c[:, 2:3])
                    vv.then_inc(v2_l2.h, 1); v2_l2.n += 1
                    v.wait_ge(v2_l2.h, v2_l2.n)
                    vv = v.scalar_tensor_tensor(out=comb[:, i * 8:(i + 1) * 8], in0=sc,
                                           scalar=tp_c[:, 3:4],
                                           in1=comb[:, i * 8:(i + 1) * 8],
                                           op0=ALU.mult, op1=ALU.mult)
                vv.then_inc(v2_l2.h, 1); v2_l2.n += 1

            @blk.sync
            def _(sync):
                if "tap_x1" in tap_aps:
                    for i in range(NT):
                        sync.dma_start(out=tap_aps["tap_x1"][i * 128:(i + 1) * 128, :],
                                       in_=y_acc[:, i * 1024:(i + 1) * 1024]
                                       ).then_inc(sout.h, 16)
                        sout.n += 16
                if "tap_comb" in tap_aps:
                    sync.wait_ge(v2_l2.h, v2_l2.n)
                    sync.dma_start(out=tap_aps["tap_comb"][:], in_=comb[:]
                                   ).then_inc(sout.h, 16)
                    sout.n += 16

            @blk.sync
            def _(sync):
                for e in range(1, E):
                    ga = (gA1 if e == 1 else gAr)
                    sync.wait_ge(ga.h, ga.n)
                    for qq in range(4):
                        sync.wait_ge(t2_m1.h,
                                     16 * NTB * (e - 1) + 16 * (NTB - 1) + 4 * qq + 4)
                        sync.wait_ge(g2w1.h, g2w1.n)
                        sync.dma_start(
                            out=bass.AP(w1s, qq * 4096,
                                        [[16384, 128], [512, 8], [1, 512]]),
                            in_=bass.AP(w1b_d, e * D * EH + qq * 512,
                                        [[EH, 128], [128 * EH, 8], [1, 512]]),
                        ).then_inc(g2w1.h, 16)
                        g2w1.n += 16

            @blk.tensor
            def _(t):
                for e in range(E):
                    for tb in range(NTB):
                        for ehc in range(16):
                            qq = ehc // 4
                            er = ehc % 4
                            if er == 0:
                                t.wait_ge(g2w1.h, 16 * (4 * e + qq + 1))
                            gsil = 16 * (NTB * e + tb) + ehc
                            if gsil >= 2:
                                t.wait_ge(s2_h.h, gsil - 1)
                            ph = (PH0, PH1)[ehc % 2]
                            for dc in range(8):
                                col = qq * 4096 + dc * 512 + er * 128
                                mm = t.matmul(ph[:, 0:TB], w1s[:, col: col + 128],
                                         x1_fm[:, dc * TPC + tb * TB:
                                               dc * TPC + (tb + 1) * TB],
                                         start=(dc == 0), stop=(dc == 7))
                            mm.then_inc(t2_m1.h, 1); t2_m1.n += 1
                        t.wait_ge(g2w2.h, 16 * (e + 1))
                        t.wait_ge(g2w2b.h, 16 * (e + 1))
                        t.wait_ge(s2_h.h, 16 * (NTB * e + tb + 1))
                        for tt in range(TTB):
                            gt = (NTB * e + tb) * TTB + tt
                            if gt >= 2:
                                t.wait_ge(v2_y.h, gt - 1)
                            PY = (PY0, PY1)[gt % 2]
                            for ehc in range(16):
                                hh = h_fm[:, ehc * TB + tt * 128: ehc * TB + (tt + 1) * 128]
                                mm = t.matmul(PY[:, 0:512], hh,
                                         w2s[:, ehc * 1024: ehc * 1024 + 512],
                                         start=(ehc == 0), stop=(ehc == 15))
                                mm = t.matmul(PY[:, 512:1024], hh,
                                         w2s[:, ehc * 1024 + 512: (ehc + 1) * 1024],
                                         start=(ehc == 0), stop=(ehc == 15))
                            mm.then_inc(t2_y.h, 1); t2_y.n += 1


            @blk.scalar
            def _(s):
                def p2d_scalar(i):
                    if not (0 <= i < NT):
                        return
                    yt = y_acc[:, i * 1024:(i + 1) * 1024]
                    s.wait_ge(v2_y.h, (NTB * (E - 1) + i // 2) * TTB + (i % 2) + 1)
                    aa = s.activation(out=scr2[:], in_=yt, func=AF.Square,
                                 accum_out=ln_c[:, 4:5]).then_inc(s2_sq2.h, 1)
                    s2_sq2.n += 1
                    s.wait_ge(v2_ln.h, v2_ln.n + i + 1)
                    aa = s.activation(out=ln_c[:, 8:9], in_=ln_c[:, 7:8], func=AF.Ln,
                                 bias=EPS).then_inc(s2_rs.h, 1)
                    s2_rs.n += 1
                    s.wait_ge(s2_rs.h, s2_rs.n)
                    aa = s.activation(out=ln_c[:, 9:10], in_=ln_c[:, 8:9], func=AF.Exp,
                                 scale=-0.5).then_inc(s2_rs.h, 1)
                    s2_rs.n += 1
                    s.wait_ge(s2_rs.h, s2_rs.n)
                    s.wait_ge(v2_nb.h, v2_nb.n + i + 1)
                    if i >= 1:
                        s.wait_ge(sout.h, sout.n + 16 * i)
                    aa = s.activation(out=z1_s[:],
                                 in_=yt, func=AF.Identity, scale=ln_c[:, 9:10],
                                 bias=ln_c[:, 11:12]).then_inc(s2_ot.h, 1)
                    s2_ot.n += 1

                for e in range(E):
                    if e >= 1:
                        gb = (gB1 if e == 1 else gBr)
                        s.wait_ge(gb.h, gb.n)
                        s.wait_ge(t2_y.h, NTB * TTB * e)
                        s.dma_start(
                            out=bass.AP(w2s, 0, [[16384, 128], [1024, 8], [1, 1024]]),
                            in_=bass.AP(w2b_d, e * EH * D,
                                        [[1024, 128], [128 * 1024, 8], [1, 1024]]),
                        ).then_inc(g2w2.h, 16)
                        g2w2.n += 16
                    for tb in range(NTB):
                        for ehc in range(16):
                            s.wait_ge(t2_m1.h, 16 * (NTB * e + tb) + ehc + 1)
                            ph = (PH0, PH1)[ehc % 2]
                            aa = s.activation(out=h_fm[:, ehc * TB:(ehc + 1) * TB],
                                         in_=ph[:, 0:TB], func=AF.Silu
                                         ).then_inc(s2_h.h, 1)
                            s2_h.n += 1
                        if e == E - 1:
                            p2d_scalar(2 * (tb - 1))
                            p2d_scalar(2 * (tb - 1) + 1)
                for i in range(2 * (NTB - 1), NT):
                    p2d_scalar(i)

            @blk.gpsimd
            def _(g):
                for e in range(1, E):
                    gb = (gB1 if e == 1 else gBr)
                    g.wait_ge(gb.h, gb.n)
                    g.wait_ge(t2_y.h, NTB * TTB * e)
                    g.dma_start(
                        out=bass.AP(w2s, 8 * 1024,
                                    [[16384, 128], [1024, 8], [1, 1024]]),
                        in_=bass.AP(w2b_d, e * EH * D + 8 * 128 * 1024,
                                    [[1024, 128], [128 * 1024, 8], [1, 1024]]),
                    ).then_inc(g2w2b.h, 16)
                    g2w2b.n += 16

            @blk.vector
            def _(v):
                def vhop(inst):
                    inst.then_inc(v_ch.h, 1)
                    v_ch.n += 1
                    v.wait_ge(v_ch.h, v_ch.n)

                def p2d_vec(i):
                    if not (0 <= i < NT):
                        return
                    yt = y_acc[:, i * 1024:(i + 1) * 1024]
                    v.wait_ge(s2_sq2.h, s2_sq2.n - NT + i + 1)
                    vhop(v.reduce_sum(ln_c[:, 3:4], yt, axis=AX.X))
                    vhop(v.tensor_scalar_mul(out=ln_c[:, 5:6], in0=ln_c[:, 3:4],
                                        scalar1=1.0 / D))
                    vv = v.tensor_tensor(out=ln_c[:, 6:7], in0=ln_c[:, 5:6],
                                    in1=ln_c[:, 5:6], op=ALU.mult)
                    vhop(v.tensor_scalar_mul(out=ln_c[:, 7:8], in0=ln_c[:, 4:5],
                                        scalar1=1.0 / D))
                    vv = v.tensor_tensor(out=ln_c[:, 7:8], in0=ln_c[:, 7:8],
                                    in1=ln_c[:, 6:7], op=ALU.subtract)
                    vv.then_inc(v2_ln.h, 1); v2_ln.n += 1
                    v.wait_ge(s2_rs.h, s2_rs.n - 2 * NT + 2 * (i + 1))
                    vhop(v.tensor_tensor(out=ln_c[:, 10:11], in0=ln_c[:, 5:6],
                                    in1=ln_c[:, 9:10], op=ALU.mult))
                    vv = v.tensor_scalar_mul(out=ln_c[:, 11:12], in0=ln_c[:, 10:11],
                                        scalar1=-1.0)
                    vv.then_inc(v2_nb.h, 1); v2_nb.n += 1

                v.wait_ge(v2_l2.h, v2_l2.n)
                for e in range(E):
                    for tb in range(NTB):
                        for tt in range(TTB):
                            gt = (NTB * e + tb) * TTB + tt
                            tile = tb * TTB + tt
                            v.wait_ge(t2_y.h, gt + 1)
                            PY = (PY0, PY1)[gt % 2]
                            vv = v.scalar_tensor_tensor(
                                out=y_acc[:, tile * 1024:(tile + 1) * 1024],
                                in0=PY[:], scalar=comb[:, tile * 8 + e: tile * 8 + e + 1],
                                in1=y_acc[:, tile * 1024:(tile + 1) * 1024],
                                op0=ALU.mult, op1=ALU.add)
                            vv.then_inc(v2_y.h, 1); v2_y.n += 1
                            if e == E - 1:
                                p2d_vec(tile - 2)
                for i in range(NT - 2, NT):
                    p2d_vec(i)

            @blk.sync
            def _(sync):
                base = s2_ot.n - NT
                for i in range(NT):
                    sync.wait_ge(s2_ot.h, base + i + 1)
                    sync.wait_ge(sout.h, sout.n)
                    sync.dma_start(out=out[i * 128:(i + 1) * 128, :],
                                   in_=z1_s[:]).then_inc(sout.h, 16)
                    sout.n += 16
                sync.wait_ge(sout.h, sout.n)

        PY1_cm.__exit__(None, None, None)
        PY0_cm.__exit__(None, None, None)
        PH1_cm.__exit__(None, None, None)
        PH0_cm.__exit__(None, None, None)
        PG_cm.__exit__(None, None, None)

    return nc, tap_aps


_CACHE = {}


def _get_nc():
    if "nc" not in _CACHE:
        _CACHE["nc"] = build(T=T_FULL, TPC=TPC_FULL)[0]
    return _CACHE["nc"]


def kernel(L0_attn_hidden_states, L0_attn_norm_weight, L0_attn_w_q, L0_attn_w_k,
           L0_attn_w_v, L0_attn_w_f, L0_attn_gnorm_weight, L0_attn_w_o,
           L0_ffn_gate_w, L0_ffn_balance_bias, L0_ffn_w1, L0_ffn_w2):
    x = np.asarray(L0_attn_hidden_states)
    nw = np.asarray(L0_attn_norm_weight)
    gw = np.asarray(L0_attn_gnorm_weight)
    w_q, w_k = np.asarray(L0_attn_w_q), np.asarray(L0_attn_w_k)
    w_v, w_f = np.asarray(L0_attn_w_v), np.asarray(L0_attn_w_f)
    w_o = np.asarray(L0_attn_w_o)
    gate_w = np.asarray(L0_ffn_gate_w)
    bb = np.asarray(L0_ffn_balance_bias)
    w1, w2 = np.asarray(L0_ffn_w1), np.asarray(L0_ffn_w2)

    triu = np.triu(np.ones((128, 128), np.float32))
    Lg = np.ascontiguousarray(triu * np.float32(-1.0 / GATE_NORM))
    w1T = np.ascontiguousarray(w1.transpose(0, 2, 1))
    w2T = np.ascontiguousarray(w2.transpose(0, 2, 1))
    wqT_f = np.ascontiguousarray(w_q.T)
    wkT_f = np.ascontiguousarray(w_k.T)
    wvT_f = np.ascontiguousarray(w_v.T)
    wfT_f = np.ascontiguousarray(w_f.T)
    woT = np.ascontiguousarray(w_o.T)
    nwc = np.ascontiguousarray(nw.reshape(8, 128).T)
    gwc = np.ascontiguousarray(gw.reshape(8, 128).T)
    gateT = np.ascontiguousarray(gate_w.T)
    bbt = np.ascontiguousarray(np.broadcast_to(bb, (128, 8)))

    in_maps = []
    for c in range(8):
        b, h = c // 4, c % 4
        qq = c % 4
        in_maps.append(dict(
            xb=np.ascontiguousarray(x[b]),
            xq=np.ascontiguousarray(x[b, qq * TPC_FULL:(qq + 1) * TPC_FULL]),
            wqT=np.ascontiguousarray(wqT_f[:, 256 * h:256 * (h + 1)]),
            wkT=np.ascontiguousarray(wkT_f[:, 256 * h:256 * (h + 1)]),
            wvT=np.ascontiguousarray(wvT_f[:, 256 * h:256 * (h + 1)]),
            wfT=np.ascontiguousarray(wfT_f[:, 256 * h:256 * (h + 1)]),
            woT=woT, nwc=nwc, gwc=gwc, gateT=gateT, bbt=bbt,
            w1T=w1T, w2T=w2T, triu=triu, Lg=Lg,
        ))

    nc = _get_nc()
    res = run_bass_kernel_spmd(nc, in_maps, core_ids=list(range(8)))
    out = np.concatenate([np.asarray(res.results[c]["out"]) for c in range(8)],
                         axis=0).reshape(2, T_FULL, D)
    return out.astype(np.float32)


# version 16
# speedup vs baseline: 1.0143x; 1.0006x over previous
import contextlib
import numpy as np
import concourse.bass as bass
import concourse.mybir as mybir
from concourse.bass_utils import run_bass_kernel_spmd

F32 = mybir.dt.float32
BF16 = mybir.dt.bfloat16
AF = mybir.ActivationFunctionType
ALU = mybir.AluOpType
AX = mybir.AxisListType

D = 1024
C = 128
E, EH = 8, 2048
GATE_NORM = 16.0
EPS = 1e-5

T_FULL = 8192
TPC_FULL = 2048

class Ctr:
    def __init__(self, nc, name):
        self.h = nc.semaphore(name).__enter__()
        self.n = 0


def build(T=8192, TPC=2048, taps=(), phase2=True, detect_races=False):
    NCH = T // C
    NT = TPC // C
    TB = 256
    NTB = TPC // TB
    TTB = TB // C
    nc = bass.Bass("TRN2", target_bir_lowering=False, detect_race_conditions=detect_races)

    xb = nc.dram_tensor("xb", [T, D], F32, kind="ExternalInput").ap()
    xq = nc.dram_tensor("xq", [TPC, D], F32, kind="ExternalInput").ap()
    wqT_d = nc.dram_tensor("wqT", [D, 256], F32, kind="ExternalInput")
    wkT_d = nc.dram_tensor("wkT", [D, 256], F32, kind="ExternalInput")
    wvT_d = nc.dram_tensor("wvT", [D, 256], F32, kind="ExternalInput")
    wfT_d = nc.dram_tensor("wfT", [D, 256], F32, kind="ExternalInput")
    woT_d = nc.dram_tensor("woT", [D, D], F32, kind="ExternalInput")
    nwc = nc.dram_tensor("nwc", [128, 8], F32, kind="ExternalInput").ap()
    gwc = nc.dram_tensor("gwc", [128, 8], F32, kind="ExternalInput").ap()
    gateT_d = nc.dram_tensor("gateT", [D, 8], F32, kind="ExternalInput")
    bbt_in = nc.dram_tensor("bbt", [128, 8], F32, kind="ExternalInput").ap()
    w1T_d = nc.dram_tensor("w1T", [E, D, EH], F32, kind="ExternalInput")
    w2T_d = nc.dram_tensor("w2T", [E, EH, D], F32, kind="ExternalInput")
    w1b_d = nc.dram_tensor("w1bf", [E * D * EH], BF16, kind="Internal")
    w2b_d = nc.dram_tensor("w2bf", [E * EH * D], BF16, kind="Internal")
    triu_in = nc.dram_tensor("triu", [128, 128], F32, kind="ExternalInput").ap()
    Lg_in = nc.dram_tensor("Lg", [128, 128], F32, kind="ExternalInput").ap()

    out = nc.dram_tensor("out", [TPC, D], F32, kind="ExternalOutput").ap()
    cc_in = nc.dram_tensor("cc_in", [T, 256], BF16, kind="Internal").ap()
    cc_out_d = nc.dram_tensor("cc_out", [4 * T, 256], BF16, kind="Internal")
    cc_out = cc_out_d.ap()

    tap_aps = {}
    def tap(name, shape, dtype=F32):
        if name in taps:
            tap_aps[name] = nc.dram_tensor(name, shape, dtype, kind="ExternalOutput").ap()

    tap("tap_o", [T, 256], BF16)
    tap("tap_x1", [TPC, D])
    tap("tap_comb", [128, 8 * NT])
    tap("tap_sp", [128, 256])
    tap("tap_A", [128, 256])
    tap("tap_z", [128, 256])

    def sb(name, shape, dtype):
        return nc.sbuf_tensor(name, shape, dtype).__enter__()

    wo_s = sb("wo_s", [128, 8 * 1024], BF16)
    gate_s = sb("gate_s", [128, 64], BF16)
    nwc_s = sb("nwc_s", [128, 8], F32)
    gwc_s = sb("gwc_s", [128, 8], F32)
    bbt_s = sb("bbt_s", [128, 8], F32)
    negsix = sb("negsix", [128, 1], F32)
    consts = sb("consts", [128, 2], F32)
    nc.const_aps.register(F32, EPS, consts[:, 0:1])
    nc.const_aps.register(F32, 1.0, consts[:, 1:2])

    S = lambda name: Ctr(nc, name)
    sx = S("sx"); s_sq = S("s_sq"); s_rinv = S("s_rinv"); g_xh = S("g_xh")
    d_xh = S("d_xh"); t_proj = S("t_proj"); t_f = S("t_f"); t_v = S("t_v")
    t_qk = S("t_qk"); s_ef = S("s_ef"); s_spf = S("s_spf"); t_gam = S("t_gam")
    t_gc = S("t_gc"); s_A = S("s_A"); s_eg = S("s_eg"); v_k2 = S("v_k2")
    d_k2 = S("d_k2"); s_swqk = S("s_swqk"); v_qk = S("v_qk"); d_k = S("d_k")
    s_swv = S("s_swv"); v_v = S("v_v"); t_S = S("t_S"); v_Sm = S("v_Sm")
    t_z = S("t_z"); v_z = S("v_z"); s_ez = S("s_ez"); v_pA = S("v_pA")
    d_pA = S("d_pA"); t_P2 = S("t_P2"); v_P2m = S("v_P2m"); t_o_ = S("t_o")
    s_o = S("s_o"); v_o = S("v_o"); sod = S("sod"); t_W = S("t_W")
    t_U = S("t_U"); t_abc = S("t_abc"); s_ac = S("s_ac"); v_hk = S("v_hk")
    v_hv = S("v_hv"); prep = S("prep"); prepg = S("prepg"); cc_s = S("cc_s"); s_acr = S("s_acr"); s_ch = S("s_ch"); v_ch = S("v_ch")
    s_hkb = S("s_hkb"); s_hvb = S("s_hvb"); gA0 = S("gA0"); gA1 = S("gA1"); gAr = S("gAr"); gB0 = S("gB0"); gB1 = S("gB1"); gBr = S("gBr")

    with contextlib.ExitStack() as p1stack:
        def sb1(name, shape, dtype):
            return p1stack.enter_context(nc.sbuf_tensor(name, shape, dtype))

        triu_s = sb1("triu_s", [128, 128], F32)
        Lg_s = sb1("Lg_s", [128, 128], F32)
        wq_s = sb1("wq_s", [128, 8 * 256], BF16)
        wk_s = sb1("wk_s", [128, 8 * 256], BF16)
        wv_s = sb1("wv_s", [128, 8 * 256], BF16)
        wf_s = sb1("wf_s", [128, 8 * 256], BF16)
        x_t = sb1("x_t", [128, 4 * D], F32)
        sq_scr = sb1("sq_scr", [128, D], F32)
        rms_c = sb1("rms_c", [128, 8], F32)
        xh_tm = sb1("xh_tm", [128, 2 * D], BF16)
        xh_fm = sb1("xh_fm", [128, 2 * D], BF16)
        q_fm = sb1("q_fm", [128, 2 * 256], BF16)
        k_fm = sb1("k_fm", [128, 2 * 256], BF16)
        k_tm = sb1("k_tm", [128, 2 * 256], BF16)
        v_tm = sb1("v_tm", [128, 3 * 256], BF16)
        sp_tm = sb1("sp_tm", [128, 2 * 256], F32)
        A_tm = sb1("A_tm", [128, 3 * 256], F32)
        Ainv_tm = sb1("Ainv_tm", [128, 256], F32)
        eg_tm = sb1("eg_tm", [128, 256], F32)
        s_tm = sb1("s_tm", [128, 256], F32)
        K2_tm = sb1("K2_tm", [128, 2 * 256], BF16)
        K2_fm = sb1("K2_fm", [128, 3 * 256], BF16)
        Sm_s = sb1("Sm_s", [128, 128], BF16)
        P2m_s = sb1("P2m_s", [128, 128], BF16)
        z_s = sb1("z_s", [128, 2 * 256], F32)
        ez_s = sb1("ez_s", [128, 256], F32)
        pA_tm = sb1("pA_tm", [128, 256], BF16)
        pA_fm = sb1("pA_fm", [128, 256], BF16)
        sm_cols = sb1("sm_cols", [128, 8], F32)
        ac_col = sb1("ac_col", [128, 4], F32)
        swi_scr = sb1("swi_scr", [128, 256], F32)
        swo_scr = sb1("swo_scr", [128, 256], F32)
        o_sw = sb1("o_sw", [128, 2 * 256], BF16)
        Hk_f = sb1("Hk_f", [128, 512], F32)
        Hv_f = sb1("Hv_f", [128, 512], F32)
        Hk_b = sb1("Hk_b", [128, 512], BF16)
        Hv_b = sb1("Hv_b", [128, 512], BF16)
        acr_s = sb1("acr_s", [1, 256], F32)

        with nc.Block() as blk:
            @blk.sync
            def _(sync):
                for (dst, src) in ((nwc_s, nwc), (gwc_s, gwc), (bbt_s, bbt_in),
                                   (triu_s, triu_in), (Lg_s, Lg_in)):
                    sync.dma_start(out=dst[:], in_=src[:]).then_inc(prep.h, 16)
                    prep.n += 16

            @blk.gpsimd
            def _(g):
                for (dst, src) in ((wq_s, wqT_d), (wk_s, wkT_d), (wv_s, wvT_d),
                                   (wf_s, wfT_d)):
                    g.dma_start(
                        out=bass.AP(dst, 0, [[2048, 128], [256, 8], [1, 256]]),
                        in_=bass.AP(src, 0, [[256, 128], [128 * 256, 8], [1, 256]]),
                    ).then_inc(prepg.h, 16)
                    prepg.n += 16
                g.dma_start(
                    out=bass.AP(wo_s, 0, [[8192, 128], [1024, 8], [1, 1024]]),
                    in_=bass.AP(woT_d, 0, [[1024, 128], [128 * 1024, 8], [1, 1024]]),
                ).then_inc(prepg.h, 16)
                prepg.n += 16
                g.dma_start(
                    out=bass.AP(gate_s, 0, [[64, 128], [8, 8], [1, 8]]),
                    in_=bass.AP(gateT_d, 0, [[8, 128], [128 * 8, 8], [1, 8]]),
                ).then_inc(prepg.h, 16)
                prepg.n += 16

            @blk.vector
            def _(v):
                v.memset(negsix[:], -1.0 / GATE_NORM)
                v.memset(sm_cols[:], 0.0)
                v.memset(consts[:, 0:1], EPS)
                v.memset(consts[:, 1:2], 1.0)
                v.memset(Hk_f[:], 0.0)
                v.memset(Hv_f[:], 0.0)
                v.memset(Hk_b[:], 0.0)
                v.memset(Hv_b[:], 0.0)
                v.wait_ge(prep.h, prep.n)
                v.wait_ge(prepg.h, prepg.n)
                for w in (wq_s, wk_s, wv_s, wf_s):
                    for dc in range(8):
                        vv = v.tensor_scalar_mul(out=w[:, dc * 256:(dc + 1) * 256],
                                            in0=w[:, dc * 256:(dc + 1) * 256],
                                            scalar1=nwc_s[:, dc:dc + 1])
                for dc in range(8):
                    vv = v.tensor_scalar_mul(out=wo_s[:, dc * 1024:(dc + 1) * 1024],
                                        in0=wo_s[:, dc * 1024:(dc + 1) * 1024],
                                        scalar1=gwc_s[:, dc:dc + 1])

        with contextlib.ExitStack() as psstack:
            def ps(name, shape, dtype):
                return psstack.enter_context(nc.psum_tensor(name, shape, dtype))

            B0 = ps("B0", [128, 512], F32)
            B1 = ps("B1", [128, 512], F32)
            B2 = ps("B2", [128, 512], F32)
            B3 = ps("B3", [128, 512], F32)
            B4 = ps("B4", [128, 512], F32)
            B5 = ps("B5", [128, 512], F32)
            B6 = ps("B6", [128, 512], F32)
            B7 = ps("B7", [128, 512], F32)

            with nc.Block() as blk:
                @blk.sync
                def _(sync):
                    def xhtr(n):
                        if n >= NCH:
                            return
                        p = n % 2
                        sync.wait_ge(g_xh.h, n + 1)
                        if n >= 2:
                            sync.wait_ge(t_qk.h, n - 1)
                        sync.wait_ge(d_xh.h, d_xh.n)
                        sync.dma_start(
                            out=bass.AP(xh_fm, p * D, [[2 * D, 128], [128, 8], [1, 128]]),
                            in_=bass.AP(xh_tm, p * D, [[2 * D, 128], [128, 8], [1, 128]]),
                            transpose=True).then_inc(d_xh.h, 16)
                        d_xh.n += 16

                    xhtr(0); xhtr(1)
                    for n in range(NCH + 2):
                        p = n % 2
                        q0 = p * 256
                        if n < NCH:
                            sync.wait_ge(v_qk.h, 2 * n + 2)
                            sync.wait_ge(d_k.h, d_k.n)
                            sync.dma_start(
                                out=bass.AP(k_tm, q0, [[512, 128], [128, 2], [1, 128]]),
                                in_=bass.AP(k_fm, q0, [[512, 128], [128, 2], [1, 128]]),
                                transpose=True).then_inc(d_k.h, 16)
                            d_k.n += 16
                            sync.wait_ge(v_k2.h, n + 1)
                            sync.wait_ge(d_k2.h, d_k2.n)
                            sync.dma_start(
                                out=bass.AP(K2_fm, (n % 3) * 256,
                                            [[768, 128], [128, 2], [1, 128]]),
                                in_=bass.AP(K2_tm, q0, [[512, 128], [128, 2], [1, 128]]),
                                transpose=True).then_inc(d_k2.h, 16)
                            d_k2.n += 16
                        if n >= 2:
                            sync.wait_ge(v_pA.h, 2 * (n - 1))
                            sync.wait_ge(d_pA.h, d_pA.n)
                            sync.dma_start(
                                out=bass.AP(pA_fm, 0, [[256, 128], [128, 2], [1, 128]]),
                                in_=bass.AP(pA_tm, 0, [[256, 128], [128, 2], [1, 128]]),
                                transpose=True).then_inc(d_pA.h, 16)
                            d_pA.n += 16
                        if n < NCH:
                            xhtr(n + 2)
                        if n == 0:
                            for nm, srcap, sem, cnt in (
                                    ("tap_sp", sp_tm[:, 0:256], s_spf, 1),
                                    ("tap_A", A_tm[:, 0:256], s_A, 1)):
                                if nm in tap_aps:
                                    sync.wait_ge(sem.h, cnt)
                                    sync.dma_start(out=tap_aps[nm][:], in_=srcap
                                                   ).then_inc(d_pA.h, 16)
                                    d_pA.n += 16
                        if n == 1 and "tap_z" in tap_aps:
                            sync.wait_ge(v_z.h, 1)
                            sync.dma_start(out=tap_aps["tap_z"][:], in_=z_s[:]
                                           ).then_inc(d_pA.h, 16)
                            d_pA.n += 16
                    if "tap_o" in tap_aps:
                        sync.wait_ge(sod.h, 16 * NCH)
                        sync.dma_start(out=tap_aps["tap_o"][:], in_=cc_in[:]
                                       ).then_inc(d_pA.h, 16)
                        d_pA.n += 16

                @blk.gpsimd
                def _(g):
                    def gxload(n):
                        if n >= NCH:
                            return
                        if n >= 4:
                            g.wait_ge(s_sq.h, n - 3)
                            g.wait_ge(g_xh.h, n - 3)
                        g.wait_ge(sx.h, sx.n)
                        g.dma_start(out=x_t[:, (n % 4) * D:(n % 4 + 1) * D],
                                    in_=xb[n * C:(n + 1) * C, :]).then_inc(sx.h, 16)
                        sx.n += 16

                    def gostore(n):
                        if n < 0 or n >= NCH:
                            return
                        p = n % 2
                        g.wait_ge(v_o.h, n + 1)
                        g.wait_ge(sod.h, sod.n)
                        g.dma_start(out=cc_in[n * C:(n + 1) * C, :],
                                    in_=o_sw[:, p * 256:(p + 1) * 256]
                                    ).then_inc(sod.h, 16)
                        sod.n += 16

                    gxload(0); gxload(1); gxload(2)
                    PCE = (E * D * EH) // 64
                    cast_pieces = [(wt, k) for k in range(64)
                                   for wt in range(2)]
                    NQ = NCH // 4
                    for n in range(NCH):
                        gxload(n + 3)
                        gostore(n - 2)
                        lo = len(cast_pieces) * n // NCH
                        hi = len(cast_pieces) * (n + 1) // NCH
                        for wt, k in cast_pieces[lo:hi]:
                            wsrc = (w1T_d, w2T_d)[wt]
                            wdst = (w1b_d, w2b_d)[wt]
                            gc = ((gA0, gA1, gAr), (gB0, gB1, gBr))[wt][
                                0 if k < 8 else (1 if k < 16 else 2)]
                            g.dma_start(
                                out=bass.AP(wdst, k * PCE,
                                            [[PCE // 8, 8], [1, PCE // 8]]),
                                in_=bass.AP(wsrc, k * PCE,
                                            [[PCE // 8, 8], [1, PCE // 8]]),
                            ).then_inc(gc.h, 16)
                            gc.n += 16
                        for qa in range(3):
                            if n == NQ * (qa + 1) + 1:
                                g.wait_ge(sod.h, 16 * NQ * (qa + 1))
                                g.collective_compute(
                                    "AllGather", ALU.bypass,
                                    replica_groups=[[0, 1, 2, 3], [4, 5, 6, 7]],
                                    ins=[cc_in[qa * (T // 4):
                                               (qa + 1) * (T // 4), :].opt()],
                                    outs=[cc_out[qa * T:(qa + 1) * T, :].opt()],
                                ).then_inc(cc_s.h, 1)
                                cc_s.n += 1
                    gostore(NCH - 2)
                    gostore(NCH - 1)
                    g.wait_ge(sod.h, 16 * NCH)
                    g.collective_compute(
                        "AllGather", ALU.bypass,
                        replica_groups=[[0, 1, 2, 3], [4, 5, 6, 7]],
                        ins=[cc_in[3 * (T // 4):T, :].opt()],
                        outs=[cc_out[3 * T:4 * T, :].opt()],
                    ).then_inc(cc_s.h, 1)
                    cc_s.n += 1

                @blk.scalar
                def _(s):
                    def hop(inst):
                        inst.then_inc(s_ch.h, 1)
                        s_ch.n += 1
                        s.wait_ge(s_ch.h, s_ch.n)

                    def swish_sig(src_ap):
                        hop(s.activation(out=swi_scr[:], in_=src_ap, func=AF.Exp,
                                         scale=-1.0))
                        hop(s.activation(out=swi_scr[:], in_=swi_scr[:], func=AF.Ln,
                                         bias=1.0))
                        return s.activation(out=swi_scr[:], in_=swi_scr[:], func=AF.Exp,
                                            scale=-1.0)

                    def rms_xhat(k):
                        if k >= NCH:
                            return
                        pk = k % 2
                        s.wait_ge(sx.h, 16 * (k + 1))
                        aa = s.activation(out=sq_scr[:],
                                          in_=x_t[:, (k % 4) * D:(k % 4 + 1) * D],
                                          func=AF.Square,
                                          accum_out=rms_c[:, (pk * 4):(pk * 4 + 1)])
                        aa.then_inc(s_sq.h, 1); s_sq.n += 1
                        s.wait_ge(s_sq.h, s_sq.n)
                        hop(s.activation(out=rms_c[:, (pk * 4 + 1):(pk * 4 + 2)],
                                         in_=rms_c[:, (pk * 4):(pk * 4 + 1)],
                                         func=AF.Ln, scale=1.0 / D, bias=EPS))
                        aa = s.activation(out=rms_c[:, (pk * 4 + 2):(pk * 4 + 3)],
                                          in_=rms_c[:, (pk * 4 + 1):(pk * 4 + 2)],
                                          func=AF.Exp, scale=-0.5)
                        aa.then_inc(s_rinv.h, 1); s_rinv.n += 1
                        s.wait_ge(s_rinv.h, s_rinv.n)
                        if k >= 2:
                            s.wait_ge(d_xh.h, 16 * (k - 1))
                        aa = s.activation(out=xh_tm[:, pk * D:(pk + 1) * D],
                                          in_=x_t[:, (k % 4) * D:(k % 4 + 1) * D],
                                          func=AF.Identity,
                                          scale=rms_c[:, (pk * 4 + 2):(pk * 4 + 3)])
                        aa.then_inc(g_xh.h, 1); g_xh.n += 1

                    rms_xhat(0); rms_xhat(1)
                    for n in range(NCH + 2):
                        p = n % 2
                        q0 = p * 256
                        if n < NCH:
                            rms_xhat(n + 2)
                        if 1 <= n <= NCH:
                            s.wait_ge(v_hk.h, n)
                            s.wait_ge(t_z.h, n)
                            aa = s.activation(out=Hk_b[:], in_=Hk_f[:], func=AF.Identity)
                            aa.then_inc(s_hkb.h, 1); s_hkb.n += 1
                        if n < NCH:
                            s.wait_ge(t_f.h, n + 1)
                            if n >= 2:
                                s.wait_ge(t_S.h, n - 1)
                                s.wait_ge(s_eg.h, n - 1)
                                s.wait_ge(t_gam.h, n - 1)
                            aa = s.activation(out=sp_tm[:, q0:q0 + 256], in_=B7[:, 0:256],
                                              func=AF.Exp, scale=-1.0)
                            aa.then_inc(s_ef.h, 1); s_ef.n += 1
                            s.wait_ge(s_ef.h, s_ef.n)
                            aa = s.activation(out=sp_tm[:, q0:q0 + 256],
                                              in_=sp_tm[:, q0:q0 + 256],
                                              func=AF.Ln, bias=1.0)
                            aa.then_inc(s_spf.h, 1); s_spf.n += 1
                            s.wait_ge(s_spf.h, s_spf.n)
                            s.wait_ge(t_gam.h, n + 1)
                            if n >= 3:
                                s.wait_ge(v_pA.h, 2 * (n - 2))
                            if n >= 2:
                                s.wait_ge(t_abc.h, n - 1)
                            if n >= 1:
                                s.wait_ge(v_k2.h, n)
                            aa = s.activation(out=A_tm[:, (n % 3) * 256:(n % 3) * 256 + 256],
                                              in_=B2[:, 0:256], func=AF.Exp)
                            aa = s.activation(out=Ainv_tm[:], in_=B2[:, 0:256],
                                              func=AF.Exp, scale=-1.0)
                            aa.then_inc(s_A.h, 1); s_A.n += 1
                            aa = s.activation(out=eg_tm[:], in_=sp_tm[:, q0:q0 + 256],
                                              func=AF.Exp, scale=-1.0 / GATE_NORM)
                            aa.then_inc(s_eg.h, 1); s_eg.n += 1
                            s.wait_ge(t_qk.h, n + 1)
                            aa = swish_sig(B0[:, 0:256])
                            aa.then_inc(s_swqk.h, 1); s_swqk.n += 1
                            s.wait_ge(v_qk.h, 2 * n + 1)
                            aa = swish_sig(B0[:, 256:512])
                            aa.then_inc(s_swqk.h, 1); s_swqk.n += 1
                            s.wait_ge(t_v.h, n + 1)
                            s.wait_ge(v_qk.h, 2 * n + 2)
                            aa = swish_sig(B1[:, 0:256])
                            aa.then_inc(s_swv.h, 1); s_swv.n += 1
                            s.wait_ge(t_S.h, n + 1)
                            s.wait_ge(v_Sm.h, n + 1)
                            if n >= 1:
                                s.wait_ge(t_abc.h, n)
                            aa = s.activation(out=acr_s[0:1, :], in_=B4[0:1, 256:512],
                                              func=AF.Exp)
                            aa.then_inc(s_acr.h, 1); s_acr.n += 1
                            if n >= 2:
                                s.wait_ge(v_hv.h, n - 1)
                            pa2 = (n % 2) * 2
                            aa = s.activation(out=ac_col[:, pa2:pa2 + 1],
                                              in_=B4[:, 128:129], func=AF.Exp)
                            aa = s.activation(out=ac_col[:, pa2 + 1:pa2 + 2],
                                              in_=B4[:, 129:130], func=AF.Exp)
                            aa.then_inc(s_ac.h, 1); s_ac.n += 1
                        if n >= 2:
                            pz = (n - 2) % 2
                            s.wait_ge(v_z.h, n - 1)
                            aa = s.activation(out=ez_s[:], in_=z_s[:, pz * 256:(pz + 1) * 256],
                                              func=AF.Exp,
                                              bias=sm_cols[:, 4 + pz:5 + pz])
                            aa.then_inc(s_ez.h, 1); s_ez.n += 1
                            s.wait_ge(t_o_.h, n - 1)
                            if n >= 3:
                                s.wait_ge(v_o.h, n - 2)
                            hop(s.activation(out=swo_scr[:], in_=B3[:, 0:256],
                                             func=AF.Exp, scale=-1.0))
                            hop(s.activation(out=swo_scr[:], in_=swo_scr[:], func=AF.Ln,
                                             bias=1.0))
                            aa = s.activation(out=swo_scr[:], in_=swo_scr[:],
                                              func=AF.Exp, scale=-1.0)
                            aa.then_inc(s_o.h, 1); s_o.n += 1
                            s.wait_ge(v_hv.h, n - 1)
                            s.wait_ge(t_o_.h, n - 1)
                            aa = s.activation(out=Hv_b[:], in_=Hv_f[:], func=AF.Identity)
                            aa.then_inc(s_hvb.h, 1); s_hvb.n += 1

                @blk.tensor
                def _(t):
                    for n in range(NCH + 2):
                        p = n % 2
                        q0 = p * 256

                        def xh_c(dc):
                            return xh_fm[:, p * D + dc * 128: p * D + (dc + 1) * 128]

                        if n < NCH:
                            t.wait_ge(d_xh.h, 16 * (n + 1))
                            if n >= 1:
                                t.wait_ge(s_ef.h, n)
                                t.wait_ge(v_z.h, n)
                            for dc in range(8):
                                mm = t.matmul(B7[:, 0:256], xh_c(dc),
                                              wf_s[:, dc * 256:(dc + 1) * 256],
                                              start=(dc == 0), stop=(dc == 7))
                            mm.then_inc(t_f.h, 1); t_f.n += 1
                            if n >= 1:
                                t.wait_ge(v_v.h, n)
                            for dc in range(8):
                                mm = t.matmul(B1[:, 0:256], xh_c(dc),
                                              wv_s[:, dc * 256:(dc + 1) * 256],
                                              start=(dc == 0), stop=(dc == 7))
                            mm.then_inc(t_v.h, 1); t_v.n += 1
                            if n >= 1:
                                t.wait_ge(v_qk.h, 2 * n)
                            for j in range(2):
                                for dc in range(8):
                                    mm = t.matmul(
                                        B0[:, j * 128:(j + 1) * 128],
                                        wq_s[:, dc * 256 + j * 128: dc * 256 + (j + 1) * 128],
                                        xh_c(dc), start=(dc == 0), stop=(dc == 7))
                            for j in range(2):
                                for dc in range(8):
                                    mm = t.matmul(
                                        B0[:, 256 + j * 128: 256 + (j + 1) * 128],
                                        wk_s[:, dc * 256 + j * 128: dc * 256 + (j + 1) * 128],
                                        xh_c(dc), start=(dc == 0), stop=(dc == 7))
                            mm.then_inc(t_qk.h, 1); t_qk.n += 1
                            t.wait_ge(s_spf.h, n + 1)
                            if n >= 1:
                                t.wait_ge(s_A.h, n)
                                t.wait_ge(v_hk.h, n)
                            mm = t.matmul(B2[:, 0:256], Lg_s[:], sp_tm[:, q0:q0 + 256],
                                          start=True, stop=True)
                            mm.then_inc(t_gam.h, 1); t_gam.n += 1
                            t.wait_ge(v_qk.h, 2 * n + 2)
                            if n >= 1:
                                t.wait_ge(v_Sm.h, n)
                                t.wait_ge(s_acr.h, n)
                                t.wait_ge(s_ac.h, n)
                            for j in range(2):
                                mm = t.matmul(B4[:, 0:128],
                                              k_fm[:, q0 + j * 128: q0 + (j + 1) * 128],
                                              q_fm[:, q0 + j * 128: q0 + (j + 1) * 128],
                                              start=(j == 0), stop=(j == 1))
                            mm = t.matmul(B4[0:1, 256:512], negsix[:],
                                          sp_tm[:, q0:q0 + 256], start=True, stop=True)
                            mm = t.matmul(B4[:, 128:129], sp_tm[:, q0:q0 + 128],
                                          negsix[:], start=True, stop=True)
                            mm = t.matmul(B4[:, 129:130], sp_tm[:, q0 + 128:q0 + 256],
                                          negsix[:], start=True, stop=True)
                            mm.then_inc(t_S.h, 1); t_S.n += 1
                            t.wait_ge(s_A.h, n + 1)
                            t.wait_ge(s_acr.h, n + 1)
                            if n >= 1:
                                t.wait_ge(v_hk.h, n)
                            mm = t.matmul(B2[:, 256:512], triu_s[0:1, :], acr_s[0:1, :],
                                          start=True, stop=True)
                            mm.then_inc(t_abc.h, 1); t_abc.n += 1
                            t.wait_ge(d_k.h, 16 * (n + 1))
                            t.wait_ge(v_k2.h, n + 1)
                            if n >= 1:
                                t.wait_ge(v_hk.h, n)
                            for j in range(2):
                                mm = t.matmul(B5[:, j * 256:(j + 1) * 256],
                                              k_tm[:, q0 + j * 128: q0 + (j + 1) * 128],
                                              K2_tm[:, q0:q0 + 256], start=True, stop=True)
                            mm.then_inc(t_W.h, 1); t_W.n += 1
                            t.wait_ge(s_ef.h, n + 1)
                            t.wait_ge(s_hkb.h, n)
                            if n >= 1:
                                t.wait_ge(v_z.h, n)
                            t.wait_ge(v_Sm.h, n + 1)
                            for j in range(2):
                                mm = t.matmul(B7[:, 256:512],
                                              q_fm[:, q0 + j * 128: q0 + (j + 1) * 128],
                                              Hk_b[:, j * 256:(j + 1) * 256],
                                              start=(j == 0), stop=False)
                            mm = t.matmul(B7[:, 256:512], Sm_s[:], K2_tm[:, q0:q0 + 256],
                                          start=False, stop=True)
                            mm.then_inc(t_z.h, 1); t_z.n += 1
                        if 1 <= n <= NCH:
                            nu = n - 1
                            qu = (nu % 2) * 256
                            t.wait_ge(v_v.h, nu + 1)
                            t.wait_ge(v_k2.h, nu + 1)
                            if n >= 2:
                                t.wait_ge(v_hv.h, nu)
                            for j in range(2):
                                mm = t.matmul(B6[:, j * 256:(j + 1) * 256],
                                              K2_tm[:, qu + j * 128: qu + (j + 1) * 128],
                                              v_tm[:, (nu % 3) * 256:(nu % 3) * 256 + 256],
                                              start=True, stop=True)
                            mm.then_inc(t_U.h, 1); t_U.n += 1
                        if n >= 2:
                            nm = n - 2
                            qm3 = (nm % 3) * 256
                            t.wait_ge(d_k2.h, 16 * (nm + 1))
                            t.wait_ge(d_pA.h, 16 * (nm + 1))
                            if n >= 3:
                                t.wait_ge(v_o.h, nm)
                            for j in range(2):
                                mm = t.matmul(B3[:, 256:384],
                                              K2_fm[:, qm3 + j * 128: qm3 + (j + 1) * 128],
                                              pA_fm[:, j * 128:(j + 1) * 128],
                                              start=(j == 0), stop=(j == 1))
                            mm.then_inc(t_P2.h, 1); t_P2.n += 1
                            t.wait_ge(s_hvb.h, nm)
                            t.wait_ge(v_P2m.h, nm + 1)
                            t.wait_ge(v_v.h, nm + 1)
                            for j in range(2):
                                mm = t.matmul(B3[:, 0:256],
                                              pA_fm[:, j * 128:(j + 1) * 128],
                                              Hv_b[:, j * 256:(j + 1) * 256],
                                              start=(j == 0), stop=False)
                            mm = t.matmul(B3[:, 0:256], P2m_s[:], v_tm[:, qm3:qm3 + 256],
                                          start=False, stop=True)
                            mm.then_inc(t_o_.h, 1); t_o_.n += 1

                @blk.vector
                def _(v):
                    def vhop(inst):
                        inst.then_inc(v_ch.h, 1)
                        v_ch.n += 1
                        v.wait_ge(v_ch.h, v_ch.n)

                    for n in range(NCH + 2):
                        p = n % 2
                        q0 = p * 256
                        if n < NCH:
                            v.wait_ge(s_swqk.h, 2 * n + 1)
                            if n >= 2:
                                v.wait_ge(d_k.h, 16 * (n - 1))
                                v.wait_ge(t_S.h, n - 1)
                                v.wait_ge(t_z.h, n - 1)
                            vv = v.tensor_tensor(out=q_fm[:, q0:q0 + 256], in0=swi_scr[:],
                                                 in1=B0[:, 0:256], op=ALU.mult)
                            vv.then_inc(v_qk.h, 1); v_qk.n += 1
                            v.wait_ge(s_swqk.h, 2 * n + 2)
                            vv = v.tensor_tensor(out=k_fm[:, q0:q0 + 256], in0=swi_scr[:],
                                                 in1=B0[:, 256:512], op=ALU.mult)
                            vv.then_inc(v_qk.h, 1); v_qk.n += 1
                            v.wait_ge(s_swv.h, n + 1)
                            if n >= 3:
                                v.wait_ge(t_U.h, n - 2)
                                v.wait_ge(t_o_.h, n - 2)
                            vv = v.tensor_tensor(out=v_tm[:, (n % 3) * 256:(n % 3) * 256 + 256],
                                                 in0=swi_scr[:],
                                                 in1=B1[:, 0:256], op=ALU.mult)
                            vv.then_inc(v_v.h, 1); v_v.n += 1
                            v.wait_ge(s_eg.h, n + 1)
                            if n >= 2:
                                v.wait_ge(t_z.h, n - 1)
                                v.wait_ge(t_W.h, n - 1)
                                v.wait_ge(t_U.h, n - 1)
                                v.wait_ge(d_k2.h, 16 * (n - 1))
                            if n >= 3:
                                v.wait_ge(t_P2.h, n - 2)
                            vhop(v.tensor_scalar(out=s_tm[:], in0=eg_tm[:], scalar1=1.0,
                                                 scalar2=-1.0, op0=ALU.subtract,
                                                 op1=ALU.mult))
                            vv = v.tensor_tensor(out=K2_tm[:, q0:q0 + 256], in0=s_tm[:],
                                                 in1=Ainv_tm[:], op=ALU.mult)
                            vv.then_inc(v_k2.h, 1); v_k2.n += 1
                            v.wait_ge(t_S.h, n + 1)
                            if n >= 1:
                                v.wait_ge(t_z.h, n)
                            vv = v.tensor_tensor(out=Sm_s[:], in0=B4[:, 0:128],
                                                 in1=triu_s[:], op=ALU.mult)
                            vv.then_inc(v_Sm.h, 1); v_Sm.n += 1
                            v.wait_ge(t_W.h, n + 1)
                            v.wait_ge(t_abc.h, n + 1)
                            v.wait_ge(s_hkb.h, n)
                            for j in range(2):
                                vhop(v.tensor_tensor(out=Hk_f[:, j * 256:(j + 1) * 256],
                                                     in0=Hk_f[:, j * 256:(j + 1) * 256],
                                                     in1=B5[:, j * 256:(j + 1) * 256],
                                                     op=ALU.add))
                                vv = v.tensor_tensor(out=Hk_f[:, j * 256:(j + 1) * 256],
                                                     in0=Hk_f[:, j * 256:(j + 1) * 256],
                                                     in1=B2[:, 256:512], op=ALU.mult)
                                vhop(vv)
                            vv = v.tensor_copy(out=sm_cols[:, 3:4], in_=sm_cols[:, 3:4])
                            vv.then_inc(v_hk.h, 1); v_hk.n += 1
                            v.wait_ge(t_z.h, n + 1)
                            if n >= 2:
                                v.wait_ge(s_ez.h, n - 1)
                            pz = n % 2
                            vhop(v.tensor_tensor(out=z_s[:, pz * 256:(pz + 1) * 256],
                                                 in0=B7[:, 256:512],
                                                 in1=A_tm[:, (n % 3) * 256:(n % 3) * 256 + 256],
                                                 op=ALU.mult))
                            vv = v.reduce_max(sm_cols[:, 4 + pz:5 + pz],
                                              z_s[:, pz * 256:(pz + 1) * 256], axis=AX.X,
                                              negate=True)
                            vv.then_inc(v_z.h, 1); v_z.n += 1
                        if n >= 2:
                            nm = n - 2
                            qm3 = (nm % 3) * 256
                            v.wait_ge(s_ez.h, nm + 1)
                            vhop(v.reduce_sum(sm_cols[:, 1:2], ez_s[:], axis=AX.X))
                            vv = v.reciprocal(out=sm_cols[:, 2:3], in_=sm_cols[:, 1:2])
                            vv.then_inc(v_pA.h, 1); v_pA.n += 1
                            v.wait_ge(v_pA.h, v_pA.n)
                            if n >= 3:
                                v.wait_ge(d_pA.h, 16 * nm)
                            vv = v.scalar_tensor_tensor(out=pA_tm[:], in0=ez_s[:],
                                                        scalar=sm_cols[:, 2:3],
                                                        in1=A_tm[:, qm3:qm3 + 256],
                                                        op0=ALU.mult, op1=ALU.mult)
                            vv.then_inc(v_pA.h, 1); v_pA.n += 1
                            v.wait_ge(t_P2.h, nm + 1)
                            if n >= 3:
                                v.wait_ge(t_o_.h, nm)
                            vv = v.tensor_tensor(out=P2m_s[:], in0=B3[:, 256:384],
                                                 in1=triu_s[:], op=ALU.mult)
                            vv.then_inc(v_P2m.h, 1); v_P2m.n += 1
                            v.wait_ge(s_o.h, nm + 1)
                            if n >= 4:
                                v.wait_ge(sod.h, 16 * (nm - 1))
                            pm = nm % 2
                            vv = v.tensor_tensor(out=o_sw[:, pm * 256:(pm + 1) * 256],
                                                 in0=swo_scr[:], in1=B3[:, 0:256],
                                                 op=ALU.mult)
                            vv.then_inc(v_o.h, 1); v_o.n += 1
                        if 1 <= n <= NCH:
                            nu = n - 1
                            v.wait_ge(t_U.h, nu + 1)
                            v.wait_ge(s_ac.h, nu + 1)
                            v.wait_ge(s_hvb.h, nu)
                            for j in range(2):
                                vhop(v.tensor_tensor(out=Hv_f[:, j * 256:(j + 1) * 256],
                                                     in0=Hv_f[:, j * 256:(j + 1) * 256],
                                                     in1=B6[:, j * 256:(j + 1) * 256],
                                                     op=ALU.add))
                                vv = v.tensor_scalar_mul(
                                    out=Hv_f[:, j * 256:(j + 1) * 256],
                                    in0=Hv_f[:, j * 256:(j + 1) * 256],
                                    scalar1=ac_col[:, (nu % 2) * 2 + j:(nu % 2) * 2 + j + 1])
                                vhop(vv)
                            vv = v.tensor_copy(out=sm_cols[:, 3:4], in_=sm_cols[:, 3:4])
                            vv.then_inc(v_hv.h, 1); v_hv.n += 1

        if not phase2:
            with nc.Block() as blk:
                @blk.sync
                def _(sync):
                    sync.wait_ge(cc_s.h, 4)
            return nc, tap_aps

    so_r = S("so_r"); sxq = S("sxq"); s2_oh = S("s2_oh"); d2_oh = S("d2_oh")
    t2_at = S("t2_at"); v2_z1 = S("v2_z1"); s2_sq2 = S("s2_sq2"); v2_ln = S("v2_ln")
    s2_rs = S("s2_rs"); v2_nb = S("v2_nb"); s2_x1 = S("s2_x1"); v2_x1b = S("v2_x1b")
    d2_x1 = S("d2_x1"); t2_g = S("t2_g"); v2_sc = S("v2_sc")
    g2w1 = S("g2w1"); g2w2 = S("g2w2"); g2w2b = S("g2w2b"); t2_q = S("t2_q"); t2_m2e = S("t2_m2e")
    s2_h = S("s2_h"); t2_m1 = S("t2_m1"); t2_y = S("t2_y"); v2_y = S("v2_y")
    s2_ot = S("s2_ot"); v2_l2 = S("v2_l2"); sout = S("sout")

    with contextlib.ExitStack() as p2stack:
        def sb2(name, shape, dtype):
            return p2stack.enter_context(nc.sbuf_tensor(name, shape, dtype))

        ot_bf = sb2("ot_bf", [128, 3 * 1024], BF16)
        oh_fm = sb2("oh_fm", [128, 2 * 1024], BF16)
        xr_s = sb2("xr_s", [128, 1024], F32)
        z1_s = sb2("z1_s", [128, 1024], F32)
        ln_c = sb2("ln_c", [128, 16], F32)
        y_acc = sb2("y_acc", [128, NT * 1024], F32)
        x1bf = sb2("x1bf", [128, 1024], BF16)
        x1_fm = sb2("x1_fm", [128, 8 * TPC], BF16)
        scr2 = sb2("scr2", [128, 1024], BF16)
        scores = sb2("scores", [128, 8 * NT], F32)
        comb = sb2("comb", [128, 8 * NT], F32)
        tp_c = sb2("tp_c", [128, 8], F32)
        t2scr = sb2("t2scr", [128, 16], F32)
        w1s = sb2("w1s", [128, 8 * EH], BF16)
        w2s = sb2("w2s", [128, 16 * 1024], BF16)
        h_fm = sb2("h_fm", [128, 16 * TB], BF16)

        PG_cm = nc.psum_tensor("PG", [128, 512], F32); PG = PG_cm.__enter__()
        PA_cm = nc.psum_tensor("PA", [128, 1024], F32); PA = PA_cm.__enter__()

        with nc.Block() as blk:
            @blk.gpsimd
            def _(g):
                with g.register("rq") as rq, g.register("roff") as roff, \
                     g.register("rt") as rt:
                    g.reg_mod(rq, g.partition_id(), 4)
                    g.reg_mul(roff, rq, T * 256)
                    g.wait_ge(gB0.h, gB0.n)
                    g.wait_ge(s_hvb.h, s_hvb.n)
                    g.wait_ge(v_hv.h, v_hv.n)
                    g.dma_start(
                        out=bass.AP(w2s, 8 * 1024,
                                    [[16384, 128], [1024, 8], [1, 1024]]),
                        in_=bass.AP(w2b_d, 8 * 128 * 1024,
                                    [[1024, 128], [128 * 1024, 8], [1, 1024]]),
                    ).then_inc(g2w2b.h, 16)
                    g2w2b.n += 16
                    g.wait_ge(cc_s.h, 4)
                    for i in range(NT):
                        g.reg_mov(rt, i * C * 256)
                        g.reg_add(rt, rt, roff)
                        if i >= 3:
                            g.wait_ge(s2_oh.h, 4 * (i - 2))
                            g.wait_ge(d2_oh.h, 16 * (i - 2))
                        g.wait_ge(so_r.h, so_r.n)
                        g.dma_start(
                            out=bass.AP(ot_bf, (i % 3) * 1024,
                                        [[3072, 128], [256, 4], [1, 256]]),
                            in_=bass.AP(cc_out_d, rt,
                                        [[256, 128], [(T // 4) * 256, 4], [1, 256]]),
                        ).then_inc(so_r.h, 16)
                        so_r.n += 16

            @blk.sync
            def _(sync):
                sync.wait_ge(gA0.h, gA0.n)
                sync.wait_ge(s_hvb.h, s_hvb.n)
                sync.wait_ge(v_hv.h, v_hv.n)
                for qq in range(4):
                    sync.wait_ge(g2w1.h, g2w1.n)
                    sync.dma_start(
                        out=bass.AP(w1s, qq * 4096, [[16384, 128], [512, 8], [1, 512]]),
                        in_=bass.AP(w1b_d, qq * 512,
                                    [[EH, 128], [128 * EH, 8], [1, 512]]),
                    ).then_inc(g2w1.h, 16)
                    g2w1.n += 16
                for i in range(NT):
                    if i >= 1:
                        sync.wait_ge(v2_z1.h, i)
                    sync.wait_ge(sxq.h, sxq.n)
                    sync.dma_start(out=xr_s[:],
                                   in_=xq[i * C:(i + 1) * C, :]).then_inc(sxq.h, 16)
                    sxq.n += 16

            @blk.scalar
            def _(s):
                def ostage(k):
                    if k >= NT:
                        return
                    ko = (k % 3) * 1024
                    s.wait_ge(so_r.h, 16 * (k + 1))
                    aa = s.activation(out=scr2[:], in_=ot_bf[:, ko:ko + 1024],
                                 func=AF.Square,
                                 accum_out=ln_c[:, 0:1]).then_inc(s2_oh.h, 1)
                    s2_oh.n += 1
                    s.wait_ge(s2_oh.h, s2_oh.n)
                    aa = s.activation(out=ln_c[:, 1:2], in_=ln_c[:, 0:1], func=AF.Ln,
                                 scale=1.0 / D, bias=EPS).then_inc(s2_oh.h, 1)
                    s2_oh.n += 1
                    s.wait_ge(s2_oh.h, s2_oh.n)
                    aa = s.activation(out=ln_c[:, 2:3], in_=ln_c[:, 1:2], func=AF.Exp,
                                 scale=-0.5).then_inc(s2_oh.h, 1)
                    s2_oh.n += 1
                    s.wait_ge(s2_oh.h, s2_oh.n)
                    aa = s.activation(out=ot_bf[:, ko:ko + 1024],
                                 in_=ot_bf[:, ko:ko + 1024], func=AF.Identity,
                                 scale=ln_c[:, 2:3]).then_inc(s2_oh.h, 1)
                    s2_oh.n += 1
                    if k >= 2:
                        s.wait_ge(t2_at.h, k - 1)
                    s.wait_ge(s2_oh.h, s2_oh.n)
                    s.wait_ge(d2_oh.h, d2_oh.n)
                    s.dma_start(out=bass.AP(oh_fm, (k % 2) * 1024,
                                            [[2048, 128], [128, 8], [1, 128]]),
                                in_=bass.AP(ot_bf, ko,
                                            [[3072, 128], [128, 8], [1, 128]]),
                                transpose=True).then_inc(d2_oh.h, 16)
                    d2_oh.n += 16

                s.wait_ge(gB0.h, gB0.n)
                s.wait_ge(v_hv.h, v_hv.n)
                s.dma_start(
                    out=bass.AP(w2s, 0, [[16384, 128], [1024, 8], [1, 1024]]),
                    in_=bass.AP(w2b_d, 0, [[1024, 128], [128 * 1024, 8], [1, 1024]]),
                ).then_inc(g2w2.h, 16)
                g2w2.n += 16
                ostage(0)
                ostage(1)
                for i in range(NT):
                    ostage(i + 2)
                    s.wait_ge(v2_z1.h, i + 1)
                    aa = s.activation(out=scr2[:], in_=z1_s[:], func=AF.Square,
                                 accum_out=ln_c[:, 4:5]).then_inc(s2_sq2.h, 1)
                    s2_sq2.n += 1
                    s.wait_ge(v2_ln.h, i + 1)
                    aa = s.activation(out=ln_c[:, 8:9], in_=ln_c[:, 7:8], func=AF.Ln,
                                 bias=EPS).then_inc(s2_rs.h, 1)
                    s2_rs.n += 1
                    s.wait_ge(s2_rs.h, s2_rs.n)
                    aa = s.activation(out=ln_c[:, 9:10], in_=ln_c[:, 8:9], func=AF.Exp,
                                 scale=-0.5).then_inc(s2_rs.h, 1)
                    s2_rs.n += 1
                    s.wait_ge(s2_rs.h, s2_rs.n)
                    s.wait_ge(v2_nb.h, i + 1)
                    aa = s.activation(out=y_acc[:, i * 1024:(i + 1) * 1024], in_=z1_s[:],
                                 func=AF.Identity, scale=ln_c[:, 9:10],
                                 bias=ln_c[:, 11:12]).then_inc(s2_x1.h, 1)
                    s2_x1.n += 1
                    s.wait_ge(v2_x1b.h, i + 1)
                    s.wait_ge(d2_x1.h, d2_x1.n)
                    s.dma_start(out=bass.AP(x1_fm, i * 128,
                                            [[8 * TPC, 128], [TPC, 8], [1, 128]]),
                                in_=bass.AP(x1bf, 0, [[1024, 128], [128, 8], [1, 128]]),
                                transpose=True).then_inc(d2_x1.h, 16)
                    d2_x1.n += 16

            @blk.tensor
            def _(t):
                for i in range(NT):
                    t.wait_ge(d2_oh.h, 16 * (i + 1))
                    if i >= 1:
                        t.wait_ge(v2_z1.h, i)
                    ko = (i % 2) * 1024
                    for dc in range(8):
                        for half in range(2):
                            mm = t.matmul(PA[:, half * 512:(half + 1) * 512],
                                     oh_fm[:, ko + dc * 128: ko + (dc + 1) * 128],
                                     wo_s[:, dc * 1024 + half * 512:
                                          dc * 1024 + (half + 1) * 512],
                                     start=(dc == 0), stop=(dc == 7))
                    mm.then_inc(t2_at.h, 1); t2_at.n += 1

            @blk.vector
            def _(v):
                def vhop(inst):
                    inst.then_inc(v_ch.h, 1)
                    v_ch.n += 1
                    v.wait_ge(v_ch.h, v_ch.n)

                for i in range(NT):
                    v.wait_ge(t2_at.h, i + 1)
                    v.wait_ge(sxq.h, 16 * (i + 1))
                    if i >= 1:
                        v.wait_ge(s2_x1.h, i)
                    vhop(v.tensor_tensor(out=z1_s[:], in0=PA[:],
                                    in1=xr_s[:], op=ALU.add))
                    vv = v.reduce_sum(ln_c[:, 3:4], z1_s[:], axis=AX.X)
                    vv.then_inc(v2_z1.h, 1); v2_z1.n += 1
                    v.wait_ge(s2_sq2.h, i + 1)
                    vhop(v.tensor_scalar_mul(out=ln_c[:, 5:6], in0=ln_c[:, 3:4],
                                        scalar1=1.0 / D))
                    vv = v.tensor_tensor(out=ln_c[:, 6:7], in0=ln_c[:, 5:6],
                                    in1=ln_c[:, 5:6], op=ALU.mult)
                    vhop(v.tensor_scalar_mul(out=ln_c[:, 7:8], in0=ln_c[:, 4:5],
                                        scalar1=1.0 / D))
                    vv = v.tensor_tensor(out=ln_c[:, 7:8], in0=ln_c[:, 7:8],
                                    in1=ln_c[:, 6:7], op=ALU.subtract)
                    vv.then_inc(v2_ln.h, 1); v2_ln.n += 1
                    v.wait_ge(s2_rs.h, 2 * (i + 1))
                    vhop(v.tensor_tensor(out=ln_c[:, 10:11], in0=ln_c[:, 5:6],
                                    in1=ln_c[:, 9:10], op=ALU.mult))
                    vv = v.tensor_scalar_mul(out=ln_c[:, 11:12], in0=ln_c[:, 10:11],
                                        scalar1=-1.0)
                    vv.then_inc(v2_nb.h, 1); v2_nb.n += 1
                    v.wait_ge(s2_x1.h, i + 1)
                    if i >= 1:
                        v.wait_ge(d2_x1.h, 16 * i)
                    vv = v.tensor_copy(out=x1bf[:], in_=y_acc[:, i * 1024:(i + 1) * 1024])
                    vv.then_inc(v2_x1b.h, 1); v2_x1b.n += 1


        PA_cm.__exit__(None, None, None)
        PH0_cm = nc.psum_tensor("PH0", [128, 512], F32); PH0 = PH0_cm.__enter__()
        PH1_cm = nc.psum_tensor("PH1", [128, 512], F32); PH1 = PH1_cm.__enter__()
        PY0_cm = nc.psum_tensor("PY0", [128, 1024], F32); PY0 = PY0_cm.__enter__()
        PY1_cm = nc.psum_tensor("PY1", [128, 1024], F32); PY1 = PY1_cm.__enter__()

        with nc.Block() as blk:
            @blk.tensor
            def _(t):
                for i in range(NT):
                    t.wait_ge(d2_x1.h, 16 * (i + 1))
                    if i >= 1:
                        t.wait_ge(v2_sc.h, i)
                    for dc in range(8):
                        mm = t.matmul(PG[:, 0:8],
                                 x1_fm[:, dc * TPC + i * 128: dc * TPC + (i + 1) * 128],
                                 gate_s[:, dc * 8:(dc + 1) * 8],
                                 start=(dc == 0), stop=(dc == 7))
                    mm.then_inc(t2_g.h, 1); t2_g.n += 1

            @blk.vector
            def _(v2x):
                for i in range(NT):
                    v2x.wait_ge(t2_g.h, i + 1)
                    vv = v2x.tensor_tensor(out=scores[:, i * 8:(i + 1) * 8],
                                           in0=PG[:, 0:8], in1=bbt_s[:], op=ALU.add)
                    vv.then_inc(v2_sc.h, 1); v2_sc.n += 1

            @blk.scalar
            def _(s):
                for i in range(NT):
                    s.wait_ge(v2_sc.h, i + 1)
                    aa = s.activation(out=scores[:, i * 8:(i + 1) * 8],
                                 in_=scores[:, i * 8:(i + 1) * 8],
                                 func=AF.Sigmoid)
                aa.then_inc(s2_ot.h, 1)
                s2_ot.n += 1

            @blk.vector
            def _(v):
                def vhop(inst):
                    inst.then_inc(v_ch.h, 1)
                    v_ch.n += 1
                    v.wait_ge(v_ch.h, v_ch.n)

                v.wait_ge(s2_ot.h, 1)
                for i in range(NT):
                    sc = scores[:, i * 8:(i + 1) * 8]
                    vv = v.reduce_max(tp_c[:, 0:1], sc, axis=AX.X)
                    vv.then_inc(v2_l2.h, 1); v2_l2.n += 1
                    v.wait_ge(v2_l2.h, v2_l2.n)
                    vhop(v.tensor_single_scalar(out=comb[:, i * 8:(i + 1) * 8],
                                           in_=sc, scalar=tp_c[:, 0:1], op=ALU.is_ge))
                    vhop(v.scalar_tensor_tensor(out=t2scr[:, 0:8], in0=comb[:, i * 8:(i + 1) * 8],
                                           scalar=-2.0, in1=sc, op0=ALU.mult, op1=ALU.add))
                    vv = v.reduce_max(tp_c[:, 1:2], t2scr[:, 0:8], axis=AX.X)
                    vv.then_inc(v2_l2.h, 1); v2_l2.n += 1
                    v.wait_ge(v2_l2.h, v2_l2.n)
                    vhop(v.tensor_single_scalar(out=t2scr[:, 8:16], in_=t2scr[:, 0:8],
                                           scalar=tp_c[:, 1:2], op=ALU.is_ge))
                    vhop(v.tensor_tensor(out=comb[:, i * 8:(i + 1) * 8],
                                    in0=comb[:, i * 8:(i + 1) * 8],
                                    in1=t2scr[:, 8:16], op=ALU.add))
                    vv = v.tensor_scalar(out=tp_c[:, 2:3], in0=tp_c[:, 0:1],
                                    scalar1=tp_c[:, 1:2], scalar2=1e-6,
                                    op0=ALU.add, op1=ALU.add)
                    vv.then_inc(v2_l2.h, 1); v2_l2.n += 1
                    v.wait_ge(v2_l2.h, v2_l2.n)
                    vv = v.reciprocal(out=tp_c[:, 3:4], in_=tp_c[:, 2:3])
                    vv.then_inc(v2_l2.h, 1); v2_l2.n += 1
                    v.wait_ge(v2_l2.h, v2_l2.n)
                    vv = v.scalar_tensor_tensor(out=comb[:, i * 8:(i + 1) * 8], in0=sc,
                                           scalar=tp_c[:, 3:4],
                                           in1=comb[:, i * 8:(i + 1) * 8],
                                           op0=ALU.mult, op1=ALU.mult)
                vv.then_inc(v2_l2.h, 1); v2_l2.n += 1

            @blk.sync
            def _(sync):
                if "tap_x1" in tap_aps:
                    for i in range(NT):
                        sync.dma_start(out=tap_aps["tap_x1"][i * 128:(i + 1) * 128, :],
                                       in_=y_acc[:, i * 1024:(i + 1) * 1024]
                                       ).then_inc(sout.h, 16)
                        sout.n += 16
                if "tap_comb" in tap_aps:
                    sync.wait_ge(v2_l2.h, v2_l2.n)
                    sync.dma_start(out=tap_aps["tap_comb"][:], in_=comb[:]
                                   ).then_inc(sout.h, 16)
                    sout.n += 16

            @blk.sync
            def _(sync):
                for e in range(1, E):
                    ga = (gA1 if e == 1 else gAr)
                    sync.wait_ge(ga.h, ga.n)
                    for qq in range(4):
                        sync.wait_ge(t2_m1.h,
                                     16 * NTB * (e - 1) + 16 * (NTB - 1) + 4 * qq + 4)
                        sync.wait_ge(g2w1.h, g2w1.n)
                        sync.dma_start(
                            out=bass.AP(w1s, qq * 4096,
                                        [[16384, 128], [512, 8], [1, 512]]),
                            in_=bass.AP(w1b_d, e * D * EH + qq * 512,
                                        [[EH, 128], [128 * EH, 8], [1, 512]]),
                        ).then_inc(g2w1.h, 16)
                        g2w1.n += 16

            @blk.tensor
            def _(t):
                for e in range(E):
                    for tb in range(NTB):
                        for ehc in range(16):
                            qq = ehc // 4
                            er = ehc % 4
                            if er == 0:
                                t.wait_ge(g2w1.h, 16 * (4 * e + qq + 1))
                            gsil = 16 * (NTB * e + tb) + ehc
                            if gsil >= 2:
                                t.wait_ge(s2_h.h, gsil - 1)
                            ph = (PH0, PH1)[ehc % 2]
                            for dc in range(8):
                                col = qq * 4096 + dc * 512 + er * 128
                                mm = t.matmul(ph[:, 0:TB], w1s[:, col: col + 128],
                                         x1_fm[:, dc * TPC + tb * TB:
                                               dc * TPC + (tb + 1) * TB],
                                         start=(dc == 0), stop=(dc == 7))
                            mm.then_inc(t2_m1.h, 1); t2_m1.n += 1
                        t.wait_ge(g2w2.h, 16 * (e + 1))
                        t.wait_ge(g2w2b.h, 16 * (e + 1))
                        t.wait_ge(s2_h.h, 16 * (NTB * e + tb + 1))
                        for tt in range(TTB):
                            gt = (NTB * e + tb) * TTB + tt
                            if gt >= 2:
                                t.wait_ge(v2_y.h, gt - 1)
                            PY = (PY0, PY1)[gt % 2]
                            for ehc in range(16):
                                hh = h_fm[:, ehc * TB + tt * 128: ehc * TB + (tt + 1) * 128]
                                mm = t.matmul(PY[:, 0:512], hh,
                                         w2s[:, ehc * 1024: ehc * 1024 + 512],
                                         start=(ehc == 0), stop=(ehc == 15))
                                mm = t.matmul(PY[:, 512:1024], hh,
                                         w2s[:, ehc * 1024 + 512: (ehc + 1) * 1024],
                                         start=(ehc == 0), stop=(ehc == 15))
                            mm.then_inc(t2_y.h, 1); t2_y.n += 1


            @blk.scalar
            def _(s):
                def p2d_scalar(i):
                    if not (0 <= i < NT):
                        return
                    yt = y_acc[:, i * 1024:(i + 1) * 1024]
                    s.wait_ge(v2_y.h, (NTB * (E - 1) + i // 2) * TTB + (i % 2) + 1)
                    aa = s.activation(out=scr2[:], in_=yt, func=AF.Square,
                                 accum_out=ln_c[:, 4:5]).then_inc(s2_sq2.h, 1)
                    s2_sq2.n += 1
                    s.wait_ge(v2_ln.h, v2_ln.n + i + 1)
                    aa = s.activation(out=ln_c[:, 8:9], in_=ln_c[:, 7:8], func=AF.Ln,
                                 bias=EPS).then_inc(s2_rs.h, 1)
                    s2_rs.n += 1
                    s.wait_ge(s2_rs.h, s2_rs.n)
                    aa = s.activation(out=ln_c[:, 9:10], in_=ln_c[:, 8:9], func=AF.Exp,
                                 scale=-0.5).then_inc(s2_rs.h, 1)
                    s2_rs.n += 1
                    s.wait_ge(s2_rs.h, s2_rs.n)
                    s.wait_ge(v2_nb.h, v2_nb.n + i + 1)
                    if i >= 1:
                        s.wait_ge(sout.h, sout.n + 16 * i)
                    aa = s.activation(out=z1_s[:],
                                 in_=yt, func=AF.Identity, scale=ln_c[:, 9:10],
                                 bias=ln_c[:, 11:12]).then_inc(s2_ot.h, 1)
                    s2_ot.n += 1

                for e in range(E):
                    if e >= 1:
                        gb = (gB1 if e == 1 else gBr)
                        s.wait_ge(gb.h, gb.n)
                        s.wait_ge(t2_y.h, NTB * TTB * e)
                        s.dma_start(
                            out=bass.AP(w2s, 0, [[16384, 128], [1024, 8], [1, 1024]]),
                            in_=bass.AP(w2b_d, e * EH * D,
                                        [[1024, 128], [128 * 1024, 8], [1, 1024]]),
                        ).then_inc(g2w2.h, 16)
                        g2w2.n += 16
                    for tb in range(NTB):
                        for ehc in range(16):
                            s.wait_ge(t2_m1.h, 16 * (NTB * e + tb) + ehc + 1)
                            ph = (PH0, PH1)[ehc % 2]
                            aa = s.activation(out=h_fm[:, ehc * TB:(ehc + 1) * TB],
                                         in_=ph[:, 0:TB], func=AF.Silu
                                         ).then_inc(s2_h.h, 1)
                            s2_h.n += 1
                        if e == E - 1:
                            p2d_scalar(2 * (tb - 1))
                            p2d_scalar(2 * (tb - 1) + 1)
                for i in range(2 * (NTB - 1), NT):
                    p2d_scalar(i)

            @blk.gpsimd
            def _(g):
                for e in range(1, E):
                    gb = (gB1 if e == 1 else gBr)
                    g.wait_ge(gb.h, gb.n)
                    g.wait_ge(t2_y.h, NTB * TTB * e)
                    g.dma_start(
                        out=bass.AP(w2s, 8 * 1024,
                                    [[16384, 128], [1024, 8], [1, 1024]]),
                        in_=bass.AP(w2b_d, e * EH * D + 8 * 128 * 1024,
                                    [[1024, 128], [128 * 1024, 8], [1, 1024]]),
                    ).then_inc(g2w2b.h, 16)
                    g2w2b.n += 16

            @blk.vector
            def _(v):
                def vhop(inst):
                    inst.then_inc(v_ch.h, 1)
                    v_ch.n += 1
                    v.wait_ge(v_ch.h, v_ch.n)

                def p2d_vec(i):
                    if not (0 <= i < NT):
                        return
                    yt = y_acc[:, i * 1024:(i + 1) * 1024]
                    v.wait_ge(s2_sq2.h, s2_sq2.n - NT + i + 1)
                    vhop(v.reduce_sum(ln_c[:, 3:4], yt, axis=AX.X))
                    vhop(v.tensor_scalar_mul(out=ln_c[:, 5:6], in0=ln_c[:, 3:4],
                                        scalar1=1.0 / D))
                    vv = v.tensor_tensor(out=ln_c[:, 6:7], in0=ln_c[:, 5:6],
                                    in1=ln_c[:, 5:6], op=ALU.mult)
                    vhop(v.tensor_scalar_mul(out=ln_c[:, 7:8], in0=ln_c[:, 4:5],
                                        scalar1=1.0 / D))
                    vv = v.tensor_tensor(out=ln_c[:, 7:8], in0=ln_c[:, 7:8],
                                    in1=ln_c[:, 6:7], op=ALU.subtract)
                    vv.then_inc(v2_ln.h, 1); v2_ln.n += 1
                    v.wait_ge(s2_rs.h, s2_rs.n - 2 * NT + 2 * (i + 1))
                    vhop(v.tensor_tensor(out=ln_c[:, 10:11], in0=ln_c[:, 5:6],
                                    in1=ln_c[:, 9:10], op=ALU.mult))
                    vv = v.tensor_scalar_mul(out=ln_c[:, 11:12], in0=ln_c[:, 10:11],
                                        scalar1=-1.0)
                    vv.then_inc(v2_nb.h, 1); v2_nb.n += 1

                v.wait_ge(v2_l2.h, v2_l2.n)
                for e in range(E):
                    for tb in range(NTB):
                        for tt in range(TTB):
                            gt = (NTB * e + tb) * TTB + tt
                            tile = tb * TTB + tt
                            v.wait_ge(t2_y.h, gt + 1)
                            PY = (PY0, PY1)[gt % 2]
                            vv = v.scalar_tensor_tensor(
                                out=y_acc[:, tile * 1024:(tile + 1) * 1024],
                                in0=PY[:], scalar=comb[:, tile * 8 + e: tile * 8 + e + 1],
                                in1=y_acc[:, tile * 1024:(tile + 1) * 1024],
                                op0=ALU.mult, op1=ALU.add)
                            vv.then_inc(v2_y.h, 1); v2_y.n += 1
                            if e == E - 1:
                                p2d_vec(tile - 2)
                for i in range(NT - 2, NT):
                    p2d_vec(i)

            @blk.sync
            def _(sync):
                base = s2_ot.n - NT
                for i in range(NT):
                    sync.wait_ge(s2_ot.h, base + i + 1)
                    sync.wait_ge(sout.h, sout.n)
                    sync.dma_start(out=out[i * 128:(i + 1) * 128, :],
                                   in_=z1_s[:]).then_inc(sout.h, 16)
                    sout.n += 16
                sync.wait_ge(sout.h, sout.n)

        PY1_cm.__exit__(None, None, None)
        PY0_cm.__exit__(None, None, None)
        PH1_cm.__exit__(None, None, None)
        PH0_cm.__exit__(None, None, None)
        PG_cm.__exit__(None, None, None)

    return nc, tap_aps


_CACHE = {}


def _get_nc():
    if "nc" not in _CACHE:
        _CACHE["nc"] = build(T=T_FULL, TPC=TPC_FULL)[0]
    return _CACHE["nc"]


def kernel(L0_attn_hidden_states, L0_attn_norm_weight, L0_attn_w_q, L0_attn_w_k,
           L0_attn_w_v, L0_attn_w_f, L0_attn_gnorm_weight, L0_attn_w_o,
           L0_ffn_gate_w, L0_ffn_balance_bias, L0_ffn_w1, L0_ffn_w2):
    x = np.asarray(L0_attn_hidden_states)
    nw = np.asarray(L0_attn_norm_weight)
    gw = np.asarray(L0_attn_gnorm_weight)
    w_q, w_k = np.asarray(L0_attn_w_q), np.asarray(L0_attn_w_k)
    w_v, w_f = np.asarray(L0_attn_w_v), np.asarray(L0_attn_w_f)
    w_o = np.asarray(L0_attn_w_o)
    gate_w = np.asarray(L0_ffn_gate_w)
    bb = np.asarray(L0_ffn_balance_bias)
    w1, w2 = np.asarray(L0_ffn_w1), np.asarray(L0_ffn_w2)

    triu = np.triu(np.ones((128, 128), np.float32))
    Lg = np.ascontiguousarray(triu * np.float32(-1.0 / GATE_NORM))
    w1T = np.ascontiguousarray(w1.transpose(0, 2, 1))
    w2T = np.ascontiguousarray(w2.transpose(0, 2, 1))
    wqT_f = np.ascontiguousarray(w_q.T)
    wkT_f = np.ascontiguousarray(w_k.T)
    wvT_f = np.ascontiguousarray(w_v.T)
    wfT_f = np.ascontiguousarray(w_f.T)
    woT = np.ascontiguousarray(w_o.T)
    nwc = np.ascontiguousarray(nw.reshape(8, 128).T)
    gwc = np.ascontiguousarray(gw.reshape(8, 128).T)
    gateT = np.ascontiguousarray(gate_w.T)
    bbt = np.ascontiguousarray(np.broadcast_to(bb, (128, 8)))

    in_maps = []
    for c in range(8):
        b, h = c // 4, c % 4
        qq = c % 4
        in_maps.append(dict(
            xb=np.ascontiguousarray(x[b]),
            xq=np.ascontiguousarray(x[b, qq * TPC_FULL:(qq + 1) * TPC_FULL]),
            wqT=np.ascontiguousarray(wqT_f[:, 256 * h:256 * (h + 1)]),
            wkT=np.ascontiguousarray(wkT_f[:, 256 * h:256 * (h + 1)]),
            wvT=np.ascontiguousarray(wvT_f[:, 256 * h:256 * (h + 1)]),
            wfT=np.ascontiguousarray(wfT_f[:, 256 * h:256 * (h + 1)]),
            woT=woT, nwc=nwc, gwc=gwc, gateT=gateT, bbt=bbt,
            w1T=w1T, w2T=w2T, triu=triu, Lg=Lg,
        ))

    nc = _get_nc()
    res = run_bass_kernel_spmd(nc, in_maps, core_ids=list(range(8)))
    out = np.concatenate([np.asarray(res.results[c]["out"]) for c in range(8)],
                         axis=0).reshape(2, T_FULL, D)
    return out.astype(np.float32)
